# version 31
# baseline (speedup 1.0000x reference)
import contextlib
import numpy as np

import concourse.bass as bass
import concourse.mybir as mybir
from concourse.bass_utils import run_bass_kernel_spmd

F32 = mybir.dt.float32
BF16 = mybir.dt.bfloat16
AF = mybir.ActivationFunctionType
ALU = mybir.AluOpType
AX = mybir.AxisListType

MAGIC = 12582912.0
LN_EPS = 1e-5
DEN_EPS = 1e-6
MIN_RANGE = 1e-6
INV_QP = float(np.float32(1.0) / np.float32(127.0))

D = 1024
DC = 8
HF = 4096
HC = 32
TB = 512


class _Sem:
    __slots__ = ("name", "handle", "count")

    def __init__(self, name, handle):
        self.name, self.handle, self.count = name, handle, 0


class _Res:
    __slots__ = ("name", "w", "rs")

    def __init__(self, name):
        self.name, self.w, self.rs = name, None, []


class _Ctx:

    def __init__(self, active, engobj, sems):
        self.active = active
        self.engobj = engobj
        self.sems = sems
        self.marks = {}

    def _wait(self, engname, tok):
        if tok is None:
            return
        semname, val = tok
        m = self.marks.setdefault(engname, {})
        if m.get(semname, 0) >= val:
            return
        m[semname] = val
        if self.active == engname:
            self.engobj.wait_ge(self.sems[semname].handle, val)

    def op(self, engname, emit, reads=(), writes=(), wnowait=(), sem=None, dinc=1,
           track=True):
        for r in reads:
            self._wait(engname, r.w)
        for w in writes:
            self._wait(engname, w.w)
            for t in w.rs:
                self._wait(engname, t)
        s = self.sems[sem if sem is not None else ("s_" + engname)]
        inst = emit(self.engobj) if self.active == engname else None
        s.count += dinc
        if inst is not None:
            inst.then_inc(s.handle, dinc)
        if not track:
            return None
        tok = (s.name, s.count)
        for r in reads:
            r.rs.append(tok)
        for w in writes:
            w.w, w.rs = tok, []
        for w in wnowait:
            w.w, w.rs = tok, []
        return tok

    def raw_wait(self, engname, semname, val):
        self._wait(engname, (semname, val))


def build_program(T_core, n_cores):
    NBLK = T_core // TB
    NB2 = NBLK // 2
    assert NBLK % 2 == 0 and NB2 >= 1

    nc = bass.Bass("TRN2", target_bir_lowering=False)

    NBT = T_core // 512
    d_xfm = nc.dram_tensor("x_fm", [NBT * 128, 4096], F32, kind="ExternalInput")
    d_xtm = nc.dram_tensor("x_tm", [NBT * 128, 4096], F32, kind="ExternalInput")
    d_wqkv = nc.dram_tensor("wqkv", [D, 3 * D], F32, kind="ExternalInput")
    d_wout = nc.dram_tensor("wout", [D, D], F32, kind="ExternalInput")
    d_w1b = nc.dram_tensor("w1b", [HF, D], F32, kind="ExternalInput")
    d_w2t = nc.dram_tensor("w2t", [HF, D], F32, kind="ExternalInput")
    d_gam = nc.dram_tensor("gam_pp", [128, DC], F32, kind="ExternalInput")
    d_beta = nc.dram_tensor("beta_pp", [128, DC], F32, kind="ExternalInput")
    d_b1 = nc.dram_tensor("b1_pp", [128, HC], F32, kind="ExternalInput")
    d_b2 = nc.dram_tensor("b2_pp", [128, DC], F32, kind="ExternalInput")
    d_ident = nc.dram_tensor("ident", [128, 128], F32, kind="ExternalInput")
    d_oh = nc.dram_tensor("onehot", [1, 8], F32, kind="ExternalInput")
    d_sel2 = nc.dram_tensor("sel2c", [2, 256], F32, kind="ExternalInput")
    d_mask = nc.dram_tensor("maskc_f", [16, 1024], F32, kind="ExternalInput")
    d_out = nc.dram_tensor("out_fm", [NBT * 128, 4096], F32, kind="ExternalOutput")

    d_q1 = nc.dram_tensor("q1_spill", [NBT * 128, 4096], BF16)
    d_h = nc.dram_tensor("h_spill", [(T_core // 512) * 16 * 128, 1024], F32)
    d_cc1i = nc.dram_tensor("cc1_in", [64, 2080], F32)
    d_cc1o = nc.dram_tensor("cc1_out", [64, 2080], F32)
    d_cc1ai = nc.dram_tensor("cc1a_in", [1, 8], F32)
    d_cc1ao = nc.dram_tensor("cc1a_out", [1, 8], F32)
    d_cc2i = nc.dram_tensor("cc2_in", [1, 8], F32)
    d_cc2o = nc.dram_tensor("cc2_out", [1, 8], F32)

    A = nc.alloc_sbuf_tensor
    btx = A("btx", [128, 4096], F32)
    bts = A("bts", [128, 4096], F32)
    bto = A("bto", [128, 4096], F32)
    bb = [A(f"bb{i}", [128, 4096], BF16) for i in range(3)]
    arena = A("arena_w", [128, 32768], BF16)
    woutbf = A("wout_bf", [128, 8192], BF16)
    wbst = [A(f"wbst{i}", [128, 1024], F32) for i in range(2)]
    xn_t = [A(f"xn{i}", [128, 512], F32) for i in range(2)]
    m_t = [A("m0", [128, 1024], F32)] * 2
    e_t = [A("e0", [128, 1024], F32)] * 2
    hq_t = [A(f"hq{i}", [128, 512], BF16) for i in range(4)]
    k1_tm = [A(f"k1tm{i}", [128, 1024], BF16) for i in range(2)]
    v65 = [A(f"v65_{i}", [128, 1040], BF16) for i in range(2)]
    kvacc = A("kvacc", [64, 2080], F32)
    kvblk = A("kvblk", [128, 2048], BF16)
    denblk = A("denblk", [128, 256], BF16)
    maskc = A("maskc", [16, 1024], BF16)
    sel2 = A("sel2", [2, 256], F32)
    gam_pp = A("gam_pp_sb", [128, DC], F32)
    ones_row = None
    beta_pp = A("beta_pp_sb", [128, DC], F32)
    b1_pp = A("b1_pp_sb", [128, HC], F32)
    b2_pp = A("b2_pp_sb", [128, DC], F32)
    b1ss = A("b1ss", [128, HC], F32)
    b2ss = A("b2ss", [128, DC], F32)
    ident_sb = A("ident_sb", [128, 128], F32)
    statp = A("statp", [128, NBLK * 4 * 6], F32)
    bnst = A("bnst", [128, 12], F32)
    agg8 = A("agg8", [128, 8], F32)
    tmp4 = A("tmp4", [128, 8], F32)
    rows2 = A("rows2", [2, 512], F32)
    den_sb = A("den_sb", [16, 512], F32)
    rec_bf = A("rec_bf", [16, 512], BF16)
    kvstage = A("kvstage", [128, 128], F32)
    denstage = A("denstage", [128, 16], F32)
    mx_xn = A("mx_xn", [128, 1], F32)
    mx_h = A("mx_h", [128, 1], F32)
    mx_w1 = A("mx_w1", [128, 1], F32)
    mx_w2 = A("mx_w2", [128, 1], F32)
    tred = A("tred", [128, 1], F32)
    scal = A("scal", [128, 24], F32)
    oh_t = A("oh_t", [1, 8], F32)

    SC_SXINV, SC_SWQINV, SC_SS1, SC_SHINV, SC_SS2 = 0, 1, 2, 3, 4
    SC_GXC, SC_SXV = 8, 9
    SC_SW1C, SC_SW1V, SC_SW1I = 10, 11, 12
    SC_GHC, SC_SHV = 13, 14
    SC_SW2C, SC_SW2V = 15, 16
    SC_T0, SC_T1, SC_DEPS, SC_EPS = 17, 18, 22, 23

    zbld = m_t[0]
    gam_row = None
    r2row = den_sb
    r2rowbf = rec_bf

    zq_t = [e_t[0][:, 0:512], e_t[0][:, 512:1024]]
    recb_v = [m_t[1][:, 0:512], m_t[1][:, 512:1024]]
    et_v = [e_t[1][:, 0:512], e_t[1][:, 512:1024]]

    gmx8 = oh_t
    ones_row = sel2
    oh_sb = bnst[0:1, 0:8]

    P = [nc.alloc_psum_tensor(f"P{i}", [128, 512], F32) for i in range(8)]

    hrel_v = [bts[:, i * 512:(i + 1) * 512] for i in range(4)]

    es = contextlib.ExitStack()
    semnames = (["s_sync", "s_gpsimd", "s_vector", "s_scalar", "s_tensor",
                 "cc1", "cc1a", "cc2"] +
                ["ld_x0", "ld_x1", "ld_t", "ld_misc", "ld_wq", "ld_h0",
                 "ld_h1", "ld_h2", "ld_h3", "ld_q1", "ld_wb0", "ld_wb1", "ld_kvb", "ld_kvm", "ld_wm",
                 "ld_cc2",
                 "ld_cc"] +
                ["st_q0", "st_q1", "st_h0", "st_h1", "st_out", "st_cc"])
    handles = {n: es.enter_context(nc.semaphore(n)) for n in semnames}

    def fresh_sems():
        return {n: _Sem(n, handles[n]) for n in semnames}

    def wqkv_bf(d):
        return arena[:, d * 3072:(d + 1) * 3072]

    def w1qv(d):
        return arena[:, d * 4096:(d + 1) * 4096]

    def w2qv(f):
        return arena[:, f * 1024:(f + 1) * 1024]

    def schedule(ctx):
        op = ctx.op
        R = {}

        def res(name):
            if name not in R:
                R[name] = _Res(name)
            return R[name]

        r_btx = res("btx")
        r_bts = res("bts")
        r_bto = res("bto")
        r_btsh = [res(f"btsh{i}") for i in range(4)]
        r_hrel = [res(f"hrel{i}") for i in range(4)]
        r_bb = [res(f"bb{i}") for i in range(3)]
        r_wq = [res(f"wq{d}") for d in range(DC)]
        r_w1q = [res(f"w1q{fo}") for fo in range(HC)]
        r_w2q = [res(f"w2q{f}") for f in range(HC)]
        r_woutbf = res("woutbf")
        r_wbst = [res(f"wbst{i}") for i in range(2)]
        r_zbld = [res("m0h0"), res("m0h1")]
        r_xn = [res(f"xn{i}") for i in range(2)]
        r_zq = [res("e0h0"), res("e0h1")]
        r_m0 = [res("m0h0"), res("m0h1")]
        r_m1 = r_m0
        r_e0 = [res("e0h0"), res("e0h1")]
        r_e1 = r_e0
        r_hq = [res(f"hq{i}") for i in range(4)]
        r_et = [res("e0h0"), res("e0h1")]
        r_k1 = [res(f"k1tm{i}") for i in range(2)]
        r_v65 = [res(f"v65_{i}") for i in range(2)]
        r_kvacc = res("kvacc")
        r_kvblk = res("kvblk")
        r_denblk = res("denblk")
        r_maskc = res("maskc")
        r_sel2 = res("sel2")
        r_ones = res("sel2")
        r_gampp = res("gam_pp_sb")
        r_beta = res("beta_pp")
        r_b1 = res("b1_pp")
        r_b2 = res("b2_pp")
        r_b1ss = res("b1ss")
        r_b2ss = res("b2ss")
        r_ident = res("ident_sb")
        r_statp = res("statp")
        r_bnst = res("bnst")
        r_agg8 = res("agg8")
        r_tmp4 = res("tmp4")
        r_rows2 = res("rows2")
        r_r2row = res("den_sb")
        r_r2rowbf = res("rec_bf")
        r_densb = res("den_sb")
        r_recbf = res("rec_bf")
        r_recb = [res("m0h0"), res("m0h1")]
        r_kvstage = res("kvstage")
        r_denstage = res("denstage")
        r_mxxn = res("mx_xn")
        r_mxh = res("mx_h")
        r_mxw1 = res("mx_w1")
        r_mxw2 = res("mx_w2")
        r_tred = res("tred")
        r_epsp = res("scal")
        r_scal = res("scal")
        r_oh = res("bnst")
        r_oht = res("oh_t")
        r_gmx8 = res("oh_t")
        r_P = [res(f"P{i}") for i in range(8)]
        r_q1d = res("q1_dram")
        r_hd = [res("h_dram0"), res("h_dram1")]
        r_cc1i = res("cc1i")
        r_cc1o = res("cc1o")
        r_cc1ai = res("cc1ai")
        r_cc1ao = res("cc1ao")
        r_cc2i = res("cc2i")
        r_cc2o = res("cc2o")
        r_out = res("out_fm")

        def V(emit, reads=(), writes=(), wnowait=()):
            return op("vector", emit, reads, writes, wnowait)

        def Sc(emit, reads=(), writes=(), wnowait=()):
            return op("scalar", emit, reads, writes, wnowait)

        def G(emit, reads=(), writes=(), wnowait=()):
            return op("gpsimd", emit, reads, writes, wnowait)

        def Tn(emit, reads=(), writes=(), wnowait=()):
            return op("tensor", emit, reads, writes, wnowait)

        def DMA(eng, emit, reads=(), writes=(), sem=None):
            return op(eng, emit, reads, writes, sem=sem, dinc=16)

        def sc_col(i):
            return scal[:, i:i + 1]

        def sc1(i):
            return scal[0:1, i:i + 1]

        def stats_rows(blk, c0, dst_rows, r_dst):
            for s in range(4):
                base = (blk * 4 + s) * 6
                Tn(lambda e, b=base, s=s: e.transpose(
                    P[6][0:2, s * 128:(s + 1) * 128],
                    statp[:, b + c0: b + c0 + 2], ident_sb[:, :]),
                   reads=[r_statp, r_ident],
                   writes=[r_P[6]] if s == 0 else [],
                   wnowait=[r_P[6]] if s == 3 else [])
            V(lambda e: e.tensor_copy(dst_rows[0:2, 0:512], P[6][0:2, 0:512]),
              reads=[r_P[6]], writes=[r_dst])

        def cross_part_max(r_acc, acc_t, out_col):
            Tn(lambda e: e.transpose(P[7][0:1, 0:128], acc_t[:, :], ident_sb[:, :]),
               reads=[r_acc, r_ident], writes=[r_P[7]])
            V(lambda e, oc=out_col: e.tensor_reduce(
                scal[0:1, oc:oc + 1], P[7][0:1, 0:128], AX.X, ALU.max),
              reads=[r_P[7]], writes=[r_scal])

        def bcast_scalar(src_col, dst_col):
            Tn(lambda e: e.matmul(P[7][:, 0:1], sel2[0:1, 0:128], sc1(src_col),
                                  start=True, stop=True),
               reads=[r_ones, r_scal], writes=[r_P[7]])
            V(lambda e: e.tensor_copy(sc_col(dst_col), P[7][:, 0:1]),
              reads=[r_P[7]], writes=[r_scal])

        const_dsts = [(r_gampp, gam_pp, d_gam), (r_beta, beta_pp, d_beta),
                      (r_b1, b1_pp, d_b1), (r_b2, b2_pp, d_b2),
                      (r_ident, ident_sb, d_ident),
            (r_sel2, sel2, d_sel2)]
        for (dst, dsth, src) in const_dsts:
            DMA("sync", lambda e, dh=dsth, s=src: e.dma_start(
                out=dh[:, :] if hasattr(dh, "ap") else dh, in_=s[:, :]),
                writes=[dst], sem="ld_misc")

        V(lambda e: e.memset(scal[:, SC_EPS:SC_EPS + 1], LN_EPS), writes=[r_epsp])
        V(lambda e: e.memset(scal[:, SC_DEPS:SC_DEPS + 1], DEN_EPS), writes=[r_epsp])
        DMA("sync", lambda e: e.dma_start(out=bto[0:16, 1024:2048], in_=d_mask[:, :]),
            writes=[r_bto], sem="ld_misc")
        _misc_total = ctx.sems["ld_misc"].count
        for (dst, _, _) in const_dsts:
            dst.w = ("ld_misc", _misc_total)
        r_bto.w = ("ld_misc", _misc_total)
        V(lambda e: e.tensor_copy(maskc[0:16, 0:1024], bto[0:16, 1024:2048]),
          reads=[r_bto], writes=[r_maskc])
        for mres, mt in [(r_mxxn, mx_xn), (r_mxh, mx_h), (r_mxw1, mx_w1), (r_mxw2, mx_w2)]:
            V(lambda e, m=mt: e.memset(m[:, :], 0.0), writes=[mres])
        V(lambda e: e.memset(kvacc[:, :], 0.0), writes=[r_kvacc])
        for s in range(2):
            V(lambda e, s=s: e.memset(
                v65[s][:, :].rearrange("p (h v) -> p h v", v=65)[:, :, 64:65], 1.0),
              writes=[r_v65[s]])

        for d in range(DC):
            DMA("sync", lambda e, d=d: e.dma_start(
                out=bto[:, 0:3072], in_=d_wqkv[d * 128:(d + 1) * 128, :]),
                writes=[r_bto], sem="ld_wq")
            V(lambda e, d=d: e.tensor_copy(wqkv_bf(d)[:, :], bto[:, 0:3072]),
              reads=[r_bto], writes=[r_wq[d]])


        def stats_rows2(blk, c0, pbank, r_pbank):
            for s in range(4):
                base = (blk * 4 + s) * 6
                Tn(lambda e, b=base, s=s, pb=pbank: e.transpose(
                    pb[0:2, s * 128:(s + 1) * 128],
                    statp[:, b + c0: b + c0 + 2], ident_sb[:, :]),
                   reads=[r_statp, r_ident],
                   writes=[r_pbank] if s == 0 else [],
                   wnowait=[r_pbank] if s == 3 else [])
            V(lambda e, pb=pbank: e.tensor_copy(rows2[0:2, 0:512], pb[0:2, 0:512]),
              reads=[r_pbank], writes=[r_rows2])

        def emit_stats(blk):
            pieces = []

            def bn_piece(slo, shi):
                def f():
                    for s in range(slo, shi):
                        xv = bts[:, s * 1024:(s + 1) * 1024]
                        V(lambda e, xv=xv: e.bn_stats(bnst[:, 0:6], xv[:, 0:512]),
                          reads=[r_bts], writes=[r_bnst])
                        V(lambda e, xv=xv: e.bn_stats(bnst[:, 6:12], xv[:, 512:1024]),
                          reads=[r_bts], writes=[r_bnst])
                        V(lambda e, s=s: e.bn_aggr(
                            agg8[:, :].rearrange("p (k s) -> p k s", k=2)[:, :, s:s + 1],
                            bnst[:, 0:12]),
                          reads=[r_bnst], writes=[r_agg8])
                return f
            pieces.append(bn_piece(0, 2))
            pieces.append(bn_piece(2, 4))

            def chain():
                bb0 = blk * 24
                MU = agg8[:, 0:4]
                VAR = agg8[:, 4:8]

                def spv(c):
                    return statp[:, bb0:bb0 + 24].rearrange(
                        "p (s c) -> p c s", c=6)[:, c:c + 1, :]
                Sc(lambda e: e.activation(tmp4[:, 0:4], VAR, AF.Ln,
                                          bias=scal[:, SC_EPS:SC_EPS + 1]),
                   reads=[r_agg8, r_scal], writes=[r_tmp4])
                Sc(lambda e: e.activation(spv(1), tmp4[:, 0:4], AF.Exp, scale=-0.5),
                   reads=[r_tmp4], writes=[r_statp])
                V(lambda e: e.tensor_tensor(tmp4[:, 4:8], VAR, spv(1), ALU.mult),
                  reads=[r_agg8, r_statp], writes=[r_tmp4])
                V(lambda e: e.tensor_tensor(tmp4[:, 4:8], tmp4[:, 4:8], spv(1), ALU.mult),
                  reads=[r_statp], writes=[r_tmp4])
                Sc(lambda e: e.activation(tmp4[:, 0:4], tmp4[:, 4:8], AF.Ln,
                                          bias=scal[:, SC_EPS:SC_EPS + 1]),
                   reads=[r_tmp4, r_scal], writes=[r_tmp4])
                Sc(lambda e: e.activation(tmp4[:, 4:8], tmp4[:, 0:4], AF.Exp, scale=-0.5),
                   reads=[r_tmp4], writes=[r_tmp4])
                V(lambda e: e.tensor_tensor(spv(2), spv(1), tmp4[:, 4:8], ALU.mult),
                  reads=[r_tmp4], writes=[r_statp])
                V(lambda e: e.tensor_tensor(spv(3), MU, spv(2), ALU.mult),
                  reads=[r_agg8], writes=[r_statp])
                V(lambda e: e.tensor_tensor(spv(4), MU, spv(1), ALU.mult),
                  reads=[r_agg8], writes=[r_statp])
                V(lambda e: e.tensor_scalar_add(spv(5), spv(1), 1.0),
                  writes=[r_statp])
                V(lambda e: e.tensor_copy(spv(0), MU),
                  reads=[r_agg8], writes=[r_statp])
            pieces.append(chain)

            def maxdev(slo, shi):
                def f():
                    bb0 = blk * 24
                    for s in range(slo, shi):
                        xv = bts[:, s * 1024:(s + 1) * 1024]
                        V(lambda e, xv=xv, s=s: e.tensor_scalar(
                            m_t[0][:, :], xv, agg8[:, s:s + 1], None, ALU.subtract),
                          reads=[r_bts, r_agg8], writes=list(r_m0))
                        V(lambda e: e.tensor_reduce(
                            tred[:, :], m_t[0][:, :], AX.X, ALU.max,
                            apply_absolute_value=True),
                          reads=list(r_m0), writes=[r_tred])
                        V(lambda e, s=s, bb0=bb0: e.tensor_tensor(
                            tred[:, :], tred[:, :],
                            statp[:, bb0 + s * 6 + 1:bb0 + s * 6 + 2], ALU.mult),
                          reads=[r_statp], writes=[r_tred])
                        V(lambda e: e.tensor_tensor(
                            mx_xn[:, :], mx_xn[:, :], tred[:, :], ALU.max),
                          reads=[r_tred], writes=[r_mxxn])
                return f
            pieces.append(maxdev(0, 2))
            pieces.append(maxdev(2, 4))
            return pieces

        def a_prep_pieces(blk):
            pieces = []

            def loads():
                DMA("sync", lambda e, b=blk: e.dma_start(
                    out=btx[:, :], in_=d_xfm[b * 128:(b + 1) * 128, :]),
                    writes=[r_btx], sem="ld_x0")
                DMA("sync", lambda e, b=blk: e.dma_start(
                    out=bts[:, :], in_=d_xtm[b * 128:(b + 1) * 128, :]),
                    writes=[r_bts] + r_btsh, sem="ld_t")
            pieces.append(("early", loads))
            pieces.extend(("early", p) for p in emit_stats(blk))

            def rows_t():
                stats_rows2(blk, 2, P[4], r_P[4])
            pieces.append(("latev", rows_t))

            def rows_mm():
                Tn(lambda e: e.matmul(P[4][:, 0:512], sel2[:, 0:128],
                                      rows2[0:2, 0:512], start=True, stop=True),
                   reads=[r_sel2, r_rows2], writes=[r_P[4]])
                Tn(lambda e: e.matmul(P[5][:, 0:512], sel2[:, 128:256],
                                      rows2[0:2, 0:512], start=True, stop=True),
                   reads=[r_sel2, r_rows2], writes=[r_P[5]])

            xn2 = bb[blk % 2]
            for dch in range(DC):
                def t2v(dch=dch):
                    xs = dch % 2
                    xnv, r_xnv = xn_t[xs], r_xn[xs]
                    V(lambda e, dch=dch, xnv=xnv: e.tensor_tensor(
                        xnv[:, :], btx[:, dch * 512:(dch + 1) * 512],
                        P[4][:, 0:512], ALU.mult),
                      reads=[r_btx, r_P[4]], writes=[r_xnv])
                    V(lambda e, xnv=xnv: e.tensor_tensor(
                        xnv[:, :], xnv[:, :], P[5][:, 0:512], ALU.subtract),
                      reads=[r_P[5]], writes=[r_xnv])

                def t2s(dch=dch, xn2=xn2):
                    xs = dch % 2
                    xnv, r_xnv = xn_t[xs], r_xn[xs]
                    Sc(lambda e, dch=dch, xnv=xnv, x2=xn2: e.activation(
                        x2[:, dch * 512:(dch + 1) * 512], xnv[:, :], AF.Identity,
                        scale=gam_pp[:, dch:dch + 1], bias=beta_pp[:, dch:dch + 1]),
                       reads=[r_xnv, r_beta, r_gampp],
                       writes=[res(f"bb{blk % 2}_{dch}")] +
                              ([r_bb[1]] if (blk % 2 == 1 and dch == 0) else []))
                pieces.append(("latev", t2v))
                pieces.append(("lates", t2s))
            return pieces, rows_mm

        def a_main(blk, pq, rows_mm=None):
            bat = blk // NB2
            xn2 = bb[blk % 2]

            def drain(n=1, kinds=None):
                for _ in range(n):
                    if kinds is None:
                        if pq:
                            pq.pop(0)[1]()
                    else:
                        for i, (k, f) in enumerate(pq):
                            if k in kinds:
                                pq.pop(i)
                                f()
                                break

            q1b, r_q1b = bb[2], r_bb[2]
            for fo in range(DC):
                pqb, r_pqb = P[4 + fo % 2], r_P[4 + fo % 2]
                for dch in range(DC):
                    Tn(lambda e, fo=fo, dch=dch, pqb=pqb: e.matmul(
                        pqb[:, 0:512], wqkv_bf(dch)[:, fo * 128:(fo + 1) * 128],
                        xn2[:, dch * 512:(dch + 1) * 512],
                        start=(dch == 0), stop=(dch == DC - 1)),
                       reads=[r_wq[dch], res(f"bb{blk % 2}_{dch}")],
                       writes=[r_pqb] if dch == 0 else [],
                       wnowait=[r_pqb] if dch == DC - 1 else [])
                ms = fo % 2
                Sc(lambda e, pqb=pqb, ms=ms: e.activation(
                    m_t[0][:, ms * 512:(ms + 1) * 512], pqb[:, 0:512],
                    AF.Relu, scale=-1.0),
                   reads=[r_pqb], writes=[r_m0[ms]])
                Sc(lambda e, ms=ms: e.activation(
                    e_t[0][:, ms * 512:(ms + 1) * 512],
                    m_t[0][:, ms * 512:(ms + 1) * 512], AF.Exp, scale=-1.0),
                   reads=[r_m0[ms]], writes=[r_e0[ms]])
                V(lambda e, pqb=pqb, ms=ms, fo=fo: e.scalar_tensor_tensor(
                    q1b[:, fo * 512:(fo + 1) * 512], pqb[:, 0:512], 0.0,
                    e_t[0][:, ms * 512:(ms + 1) * 512], ALU.max, ALU.add),
                  reads=[r_pqb, r_e0[ms]], writes=[r_q1b])
                drain(1, kinds={"early"})
            DMA("gpsimd", lambda e, b=blk, q=q1b: e.dma_start(
                out=d_q1[b * 128:(b + 1) * 128, :], in_=q[:, :]),
                reads=[r_q1b], writes=[r_q1d], sem=f"st_q{blk % 2}")

            def kv_pair_mms(tc, hs, k1v, vv, r_k1v, r_vv):
                for h in hs:
                    pb, r_pb = P[h // 8], r_P[h // 8]
                    first = (tc == 0 and h % 8 == 0)
                    last = (tc == 3 and h % 8 == 7)
                    Tn(lambda e, h=h, pb=pb, fs=first, ls=last,
                       k1v=k1v, vv=vv: e.matmul(
                        pb[0:64, (h % 8) * 64:(h % 8 + 1) * 64],
                        k1v[:, h * 64:(h + 1) * 64],
                        vv[:, h * 65:h * 65 + 64],
                        start=fs, stop=ls),
                       reads=[r_k1v, r_vv],
                       writes=[r_pb] if first else [],
                       wnowait=[r_pb] if last else [])
                    Tn(lambda e, h=h, tc=tc, k1v=k1v, vv=vv: e.matmul(
                        P[2][0:64, h:h + 1],
                        k1v[:, h * 64:(h + 1) * 64],
                        vv[:, h * 65 + 64:h * 65 + 65],
                        start=(tc == 0 and h == 0), stop=(tc == 3 and h == 15)),
                       reads=[r_k1v, r_vv],
                       writes=[r_P[2]] if (tc == 0 and h == 0) else [],
                       wnowait=[r_P[2]] if (tc == 3 and h == 15) else [])

            pend = None
            for tc in range(4):
                ks = tc % 2
                k1v, r_k1v = k1_tm[ks], r_k1[ks]
                vv, r_vv = v65[ks], r_v65[ks]
                for qi, qtr in enumerate([0, 2, 1, 3]):
                    kb = [3, 6, 7][(tc * 4 + qi) % 3]
                    pk, r_pk = P[kb], r_P[kb]
                    for dch in range(DC):
                        Tn(lambda e, tc=tc, qtr=qtr, dch=dch, pk=pk: e.matmul(
                            pk[:, 0:512],
                            xn2[:, dch * 512 + tc * 128: dch * 512 + (tc + 1) * 128],
                            wqkv_bf(dch)[:, 1024 + qtr * 512:1024 + (qtr + 1) * 512],
                            start=(dch == 0), stop=(dch == DC - 1)),
                           reads=[res(f"bb{blk % 2}_{dch}"), r_wq[dch]],
                           writes=[r_pk] if dch == 0 else [],
                           wnowait=[r_pk] if dch == DC - 1 else [])
                    if pend is not None:
                        kv_pair_mms(*pend)
                        pend = None
                    if qtr < 2:
                        Sc(lambda e, pk=pk, qtr=qtr: e.activation(
                            m_t[1][:, qtr * 512:(qtr + 1) * 512], pk[:, 0:512],
                            AF.Relu, scale=-1.0),
                           reads=[r_pk], writes=[r_m1[qtr]])
                        Sc(lambda e, ks=ks, qtr=qtr: e.activation(
                            e_t[1][:, qtr * 512:(qtr + 1) * 512],
                            m_t[1][:, qtr * 512:(qtr + 1) * 512], AF.Exp, scale=-1.0),
                           reads=[r_m1[qtr]], writes=[r_e1[qtr]])
                        V(lambda e, pk=pk, ks=ks, qtr=qtr, k1v=k1v: e.scalar_tensor_tensor(
                            k1v[:, qtr * 512:(qtr + 1) * 512], pk[:, 0:512], 0.0,
                            e_t[1][:, qtr * 512:(qtr + 1) * 512], ALU.max, ALU.add),
                          reads=[r_pk, r_e1[qtr]], writes=[r_k1v])
                    else:
                        Sc(lambda e, pk=pk, vv=vv, qtr=qtr: e.activation(
                            vv[:, :].rearrange("p (h v) -> p h v", v=65)
                              [:, (qtr - 2) * 8:(qtr - 1) * 8, 0:64],
                            pk[:, 0:512].rearrange("p (h v) -> p h v", v=64),
                            AF.Copy),
                           reads=[r_pk], writes=[r_vv])
                    if qi == 1:
                        if tc == 0 and rows_mm is not None:
                            rows_mm()
                        pend = (tc, range(0, 8), k1v, vv, r_k1v, r_vv)
                        drain(2)
                    elif qi == 3:
                        if tc == 3:
                            kv_pair_mms(tc, range(8, 16), k1v, vv, r_k1v, r_vv)
                        else:
                            pend = (tc, range(8, 16), k1v, vv, r_k1v, r_vv)
                        drain(2)
                    else:
                        drain(1, kinds={"early", "latev"})
                drain(1)
            for bk in range(2):
                coff = bat * 1024 + bk * 512
                V(lambda e, bk=bk, coff=coff: e.tensor_tensor(
                    kvacc[0:64, coff:coff + 512], kvacc[0:64, coff:coff + 512],
                    P[bk][0:64, 0:512], ALU.add),
                  reads=[r_P[bk]], writes=[r_kvacc])
            V(lambda e, bat=bat: e.tensor_tensor(
                kvacc[0:64, 2048 + bat * 16:2048 + (bat + 1) * 16],
                kvacc[0:64, 2048 + bat * 16:2048 + (bat + 1) * 16],
                P[2][0:64, 0:16], ALU.add),
              reads=[r_P[2]], writes=[r_kvacc])

            den = max(NBLK - 1, 1)
            for wci in (range(blk * 8 // den, (blk + 1) * 8 // den)
                        if blk < den else []):
                DMA("sync", lambda e, rlo=wci * 512: e.dma_start(
                    out=bto[:, 0:4096].rearrange("p (c f) -> p c f", c=4),
                    in_=d_w1b[rlo:rlo + 512, :].rearrange("(p c) f -> p c f", c=4)),
                    writes=[r_bto], sem="ld_wm")
                V(lambda e: e.tensor_reduce(
                    tred[:, :], bto[:, 0:4096], AX.X, ALU.max, apply_absolute_value=True),
                  reads=[r_bto], writes=[r_tred])
                V(lambda e: e.tensor_tensor(
                    mx_w1[:, :], mx_w1[:, :], tred[:, :], ALU.max),
                  reads=[r_tred], writes=[r_mxw1])
            drain(99)

        bslot = [wbst[0], wbst[1], bto[:, 0:1024]]
        r_bslot = [r_wbst[0], r_wbst[1], r_bto]
        bsem = ["ld_wb0", "ld_wb1", "ld_wm"]

        def build_load(fo):
            ws = fo % 3
            op("scalar", lambda e, fo=fo, ws=ws: e.dma_start(
                out=bslot[ws][:, 0:1024],
                in_=d_w1b[fo * 128:(fo + 1) * 128, :]),
                writes=[r_bslot[ws]], sem=bsem[ws], dinc=16)

        def build_compute(fo):
            ws = fo % 3
            Sc(lambda e, ws=ws: e.activation(zbld[:, :], bslot[ws][:, :1024], AF.Copy,
                                             bias=MAGIC, scale=sc_col(SC_SWQINV)),
               reads=[r_bslot[ws], r_scal], writes=list(r_zbld))
            wr = [r_w1q[fo]] + (r_wq if fo == 0 else [])
            V(lambda e, fo=fo: e.tensor_scalar_add(
                arena[:, :].rearrange("p (c f) -> p c f", c=DC)[:, :, fo * 128:(fo + 1) * 128],
                zbld[:, :].rearrange("p (c f) -> p c f", c=DC), -MAGIC),
              reads=list(r_zbld), writes=wr)

        def boundary67():
            cross_part_max(r_mxxn, mx_xn, SC_T0)
            DMA("sync", lambda e: e.dma_start(out=oh_sb, in_=d_oh[:, :]),
                writes=[r_oh], sem="ld_cc")
            V(lambda e: e.tensor_scalar(oh_t[0:1, 0:8], oh_sb[0:1, 0:8],
                                        sc1(SC_T0), None, ALU.mult),
              reads=[r_oh, r_scal], writes=[r_oht])
            DMA("gpsimd", lambda e: e.dma_start(out=d_cc1ai[0:1, 0:8], in_=oh_t[0:1, 0:8]),
                reads=[r_oht], writes=[r_cc1ai], sem="st_cc")
            op("gpsimd", lambda e: e.collective_compute(
                "AllReduce", ALU.add, replica_groups=[list(range(n_cores))],
                ins=[d_cc1ai.ap()], outs=[d_cc1ao.ap()]),
               reads=[r_cc1ai], writes=[r_cc1ao], sem="cc1a", dinc=1)
            DMA("sync", lambda e: e.dma_start(
                out=btx[:, :], in_=d_xfm[0:128, :]),
                writes=[r_btx], sem="ld_x0")
            cross_part_max(r_mxw1, mx_w1, SC_SW1C)
            V(lambda e: e.tensor_scalar(sc1(SC_SW1V), sc1(SC_SW1C), MIN_RANGE,
                                        INV_QP, ALU.max, ALU.mult),
              writes=[r_scal])
            V(lambda e: e.reciprocal(sc1(SC_SW1I), sc1(SC_SW1V)), writes=[r_scal])
            bcast_scalar(SC_SW1I, SC_SWQINV)
            for fo in range(3):
                build_load(fo)

        pq_a, rm0 = a_prep_pieces(0)
        for k, f in pq_a[:7]:
            f()
        rm0()
        for k, f in pq_a[7:]:
            f()
        for blk in range(NBLK):
            if blk + 1 < NBLK:
                nxt, rm_next = a_prep_pieces(blk + 1)
            else:
                nxt, rm_next = [], None
            if blk == NBLK - 1:
                boundary67()
            a_main(blk, nxt, rm_next)

        DMA("gpsimd", lambda e: e.dma_start(out=gmx8[0:1, 0:8], in_=d_cc1ao[0:1, 0:8]),
            reads=[r_cc1ao], writes=[r_gmx8], sem="ld_cc2")

        DMA("gpsimd", lambda e: e.dma_start(out=d_cc1i[:, 0:2080], in_=kvacc[:, :]),
            reads=[r_kvacc], writes=[r_cc1i], sem="st_cc")
        op("gpsimd", lambda e: e.collective_compute(
            "AllReduce", ALU.add, replica_groups=[list(range(n_cores))],
            ins=[d_cc1i.ap()], outs=[d_cc1o.ap()]),
           reads=[r_cc1i], writes=[r_cc1o], sem="cc1", dinc=1)

        build_compute(0)
        build_load(3)
        build_compute(1)
        build_load(4)

        def b1_prep_pieces(blk):
            pieces = []

            def loads():
                DMA("sync", lambda e, b=blk: e.dma_start(
                    out=btx[:, :], in_=d_xfm[b * 128:(b + 1) * 128, :]),
                    writes=[r_btx], sem="ld_x0")
            pieces.append(loads)

            def rows_bcast():
                bb0 = blk * 24

                def spv(c):
                    return statp[:, bb0:bb0 + 24].rearrange(
                        "p (s c) -> p c s", c=6)[:, c:c + 1, :]
                V(lambda e: e.tensor_scalar(spv(2), spv(1), sc_col(SC_SXINV), None,
                                            ALU.mult),
                  reads=[r_scal], writes=[r_statp])
                V(lambda e: e.tensor_tensor(spv(3), spv(0), spv(2), ALU.mult),
                  writes=[r_statp])
                stats_rows2(blk, 2, P[6], r_P[6])
                Tn(lambda e: e.matmul(P[0][:, 0:512], sel2[:, 0:128],
                                      rows2[0:2, 0:512], start=True, stop=True),
                   reads=[r_sel2, r_rows2], writes=[r_P[0]])
                Tn(lambda e: e.matmul(P[1][:, 0:512], sel2[:, 128:256],
                                      rows2[0:2, 0:512], start=True, stop=True),
                   reads=[r_sel2, r_rows2], writes=[r_P[1]])
            pieces.append(rows_bcast)

            for dch in range(DC):
                def xq_piece(dch=dch):
                    xs = dch % 2
                    zqv, r_zqv = zq_t[xs], r_zq[xs]
                    V(lambda e, dch=dch, zqv=zqv: e.tensor_tensor(
                        zqv[:, :], btx[:, dch * 512:(dch + 1) * 512],
                        P[0][:, 0:512], ALU.mult),
                      reads=[r_btx, r_P[0]], writes=[r_zqv])
                    V(lambda e, zqv=zqv: e.tensor_tensor(
                        zqv[:, :], zqv[:, :], P[1][:, 0:512], ALU.subtract),
                      reads=[r_P[1]], writes=[r_zqv])
                    V(lambda e, dch=dch, zqv=zqv, b=blk: e.tensor_scalar(
                        bb[b % 2][:, dch * 512:(dch + 1) * 512],
                        zqv[:, :], MAGIC, MAGIC, ALU.add, ALU.subtract),
                      reads=[r_zqv], writes=[res(f"bb{blk % 2}_{dch}")])
                pieces.append(xq_piece)
            return pieces

        pq_b1 = b1_prep_pieces(0)[1:]

        V(lambda e: e.tensor_reduce(sc1(SC_T0), gmx8[0:1, 0:8], AX.X, ALU.max),
          reads=[r_gmx8], writes=[r_scal])
        V(lambda e: e.tensor_scalar(sc1(SC_GXC), sc1(SC_T0), MIN_RANGE, INV_QP,
                                    ALU.max, ALU.mult),
          writes=[r_scal])
        V(lambda e: e.reciprocal(sc1(SC_SXV), sc1(SC_GXC)), writes=[r_scal])
        V(lambda e: e.tensor_tensor(sc1(SC_T1), sc1(SC_SW1V), sc1(SC_GXC), ALU.mult),
          writes=[r_scal])
        bcast_scalar(SC_SXV, SC_SXINV)
        bcast_scalar(SC_T1, SC_SS1)
        V(lambda e: e.tensor_scalar(b1ss[:, :], b1_pp[:, :], sc_col(SC_SS1), None, ALU.mult),
          reads=[r_b1, r_scal], writes=[r_b1ss])

        while pq_b1:
            pq_b1.pop(0)()

        def kvblk_dma_piece(b, c):
            def f():
                off = b * 1024 + 2 * c * 64
                soff = 2048 + b * 16 + 2 * c
                V(lambda e: e.memset(kvstage[:, :], 0.0), writes=[r_kvstage])
                V(lambda e: e.memset(denstage[:, :], 0.0), writes=[r_denstage])
                op("sync", lambda e, off=off: e.dma_start(
                    out=kvstage[0:64, 0:64], in_=d_cc1o[0:64, off:off + 64]),
                   reads=[r_cc1o], writes=[r_kvstage], sem="ld_kvb", dinc=16)
                op("sync", lambda e, off=off: e.dma_start(
                    out=kvstage[64:128, 64:128], in_=d_cc1o[0:64, off + 64:off + 128]),
                   reads=[r_cc1o], writes=[r_kvstage], sem="ld_kvb", dinc=16)
                op("sync", lambda e, soff=soff, c=c: e.dma_start(
                    out=denstage[0:64, 2 * c:2 * c + 1], in_=d_cc1o[0:64, soff:soff + 1],
                    allow_slow_non_contiguous=True),
                   reads=[r_cc1o], writes=[r_denstage], sem="ld_kvb", dinc=16)
                op("sync", lambda e, soff=soff, c=c: e.dma_start(
                    out=denstage[64:128, 2 * c + 1:2 * c + 2],
                    in_=d_cc1o[0:64, soff + 1:soff + 2],
                    allow_slow_non_contiguous=True),
                   reads=[r_cc1o], writes=[r_denstage], sem="ld_kvb", dinc=16)
                tot = ("ld_kvb", ctx.sems["ld_kvb"].count)
                r_kvstage.w, r_kvstage.rs = tot, []
                r_denstage.w, r_denstage.rs = tot, []
            return f

        def kvblk_cast_piece(b, c):
            def f():
                t = b * 8 + c
                V(lambda e, t=t: e.tensor_copy(kvblk[:, t * 128:(t + 1) * 128], kvstage[:, :]),
                  reads=[r_kvstage], writes=[r_kvblk])
                V(lambda e, t=t: e.tensor_copy(denblk[:, t * 16:(t + 1) * 16], denstage[:, :]),
                  reads=[r_denstage], writes=[r_denblk])
            return f

        def wout_piece(d):
            def f():
                DMA("sync", lambda e, d=d: e.dma_start(
                    out=wbst[d % 2][:, :], in_=d_wout[d * 128:(d + 1) * 128, :]),
                    writes=[r_wbst[d % 2]], sem=f"ld_wb{d % 2}")
                V(lambda e, d=d: e.tensor_copy(woutbf[:, d * 1024:(d + 1) * 1024],
                                               wbst[d % 2][:, :]),
                  reads=[r_wbst[d % 2]], writes=[r_woutbf])
            return f

        def w2max_piece(wci):
            def f():
                rlo = wci * 512
                DMA("sync", lambda e, rlo=rlo: e.dma_start(
                    out=bto[:, 0:4096].rearrange("p (c f) -> p c f", c=4),
                    in_=d_w2t[rlo:rlo + 512, :].rearrange("(c p) f -> p c f", p=128)),
                    writes=[r_bto], sem="ld_wm")
                V(lambda e: e.tensor_reduce(
                    tred[:, :], bto[:, 0:4096], AX.X, ALU.max, apply_absolute_value=True),
                  reads=[r_bto], writes=[r_tred])
                V(lambda e: e.tensor_tensor(
                    mx_w2[:, :], mx_w2[:, :], tred[:, :], ALU.max),
                  reads=[r_tred], writes=[r_mxw2])
            return f

        extra_b1 = {}
        kv_bc = [(b, c) for b in range(2) for c in range(8)]
        all_extras = []
        for i, (b, c) in enumerate(kv_bc):
            all_extras.append(kvblk_dma_piece(b, c))
            all_extras.append(kvblk_cast_piece(b, c))
        all_extras += [wout_piece(d) for d in range(DC)]
        all_extras += [w2max_piece(i) for i in range(8)]
        for i, p in enumerate(all_extras):
            key = min(2 + i * (NBLK - 2) // len(all_extras), NBLK - 1)
            extra_b1.setdefault(key, []).append(p)

        for blk in range(NBLK):
            pq = (b1_prep_pieces(blk + 1) if blk + 1 < NBLK else [])
            pq = pq + extra_b1.get(blk, [])

            def drain(n=1):
                for _ in range(n):
                    if pq:
                        pq.pop(0)()

            xqb = bb[blk % 2]
            for fo in range(HC):
                pf, r_pf = P[2 + fo % 4], r_P[2 + fo % 4]
                for dch in range(DC):
                    Tn(lambda e, fo=fo, dch=dch, pf=pf: e.matmul(
                        pf[:, 0:512], w1qv(dch)[:, fo * 128:(fo + 1) * 128],
                        xqb[:, dch * 512:(dch + 1) * 512],
                        start=(dch == 0), stop=(dch == DC - 1)),
                       reads=[r_w1q[fo], res(f"bb{blk % 2}_{dch}")],
                       writes=[r_pf] if dch == 0 else [],
                       wnowait=[r_pf] if dch == DC - 1 else [])
                qrt = (fo // 2) % 4
                hsl = bts[:, qrt * 1024 + (fo % 2) * 512: qrt * 1024 + (fo % 2 + 1) * 512]
                hres = r_btsh[qrt]
                extra = [r_bts] if (blk == 0 and fo < 8) else []
                Sc(lambda e, pf=pf, hsl=hsl, fo=fo: e.activation(
                    hsl, pf[:, 0:512], AF.Gelu,
                    bias=b1ss[:, fo:fo + 1], scale=sc_col(SC_SS1)),
                   reads=[r_pf, r_b1ss, r_scal], writes=[hres] + extra)
                if fo % 2 == 1:
                    V(lambda e, qrt=qrt: e.tensor_reduce(
                        tred[:, :], bts[:, qrt * 1024:(qrt + 1) * 1024],
                        AX.X, ALU.max, apply_absolute_value=True),
                      reads=[hres], writes=[r_tred])
                    V(lambda e: e.tensor_tensor(mx_h[:, :], mx_h[:, :], tred[:, :], ALU.max),
                      reads=[r_tred], writes=[r_mxh])
                    tile = blk * 16 + (fo - 1) // 2
                    seng = "gpsimd" if qrt % 2 == 0 else "scalar"
                    op(seng, lambda e, tile=tile, qrt=qrt: e.dma_start(
                        out=d_h[tile * 128:(tile + 1) * 128, :],
                        in_=bts[:, qrt * 1024:(qrt + 1) * 1024]),
                       reads=[hres], writes=[r_hd[qrt % 2]], sem=f"st_h{qrt % 2}", dinc=16)
                    drain(1)
                if blk == 0:
                    if fo + 2 < HC:
                        build_compute(fo + 2)
                    if fo + 5 < HC:
                        build_load(fo + 5)
            drain(99)

        cross_part_max(r_mxh, mx_h, SC_T0)
        DMA("gpsimd", lambda e: e.dma_start(out=oh_sb, in_=d_oh[:, :]),
            writes=[r_oh], sem="ld_cc2")
        V(lambda e: e.tensor_scalar(oh_t[0:1, 0:8], oh_sb[0:1, 0:8],
                                    sc1(SC_T0), None, ALU.mult),
          reads=[r_oh, r_scal], writes=[r_oht])
        DMA("gpsimd", lambda e: e.dma_start(out=d_cc2i[0:1, 0:8], in_=oh_t[0:1, 0:8]),
            reads=[r_oht], writes=[r_cc2i], sem="st_cc")
        op("gpsimd", lambda e: e.collective_compute(
            "AllReduce", ALU.add, replica_groups=[list(range(n_cores))],
            ins=[d_cc2i.ap()], outs=[d_cc2o.ap()]),
           reads=[r_cc2i], writes=[r_cc2o], sem="cc2", dinc=1)

        cross_part_max(r_mxw2, mx_w2, SC_SW2C)
        V(lambda e: e.tensor_scalar(sc1(SC_SW2V), sc1(SC_SW2C), MIN_RANGE, INV_QP,
                                    ALU.max, ALU.mult),
          writes=[r_scal])
        V(lambda e: e.reciprocal(sc1(SC_T0), sc1(SC_SW2V)), writes=[r_scal])
        bcast_scalar(SC_T0, SC_SWQINV)

        def w2_load(f):
            ws = f % 2
            DMA("sync", lambda e, f=f, ws=ws: e.dma_start(
                out=wbst[ws][:, :], in_=d_w2t[f * 128:(f + 1) * 128, :]),
                writes=[r_wbst[ws]], sem=f"ld_wb{ws}")

        def w2_compute(f):
            ws = f % 2
            Sc(lambda e, ws=ws: e.activation(zbld[:, :], wbst[ws][:, :], AF.Copy,
                                             bias=MAGIC, scale=sc_col(SC_SWQINV)),
               reads=[r_wbst[ws], r_scal], writes=list(r_zbld))
            wr = [r_w2q[f]] + (r_w1q if f == 0 else [])
            V(lambda e, f=f: e.tensor_scalar_add(w2qv(f)[:, :], zbld[:, :], -MAGIC),
              reads=list(r_zbld), writes=wr)

        def c_quant(blk, f):
            hs = f % 4
            zs = f % 2
            extra = ([r_bts] + r_btsh if (blk == 0 and f < 4) else [])
            tile = blk * 16 + f // 2
            leng = "sync" if hs % 2 == 0 else "gpsimd"
            op(leng, lambda e, tile=tile, f=f, hs=hs: e.dma_start(
                out=hrel_v[hs],
                in_=d_h[tile * 128:(tile + 1) * 128,
                        (f % 2) * 512:(f % 2 + 1) * 512]),
               reads=[r_hd[(f // 2) % 2]], writes=[r_hrel[hs]] + extra,
               sem=f"ld_h{hs}", dinc=16)
            Sc(lambda e, hs=hs, zs=zs: e.activation(
                zq_t[zs][:, :], hrel_v[hs], AF.Copy,
                bias=MAGIC, scale=sc_col(SC_SHINV)),
               reads=[r_hrel[hs], r_scal], writes=[r_zq[zs]])
            V(lambda e, hs=hs, zs=zs: e.tensor_scalar_add(
                hq_t[hs][:, :], zq_t[zs][:, :], -MAGIC),
              reads=[r_zq[zs]], writes=[r_hq[hs]])

        def c_loads(blk):
            DMA("scalar", lambda e, b=blk: e.dma_start(
                out=btx[:, :], in_=d_xfm[b * 128:(b + 1) * 128, :]),
                writes=[r_btx], sem="ld_x0")
            DMA("scalar", lambda e, b=blk: e.dma_start(
                out=bb[1][:, :], in_=d_q1[b * 128:(b + 1) * 128, :]),
                reads=[r_q1d],
                writes=[r_bb[1]] + [res(f"bb1_{i}") for i in range(DC)], sem="ld_q1")

        def c_attn(blk):
            bat = blk // NB2
            for c in range(8):
                t = bat * 8 + c
                Tn(lambda e, t=t, c=c: e.matmul(
                    P[0][0:16, 0:512], denblk[:, t * 16:(t + 1) * 16],
                    bb[1][:, c * 512:(c + 1) * 512],
                    start=(c == 0), stop=(c == 7)),
                   reads=[r_denblk, r_bb[1]],
                   writes=[r_P[0]] if c == 0 else [],
                   wnowait=[r_P[0]] if c == 7 else [])
            Sc(lambda e: e.activation(den_sb[0:16, 0:512], P[0][0:16, 0:512],
                                      AF.Ln, bias=scal[0:16, SC_DEPS:SC_DEPS + 1]),
               reads=[r_P[0], r_scal], writes=[r_densb])
            Sc(lambda e: e.activation(rec_bf[0:16, 0:512], den_sb[0:16, 0:512],
                                      AF.Exp, scale=-1.0),
               reads=[r_densb], writes=[r_recbf])
            def qkv_mm(c):
                t = bat * 8 + c
                pqb, r_pqb = P[1 + c % 4], r_P[1 + c % 4]
                Tn(lambda e, t=t, c=c, pqb=pqb: e.matmul(
                    pqb[:, 0:512], kvblk[:, t * 128:(t + 1) * 128],
                    bb[1][:, c * 512:(c + 1) * 512], start=True, stop=True),
                   reads=[r_kvblk, r_bb[1]], writes=[r_pqb])
            for c in range(4):
                qkv_mm(c)
            rslot = [recb_v[0], recb_v[1], xn_t[0][:, :], xn_t[1][:, :]]
            r_rslot = [r_recb[0], r_recb[1], r_xn[0], r_xn[1]]
            rbank = [5, 6, 7, 0]

            def recb_bcast(c):
                pr, r_pr = P[rbank[c % 4]], r_P[rbank[c % 4]]
                Tn(lambda e, c=c, pr=pr: e.matmul(
                    pr[:, 0:512], maskc[:, c * 128:(c + 1) * 128],
                    rec_bf[0:16, 0:512], start=True, stop=True),
                   reads=[r_maskc, r_recbf], writes=[r_pr])
            for c in range(4):
                recb_bcast(c)
            for c in range(8):
                pqb, r_pqb = P[1 + c % 4], r_P[1 + c % 4]
                pr, r_pr = P[rbank[c % 4]], r_P[rbank[c % 4]]
                cs = c % 4
                Sc(lambda e, pr=pr, cs=cs: e.activation(
                    rslot[cs], pr[:, 0:512], AF.Copy),
                   reads=[r_pr],
                   writes=[r_rslot[cs]])
                V(lambda e, pqb=pqb, cs=cs, c=c: e.tensor_tensor(
                    bb[2][:, c * 512:(c + 1) * 512], pqb[:, 0:512],
                    rslot[cs], ALU.mult),
                  reads=[r_pqb, r_rslot[cs]], writes=[r_bb[2]])
                if c + 4 < 8:
                    qkv_mm(c + 4)
                    recb_bcast(c + 4)
            if blk > 0:
                for f in range(4):
                    c_quant(blk, f)

            def wout_group(do):
                pw, r_pw = P[5 + do % 2], r_P[5 + do % 2]
                for di in range(DC):
                    Tn(lambda e, do=do, di=di, pw=pw: e.matmul(
                        pw[:, 0:512],
                        woutbf[:, di * 1024 + do * 128: di * 1024 + (do + 1) * 128],
                        bb[2][:, di * 512:(di + 1) * 512],
                        start=(di == 0), stop=(di == DC - 1)),
                       reads=[r_woutbf, r_bb[2]],
                       writes=[r_pw] if di == 0 else [],
                       wnowait=[r_pw] if di == DC - 1 else [])

            stats_rows2(blk, 4, P[0], r_P[0])
            wout_group(0)
            wout_group(1)
            Tn(lambda e: e.matmul(P[1][:, 0:512], sel2[:, 128:256], rows2[0:2, 0:512],
                                  start=True, stop=True),
               reads=[r_sel2, r_rows2], writes=[r_P[1]])
            Tn(lambda e: e.matmul(P[2][:, 0:512], sel2[:, 0:128], rows2[0:2, 0:512],
                                  start=True, stop=True),
               reads=[r_sel2, r_rows2], writes=[r_P[2]])
            for do in range(DC):
                V(lambda e, do=do: e.tensor_tensor(
                    bto[:, do * 512:(do + 1) * 512],
                    btx[:, do * 512:(do + 1) * 512], P[1][:, 0:512], ALU.mult),
                  reads=[r_btx, r_P[1]], writes=[r_bto])
                V(lambda e, do=do: e.tensor_tensor(
                    bto[:, do * 512:(do + 1) * 512],
                    bto[:, do * 512:(do + 1) * 512], P[2][:, 0:512], ALU.subtract),
                  reads=[r_P[2]], writes=[r_bto])
                pw, r_pw = P[5 + do % 2], r_P[5 + do % 2]
                V(lambda e, do=do, pw=pw: e.tensor_tensor(
                    bto[:, do * 512:(do + 1) * 512], pw[:, 0:512],
                    bto[:, do * 512:(do + 1) * 512], ALU.add),
                  reads=[r_pw], writes=[r_bto])
                if do + 2 < DC:
                    wout_group(do + 2)

        def c_ffn2(blk):
            if blk == 0:
                w2_load(0)
                w2_load(1)
            for f in range(HC):
                if blk == 0:
                    w2_compute(f)
                    if f + 2 < HC:
                        w2_load(f + 2)
                hs = f % 4
                if blk == 0 or f >= 4:
                    c_quant(blk, f)
                for do in range(DC):
                    Tn(lambda e, f=f, do=do, hs=hs: e.matmul(
                        P[do][:, 0:512], w2qv(f)[:, do * 128:(do + 1) * 128],
                        hq_t[hs][:, :], start=(f == 0), stop=(f == HC - 1)),
                       reads=[r_w2q[f], r_hq[hs]],
                       writes=[r_P[do]] if f == 0 else [],
                       wnowait=[r_P[do]] if f == HC - 1 else [])
                if blk + 1 < NBLK and f == 24:
                    c_loads(blk + 1)
            for do in range(DC):
                ds = do % 2
                Sc(lambda e, do=do, ds=ds: e.activation(
                    et_v[ds], P[do][:, 0:512], AF.Identity,
                    bias=b2ss[:, do:do + 1], scale=sc_col(SC_SS2)),
                   reads=[r_P[do], r_b2ss, r_scal],
                   writes=[r_et[ds]])
                V(lambda e, do=do, ds=ds: e.tensor_tensor(
                    bto[:, do * 512:(do + 1) * 512], et_v[ds],
                    bto[:, do * 512:(do + 1) * 512], ALU.add),
                  reads=[r_et[ds]], writes=[r_bto])
            DMA("scalar", lambda e, b=blk: e.dma_start(
                out=d_out[b * 128:(b + 1) * 128, :], in_=bto[:, :]),
                reads=[r_bto], writes=[r_out], sem="st_out")

        c_loads(0)
        c_attn(0)

        DMA("gpsimd", lambda e: e.dma_start(out=gmx8[0:1, 0:8], in_=d_cc2o[0:1, 0:8]),
            reads=[r_cc2o], writes=[r_gmx8], sem="ld_cc2")
        V(lambda e: e.tensor_reduce(sc1(SC_GHC), gmx8[0:1, 0:8], AX.X, ALU.max),
          reads=[r_gmx8], writes=[r_scal])
        V(lambda e: e.tensor_scalar(sc1(SC_SHV), sc1(SC_GHC), MIN_RANGE, INV_QP,
                                    ALU.max, ALU.mult),
          writes=[r_scal])
        V(lambda e: e.reciprocal(sc1(SC_T0), sc1(SC_SHV)), writes=[r_scal])
        V(lambda e: e.tensor_tensor(sc1(SC_T1), sc1(SC_SW2V), sc1(SC_SHV), ALU.mult),
          writes=[r_scal])
        bcast_scalar(SC_T0, SC_SHINV)
        bcast_scalar(SC_T1, SC_SS2)
        V(lambda e: e.tensor_scalar(b2ss[:, :], b2_pp[:, :], sc_col(SC_SS2), None, ALU.mult),
          reads=[r_b2, r_scal], writes=[r_b2ss])

        c_ffn2(0)
        for blk in range(1, NBLK):
            c_attn(blk)
            c_ffn2(blk)

        ctx.raw_wait("gpsimd", "st_out", 16 * NBLK)

    with nc.Block() as block:

        @block.sync
        def _(sync):
            schedule(_Ctx("sync", sync, fresh_sems()))

        @block.gpsimd
        def _(gpsimd):
            schedule(_Ctx("gpsimd", gpsimd, fresh_sems()))

        @block.vector
        def _(vector):
            schedule(_Ctx("vector", vector, fresh_sems()))

        @block.scalar
        def _(scalar):
            schedule(_Ctx("scalar", scalar, fresh_sems()))

        @block.tensor
        def _(tensor):
            schedule(_Ctx("tensor", tensor, fresh_sems()))

    es.close()
    return nc


_NC_CACHE = {}


def _get_program(T_core, n_cores):
    key = (T_core, n_cores)
    if key not in _NC_CACHE:
        _NC_CACHE[key] = build_program(T_core, n_cores)
    return _NC_CACHE[key]


def kernel(L0_attn_x, L0_attn_gamma, L0_attn_beta, L0_attn_Wqkv, L0_attn_Wout,
           L0_ffn_w1, L0_ffn_b1, L0_ffn_w2, L0_ffn_b2):
    x = np.asarray(L0_attn_x, dtype=np.float32)
    B, L, D_ = x.shape
    n = 8
    Lc = L // n
    T_core = B * Lc

    nc = _get_program(T_core, n)

    wqkv = np.ascontiguousarray(np.asarray(L0_attn_Wqkv, dtype=np.float32))
    wout = np.ascontiguousarray(np.asarray(L0_attn_Wout, dtype=np.float32))
    w1 = np.asarray(L0_ffn_w1, dtype=np.float32)
    w1b = np.ascontiguousarray(
        w1.reshape(HC, 128, DC, 128).transpose(0, 3, 2, 1).reshape(HF, D))
    w2t = np.ascontiguousarray(np.asarray(L0_ffn_w2, dtype=np.float32).T)
    gam = np.ascontiguousarray(np.asarray(L0_attn_gamma, dtype=np.float32).reshape(DC, 128).T)
    beta_pp = np.ascontiguousarray(np.asarray(L0_attn_beta, dtype=np.float32).reshape(DC, 128).T)
    b1_pp = np.ascontiguousarray(np.asarray(L0_ffn_b1, dtype=np.float32).reshape(HC, 128).T)
    b2_pp = np.ascontiguousarray(np.asarray(L0_ffn_b2, dtype=np.float32).reshape(DC, 128).T)
    ident = np.eye(128, dtype=np.float32)
    sel2c = np.zeros((2, 256), np.float32)
    sel2c[0, 0:128] = 1.0
    sel2c[1, 128:256] = 1.0
    maskc_f = np.zeros((16, 1024), np.float32)
    for cc in range(8):
        maskc_f[2 * cc, cc * 128:cc * 128 + 64] = 1.0
        maskc_f[2 * cc + 1, cc * 128 + 64:(cc + 1) * 128] = 1.0

    in_maps = []
    nbt = T_core // 512
    for c in range(n):
        xs = x[:, c * Lc:(c + 1) * Lc, :]
        x_tm0 = xs.reshape(T_core, D_)
        x_fm = np.ascontiguousarray(
            x_tm0.reshape(nbt, 512, 8, 128).transpose(0, 3, 2, 1).reshape(nbt * 128, 4096))
        x_tm = np.ascontiguousarray(
            x_tm0.reshape(nbt, 4, 128, 1024).transpose(0, 2, 1, 3).reshape(nbt * 128, 4096))
        oh = np.zeros((1, 8), np.float32)
        oh[0, c] = 1.0
        in_maps.append(dict(x_fm=x_fm, x_tm=x_tm, wqkv=wqkv, wout=wout,
                            w1b=w1b, w2t=w2t, gam_pp=gam, beta_pp=beta_pp,
                            b1_pp=b1_pp, b2_pp=b2_pp, ident=ident, onehot=oh,
                            sel2c=sel2c, maskc_f=maskc_f))

    res = run_bass_kernel_spmd(nc, in_maps, core_ids=list(range(n)))
    parts = []
    for r in res.results:
        arr = np.asarray(r["out_fm"])
        out_core = arr.reshape(nbt, 128, 8, 512).transpose(0, 3, 2, 1).reshape(T_core, D_)
        parts.append(out_core.reshape(B, Lc, D_))
    return np.ascontiguousarray(np.concatenate(parts, axis=1)).astype(np.float32)


# version 32
# speedup vs baseline: 1.0006x; 1.0006x over previous
import contextlib
import numpy as np

import concourse.bass as bass
import concourse.mybir as mybir
from concourse.bass_utils import run_bass_kernel_spmd

F32 = mybir.dt.float32
BF16 = mybir.dt.bfloat16
AF = mybir.ActivationFunctionType
ALU = mybir.AluOpType
AX = mybir.AxisListType

MAGIC = 12582912.0
LN_EPS = 1e-5
DEN_EPS = 1e-6
MIN_RANGE = 1e-6
INV_QP = float(np.float32(1.0) / np.float32(127.0))

D = 1024
DC = 8
HF = 4096
HC = 32
TB = 512


class _Sem:
    __slots__ = ("name", "handle", "count")

    def __init__(self, name, handle):
        self.name, self.handle, self.count = name, handle, 0


class _Res:
    __slots__ = ("name", "w", "rs")

    def __init__(self, name):
        self.name, self.w, self.rs = name, None, []


class _Ctx:

    def __init__(self, active, engobj, sems):
        self.active = active
        self.engobj = engobj
        self.sems = sems
        self.marks = {}

    def _wait(self, engname, tok):
        if tok is None:
            return
        semname, val = tok
        m = self.marks.setdefault(engname, {})
        if m.get(semname, 0) >= val:
            return
        m[semname] = val
        if self.active == engname:
            self.engobj.wait_ge(self.sems[semname].handle, val)

    def op(self, engname, emit, reads=(), writes=(), wnowait=(), sem=None, dinc=1,
           track=True):
        for r in reads:
            self._wait(engname, r.w)
        for w in writes:
            self._wait(engname, w.w)
            for t in w.rs:
                self._wait(engname, t)
        s = self.sems[sem if sem is not None else ("s_" + engname)]
        inst = emit(self.engobj) if self.active == engname else None
        s.count += dinc
        if inst is not None:
            inst.then_inc(s.handle, dinc)
        if not track:
            return None
        tok = (s.name, s.count)
        for r in reads:
            r.rs.append(tok)
        for w in writes:
            w.w, w.rs = tok, []
        for w in wnowait:
            w.w, w.rs = tok, []
        return tok

    def raw_wait(self, engname, semname, val):
        self._wait(engname, (semname, val))


def build_program(T_core, n_cores):
    NBLK = T_core // TB
    NB2 = NBLK // 2
    assert NBLK % 2 == 0 and NB2 >= 1

    nc = bass.Bass("TRN2", target_bir_lowering=False)

    NBT = T_core // 512
    d_xfm = nc.dram_tensor("x_fm", [NBT * 128, 4096], F32, kind="ExternalInput")
    d_xtm = nc.dram_tensor("x_tm", [NBT * 128, 4096], F32, kind="ExternalInput")
    d_wqkv = nc.dram_tensor("wqkv", [D, 3 * D], F32, kind="ExternalInput")
    d_wout = nc.dram_tensor("wout", [D, D], F32, kind="ExternalInput")
    d_w1b = nc.dram_tensor("w1b", [HF, D], F32, kind="ExternalInput")
    d_w2t = nc.dram_tensor("w2t", [HF, D], F32, kind="ExternalInput")
    d_gam = nc.dram_tensor("gam_pp", [128, DC], F32, kind="ExternalInput")
    d_beta = nc.dram_tensor("beta_pp", [128, DC], F32, kind="ExternalInput")
    d_b1 = nc.dram_tensor("b1_pp", [128, HC], F32, kind="ExternalInput")
    d_b2 = nc.dram_tensor("b2_pp", [128, DC], F32, kind="ExternalInput")
    d_ident = nc.dram_tensor("ident", [128, 128], F32, kind="ExternalInput")
    d_oh = nc.dram_tensor("onehot", [1, 8], F32, kind="ExternalInput")
    d_sel2 = nc.dram_tensor("sel2c", [2, 256], F32, kind="ExternalInput")
    d_mask = nc.dram_tensor("maskc_f", [16, 1024], F32, kind="ExternalInput")
    d_out = nc.dram_tensor("out_fm", [NBT * 128, 4096], F32, kind="ExternalOutput")

    d_q1 = nc.dram_tensor("q1_spill", [NBT * 128, 4096], BF16)
    d_h = nc.dram_tensor("h_spill", [(T_core // 512) * 16 * 128, 1024], F32)
    d_cc1i = nc.dram_tensor("cc1_in", [64, 2080], F32)
    d_cc1o = nc.dram_tensor("cc1_out", [64, 2080], F32)
    d_cc1ai = nc.dram_tensor("cc1a_in", [1, 8], F32)
    d_cc1ao = nc.dram_tensor("cc1a_out", [1, 8], F32)
    d_cc2i = nc.dram_tensor("cc2_in", [1, 8], F32)
    d_cc2o = nc.dram_tensor("cc2_out", [1, 8], F32)

    A = nc.alloc_sbuf_tensor
    btx = A("btx", [128, 4096], F32)
    bts = A("bts", [128, 4096], F32)
    bto = A("bto", [128, 4096], F32)
    bb = [A(f"bb{i}", [128, 4096], BF16) for i in range(3)]
    arena = A("arena_w", [128, 32768], BF16)
    woutbf = A("wout_bf", [128, 8192], BF16)
    wbst = [A(f"wbst{i}", [128, 1024], F32) for i in range(2)]
    xn_t = [A(f"xn{i}", [128, 512], F32) for i in range(2)]
    m_t = [A("m0", [128, 1024], F32)] * 2
    e_t = [A("e0", [128, 1024], F32)] * 2
    hq_t = [A(f"hq{i}", [128, 512], BF16) for i in range(4)]
    k1_tm = [A(f"k1tm{i}", [128, 1024], BF16) for i in range(2)]
    v65 = [A(f"v65_{i}", [128, 1040], BF16) for i in range(2)]
    kvacc = A("kvacc", [64, 2080], F32)
    kvblk = A("kvblk", [128, 2048], BF16)
    denblk = A("denblk", [128, 256], BF16)
    maskc = A("maskc", [16, 1024], BF16)
    sel2 = A("sel2", [2, 256], F32)
    gam_pp = A("gam_pp_sb", [128, DC], F32)
    ones_row = None
    beta_pp = A("beta_pp_sb", [128, DC], F32)
    b1_pp = A("b1_pp_sb", [128, HC], F32)
    b2_pp = A("b2_pp_sb", [128, DC], F32)
    b1ss = A("b1ss", [128, HC], F32)
    b2ss = A("b2ss", [128, DC], F32)
    ident_sb = A("ident_sb", [128, 128], F32)
    statp = A("statp", [128, NBLK * 4 * 6], F32)
    bnst = A("bnst", [128, 12], F32)
    agg8 = A("agg8", [128, 8], F32)
    tmp4 = A("tmp4", [128, 8], F32)
    rows2 = A("rows2", [2, 512], F32)
    den_sb = A("den_sb", [16, 512], F32)
    rec_bf = A("rec_bf", [16, 512], BF16)
    kvstage = A("kvstage", [128, 128], F32)
    denstage = A("denstage", [128, 16], F32)
    mx_xn = A("mx_xn", [128, 1], F32)
    mx_h = A("mx_h", [128, 1], F32)
    mx_w1 = A("mx_w1", [128, 1], F32)
    mx_w2 = A("mx_w2", [128, 1], F32)
    tred = A("tred", [128, 1], F32)
    scal = A("scal", [128, 24], F32)
    oh_t = A("oh_t", [1, 8], F32)

    SC_SXINV, SC_SWQINV, SC_SS1, SC_SHINV, SC_SS2 = 0, 1, 2, 3, 4
    SC_GXC, SC_SXV = 8, 9
    SC_SW1C, SC_SW1V, SC_SW1I = 10, 11, 12
    SC_GHC, SC_SHV = 13, 14
    SC_SW2C, SC_SW2V = 15, 16
    SC_T0, SC_T1, SC_DEPS, SC_EPS = 17, 18, 22, 23

    zbld = m_t[0]
    gam_row = None
    r2row = den_sb
    r2rowbf = rec_bf

    zq_t = [e_t[0][:, 0:512], e_t[0][:, 512:1024]]
    recb_v = [m_t[1][:, 0:512], m_t[1][:, 512:1024]]
    et_v = [e_t[1][:, 0:512], e_t[1][:, 512:1024]]

    gmx8 = oh_t
    ones_row = sel2
    oh_sb = bnst[0:1, 0:8]

    P = [nc.alloc_psum_tensor(f"P{i}", [128, 512], F32) for i in range(8)]

    hrel_v = [bts[:, i * 512:(i + 1) * 512] for i in range(4)]

    es = contextlib.ExitStack()
    semnames = (["s_sync", "s_gpsimd", "s_vector", "s_scalar", "s_tensor",
                 "cc1", "cc1a", "cc2"] +
                ["ld_x0", "ld_x1", "ld_t", "ld_misc", "ld_wq", "ld_h0",
                 "ld_h1", "ld_h2", "ld_h3", "ld_q1", "ld_wb0", "ld_wb1", "ld_kvb", "ld_kvm", "ld_wm",
                 "ld_cc2",
                 "ld_cc"] +
                ["st_q0", "st_q1", "st_h0", "st_h1", "st_out", "st_cc"])
    handles = {n: es.enter_context(nc.semaphore(n)) for n in semnames}

    def fresh_sems():
        return {n: _Sem(n, handles[n]) for n in semnames}

    def wqkv_bf(d):
        return arena[:, d * 3072:(d + 1) * 3072]

    def w1qv(d):
        return arena[:, d * 4096:(d + 1) * 4096]

    def w2qv(f):
        return arena[:, f * 1024:(f + 1) * 1024]

    def schedule(ctx):
        op = ctx.op
        R = {}

        def res(name):
            if name not in R:
                R[name] = _Res(name)
            return R[name]

        r_btx = res("btx")
        r_bts = res("bts")
        r_bto = res("bto")
        r_btsh = [res(f"btsh{i}") for i in range(4)]
        r_hrel = [res(f"hrel{i}") for i in range(4)]
        r_bb = [res(f"bb{i}") for i in range(3)]
        r_wq = [res(f"wq{d}") for d in range(DC)]
        r_w1q = [res(f"w1q{fo}") for fo in range(HC)]
        r_w2q = [res(f"w2q{f}") for f in range(HC)]
        r_woutbf = res("woutbf")
        r_wbst = [res(f"wbst{i}") for i in range(2)]
        r_zbld = [res("m0h0"), res("m0h1")]
        r_xn = [res(f"xn{i}") for i in range(2)]
        r_zq = [res("e0h0"), res("e0h1")]
        r_m0 = [res("m0h0"), res("m0h1")]
        r_m1 = r_m0
        r_e0 = [res("e0h0"), res("e0h1")]
        r_e1 = r_e0
        r_hq = [res(f"hq{i}") for i in range(4)]
        r_et = [res("e0h0"), res("e0h1")]
        r_k1 = [res(f"k1tm{i}") for i in range(2)]
        r_v65 = [res(f"v65_{i}") for i in range(2)]
        r_kvacc = res("kvacc")
        r_kvblk = res("kvblk")
        r_denblk = res("denblk")
        r_maskc = res("maskc")
        r_sel2 = res("sel2")
        r_ones = res("sel2")
        r_gampp = res("gam_pp_sb")
        r_beta = res("beta_pp")
        r_b1 = res("b1_pp")
        r_b2 = res("b2_pp")
        r_b1ss = res("b1ss")
        r_b2ss = res("b2ss")
        r_ident = res("ident_sb")
        r_statp = res("statp")
        r_bnst = res("bnst")
        r_agg8 = res("agg8")
        r_tmp4 = res("tmp4")
        r_rows2 = res("rows2")
        r_r2row = res("den_sb")
        r_r2rowbf = res("rec_bf")
        r_densb = res("den_sb")
        r_recbf = res("rec_bf")
        r_recb = [res("m0h0"), res("m0h1")]
        r_kvstage = res("kvstage")
        r_denstage = res("denstage")
        r_mxxn = res("mx_xn")
        r_mxh = res("mx_h")
        r_mxw1 = res("mx_w1")
        r_mxw2 = res("mx_w2")
        r_tred = res("tred")
        r_epsp = res("scal")
        r_scal = res("scal")
        r_oh = res("bnst")
        r_oht = res("oh_t")
        r_gmx8 = res("oh_t")
        r_P = [res(f"P{i}") for i in range(8)]
        r_q1d = res("q1_dram")
        r_hd = [res("h_dram0"), res("h_dram1")]
        r_cc1i = res("cc1i")
        r_cc1o = res("cc1o")
        r_cc1ai = res("cc1ai")
        r_cc1ao = res("cc1ao")
        r_cc2i = res("cc2i")
        r_cc2o = res("cc2o")
        r_out = res("out_fm")

        def V(emit, reads=(), writes=(), wnowait=()):
            return op("vector", emit, reads, writes, wnowait)

        def Sc(emit, reads=(), writes=(), wnowait=()):
            return op("scalar", emit, reads, writes, wnowait)

        def G(emit, reads=(), writes=(), wnowait=()):
            return op("gpsimd", emit, reads, writes, wnowait)

        def Tn(emit, reads=(), writes=(), wnowait=()):
            return op("tensor", emit, reads, writes, wnowait)

        def DMA(eng, emit, reads=(), writes=(), sem=None):
            return op(eng, emit, reads, writes, sem=sem, dinc=16)

        def sc_col(i):
            return scal[:, i:i + 1]

        def sc1(i):
            return scal[0:1, i:i + 1]

        def stats_rows(blk, c0, dst_rows, r_dst):
            for s in range(4):
                base = (blk * 4 + s) * 6
                Tn(lambda e, b=base, s=s: e.transpose(
                    P[6][0:2, s * 128:(s + 1) * 128],
                    statp[:, b + c0: b + c0 + 2], ident_sb[:, :]),
                   reads=[r_statp, r_ident],
                   writes=[r_P[6]] if s == 0 else [],
                   wnowait=[r_P[6]] if s == 3 else [])
            V(lambda e: e.tensor_copy(dst_rows[0:2, 0:512], P[6][0:2, 0:512]),
              reads=[r_P[6]], writes=[r_dst])

        def cross_part_max(r_acc, acc_t, out_col):
            Tn(lambda e: e.transpose(P[7][0:1, 0:128], acc_t[:, :], ident_sb[:, :]),
               reads=[r_acc, r_ident], writes=[r_P[7]])
            V(lambda e, oc=out_col: e.tensor_reduce(
                scal[0:1, oc:oc + 1], P[7][0:1, 0:128], AX.X, ALU.max),
              reads=[r_P[7]], writes=[r_scal])

        def bcast_scalar(src_col, dst_col):
            Tn(lambda e: e.matmul(P[7][:, 0:1], sel2[0:1, 0:128], sc1(src_col),
                                  start=True, stop=True),
               reads=[r_ones, r_scal], writes=[r_P[7]])
            V(lambda e: e.tensor_copy(sc_col(dst_col), P[7][:, 0:1]),
              reads=[r_P[7]], writes=[r_scal])

        const_dsts = [(r_gampp, gam_pp, d_gam), (r_beta, beta_pp, d_beta),
                      (r_b1, b1_pp, d_b1), (r_b2, b2_pp, d_b2),
                      (r_ident, ident_sb, d_ident),
            (r_sel2, sel2, d_sel2)]
        for (dst, dsth, src) in const_dsts:
            DMA("sync", lambda e, dh=dsth, s=src: e.dma_start(
                out=dh[:, :] if hasattr(dh, "ap") else dh, in_=s[:, :]),
                writes=[dst], sem="ld_misc")

        V(lambda e: e.memset(scal[:, SC_EPS:SC_EPS + 1], LN_EPS), writes=[r_epsp])
        V(lambda e: e.memset(scal[:, SC_DEPS:SC_DEPS + 1], DEN_EPS), writes=[r_epsp])
        DMA("sync", lambda e: e.dma_start(out=bto[0:16, 1024:2048], in_=d_mask[:, :]),
            writes=[r_bto], sem="ld_misc")
        _misc_total = ctx.sems["ld_misc"].count
        for (dst, _, _) in const_dsts:
            dst.w = ("ld_misc", _misc_total)
        r_bto.w = ("ld_misc", _misc_total)
        V(lambda e: e.tensor_copy(maskc[0:16, 0:1024], bto[0:16, 1024:2048]),
          reads=[r_bto], writes=[r_maskc])
        for mres, mt in [(r_mxxn, mx_xn), (r_mxh, mx_h), (r_mxw1, mx_w1), (r_mxw2, mx_w2)]:
            V(lambda e, m=mt: e.memset(m[:, :], 0.0), writes=[mres])
        V(lambda e: e.memset(kvacc[:, :], 0.0), writes=[r_kvacc])
        for s in range(2):
            V(lambda e, s=s: e.memset(
                v65[s][:, :].rearrange("p (h v) -> p h v", v=65)[:, :, 64:65], 1.0),
              writes=[r_v65[s]])

        for d in range(DC):
            DMA("sync", lambda e, d=d: e.dma_start(
                out=bto[:, 0:3072], in_=d_wqkv[d * 128:(d + 1) * 128, :]),
                writes=[r_bto], sem="ld_wq")
            V(lambda e, d=d: e.tensor_copy(wqkv_bf(d)[:, :], bto[:, 0:3072]),
              reads=[r_bto], writes=[r_wq[d]])


        def stats_rows2(blk, c0, pbank, r_pbank):
            for s in range(4):
                base = (blk * 4 + s) * 6
                Tn(lambda e, b=base, s=s, pb=pbank: e.transpose(
                    pb[0:2, s * 128:(s + 1) * 128],
                    statp[:, b + c0: b + c0 + 2], ident_sb[:, :]),
                   reads=[r_statp, r_ident],
                   writes=[r_pbank] if s == 0 else [],
                   wnowait=[r_pbank] if s == 3 else [])
            V(lambda e, pb=pbank: e.tensor_copy(rows2[0:2, 0:512], pb[0:2, 0:512]),
              reads=[r_pbank], writes=[r_rows2])

        def emit_stats(blk):
            pieces = []

            def bn_piece(slo, shi):
                def f():
                    for s in range(slo, shi):
                        xv = bts[:, s * 1024:(s + 1) * 1024]
                        V(lambda e, xv=xv: e.bn_stats(bnst[:, 0:6], xv[:, 0:512]),
                          reads=[r_bts], writes=[r_bnst])
                        V(lambda e, xv=xv: e.bn_stats(bnst[:, 6:12], xv[:, 512:1024]),
                          reads=[r_bts], writes=[r_bnst])
                        V(lambda e, s=s: e.bn_aggr(
                            agg8[:, :].rearrange("p (k s) -> p k s", k=2)[:, :, s:s + 1],
                            bnst[:, 0:12]),
                          reads=[r_bnst], writes=[r_agg8])
                return f
            pieces.append(bn_piece(0, 2))
            pieces.append(bn_piece(2, 4))

            def chain():
                bb0 = blk * 24
                MU = agg8[:, 0:4]
                VAR = agg8[:, 4:8]

                def spv(c):
                    return statp[:, bb0:bb0 + 24].rearrange(
                        "p (s c) -> p c s", c=6)[:, c:c + 1, :]
                Sc(lambda e: e.activation(tmp4[:, 0:4], VAR, AF.Ln,
                                          bias=scal[:, SC_EPS:SC_EPS + 1]),
                   reads=[r_agg8, r_scal], writes=[r_tmp4])
                Sc(lambda e: e.activation(spv(1), tmp4[:, 0:4], AF.Exp, scale=-0.5),
                   reads=[r_tmp4], writes=[r_statp])
                V(lambda e: e.tensor_tensor(tmp4[:, 4:8], VAR, spv(1), ALU.mult),
                  reads=[r_agg8, r_statp], writes=[r_tmp4])
                V(lambda e: e.tensor_tensor(tmp4[:, 4:8], tmp4[:, 4:8], spv(1), ALU.mult),
                  reads=[r_statp], writes=[r_tmp4])
                Sc(lambda e: e.activation(tmp4[:, 0:4], tmp4[:, 4:8], AF.Ln,
                                          bias=scal[:, SC_EPS:SC_EPS + 1]),
                   reads=[r_tmp4, r_scal], writes=[r_tmp4])
                Sc(lambda e: e.activation(tmp4[:, 4:8], tmp4[:, 0:4], AF.Exp, scale=-0.5),
                   reads=[r_tmp4], writes=[r_tmp4])
                V(lambda e: e.tensor_tensor(spv(2), spv(1), tmp4[:, 4:8], ALU.mult),
                  reads=[r_tmp4], writes=[r_statp])
                V(lambda e: e.tensor_tensor(spv(3), MU, spv(2), ALU.mult),
                  reads=[r_agg8], writes=[r_statp])
                V(lambda e: e.tensor_tensor(spv(4), MU, spv(1), ALU.mult),
                  reads=[r_agg8], writes=[r_statp])
                V(lambda e: e.tensor_scalar_add(spv(5), spv(1), 1.0),
                  writes=[r_statp])
                V(lambda e: e.tensor_copy(spv(0), MU),
                  reads=[r_agg8], writes=[r_statp])
            pieces.append(chain)

            def maxdev(slo, shi):
                def f():
                    bb0 = blk * 24
                    for s in range(slo, shi):
                        xv = bts[:, s * 1024:(s + 1) * 1024]
                        V(lambda e, xv=xv, s=s: e.tensor_scalar(
                            m_t[0][:, :], xv, agg8[:, s:s + 1], None, ALU.subtract),
                          reads=[r_bts, r_agg8], writes=list(r_m0))
                        V(lambda e: e.tensor_reduce(
                            tred[:, :], m_t[0][:, :], AX.X, ALU.max,
                            apply_absolute_value=True),
                          reads=list(r_m0), writes=[r_tred])
                        V(lambda e, s=s, bb0=bb0: e.tensor_tensor(
                            tred[:, :], tred[:, :],
                            statp[:, bb0 + s * 6 + 1:bb0 + s * 6 + 2], ALU.mult),
                          reads=[r_statp], writes=[r_tred])
                        V(lambda e: e.tensor_tensor(
                            mx_xn[:, :], mx_xn[:, :], tred[:, :], ALU.max),
                          reads=[r_tred], writes=[r_mxxn])
                return f
            pieces.append(maxdev(0, 2))
            pieces.append(maxdev(2, 4))
            return pieces

        def a_prep_pieces(blk):
            pieces = []

            def loads():
                DMA("sync", lambda e, b=blk: e.dma_start(
                    out=btx[:, :], in_=d_xfm[b * 128:(b + 1) * 128, :]),
                    writes=[r_btx], sem="ld_x0")
                DMA("sync", lambda e, b=blk: e.dma_start(
                    out=bts[:, :], in_=d_xtm[b * 128:(b + 1) * 128, :]),
                    writes=[r_bts] + r_btsh, sem="ld_t")
            pieces.append(("early", loads))
            pieces.extend(("early", p) for p in emit_stats(blk))

            def rows_t():
                stats_rows2(blk, 2, P[4], r_P[4])
            pieces.append(("latev", rows_t))

            def rows_mm():
                Tn(lambda e: e.matmul(P[4][:, 0:512], sel2[:, 0:128],
                                      rows2[0:2, 0:512], start=True, stop=True),
                   reads=[r_sel2, r_rows2], writes=[r_P[4]])
                Tn(lambda e: e.matmul(P[5][:, 0:512], sel2[:, 128:256],
                                      rows2[0:2, 0:512], start=True, stop=True),
                   reads=[r_sel2, r_rows2], writes=[r_P[5]])

            xn2 = bb[blk % 2]
            for dch in range(DC):
                def t2v(dch=dch):
                    xs = dch % 2
                    xnv, r_xnv = xn_t[xs], r_xn[xs]
                    V(lambda e, dch=dch, xnv=xnv: e.tensor_tensor(
                        xnv[:, :], btx[:, dch * 512:(dch + 1) * 512],
                        P[4][:, 0:512], ALU.mult),
                      reads=[r_btx, r_P[4]], writes=[r_xnv])
                    V(lambda e, xnv=xnv: e.tensor_tensor(
                        xnv[:, :], xnv[:, :], P[5][:, 0:512], ALU.subtract),
                      reads=[r_P[5]], writes=[r_xnv])

                def t2s(dch=dch, xn2=xn2):
                    xs = dch % 2
                    xnv, r_xnv = xn_t[xs], r_xn[xs]
                    Sc(lambda e, dch=dch, xnv=xnv, x2=xn2: e.activation(
                        x2[:, dch * 512:(dch + 1) * 512], xnv[:, :], AF.Identity,
                        scale=gam_pp[:, dch:dch + 1], bias=beta_pp[:, dch:dch + 1]),
                       reads=[r_xnv, r_beta, r_gampp],
                       writes=[res(f"bb{blk % 2}_{dch}")] +
                              ([r_bb[1]] if (blk % 2 == 1 and dch == 0) else []))
                pieces.append(("latev", t2v))
                pieces.append(("lates", t2s))
            return pieces, rows_mm

        def a_main(blk, pq, rows_mm=None):
            bat = blk // NB2
            xn2 = bb[blk % 2]

            def drain(n=1, kinds=None):
                for _ in range(n):
                    if kinds is None:
                        if pq:
                            pq.pop(0)[1]()
                    else:
                        for i, (k, f) in enumerate(pq):
                            if k in kinds:
                                pq.pop(i)
                                f()
                                break

            q1b, r_q1b = bb[2], r_bb[2]
            for fo in range(DC):
                pqb, r_pqb = P[4 + fo % 2], r_P[4 + fo % 2]
                for dch in range(DC):
                    Tn(lambda e, fo=fo, dch=dch, pqb=pqb: e.matmul(
                        pqb[:, 0:512], wqkv_bf(dch)[:, fo * 128:(fo + 1) * 128],
                        xn2[:, dch * 512:(dch + 1) * 512],
                        start=(dch == 0), stop=(dch == DC - 1)),
                       reads=[r_wq[dch], res(f"bb{blk % 2}_{dch}")],
                       writes=[r_pqb] if dch == 0 else [],
                       wnowait=[r_pqb] if dch == DC - 1 else [])
                ms = fo % 2
                Sc(lambda e, pqb=pqb, ms=ms: e.activation(
                    m_t[0][:, ms * 512:(ms + 1) * 512], pqb[:, 0:512],
                    AF.Relu, scale=-1.0),
                   reads=[r_pqb], writes=[r_m0[ms]])
                Sc(lambda e, ms=ms: e.activation(
                    e_t[0][:, ms * 512:(ms + 1) * 512],
                    m_t[0][:, ms * 512:(ms + 1) * 512], AF.Exp, scale=-1.0),
                   reads=[r_m0[ms]], writes=[r_e0[ms]])
                V(lambda e, pqb=pqb, ms=ms, fo=fo: e.scalar_tensor_tensor(
                    q1b[:, fo * 512:(fo + 1) * 512], pqb[:, 0:512], 0.0,
                    e_t[0][:, ms * 512:(ms + 1) * 512], ALU.max, ALU.add),
                  reads=[r_pqb, r_e0[ms]], writes=[r_q1b])
                drain(1, kinds={"early"})
            DMA("gpsimd", lambda e, b=blk, q=q1b: e.dma_start(
                out=d_q1[b * 128:(b + 1) * 128, :], in_=q[:, :]),
                reads=[r_q1b], writes=[r_q1d], sem=f"st_q{blk % 2}")

            def kv_pair_mms(tc, hs, k1v, vv, r_k1v, r_vv):
                for h in hs:
                    pb, r_pb = P[h // 8], r_P[h // 8]
                    first = (tc == 0 and h % 8 == 0)
                    last = (tc == 3 and h % 8 == 7)
                    Tn(lambda e, h=h, pb=pb, fs=first, ls=last,
                       k1v=k1v, vv=vv: e.matmul(
                        pb[0:64, (h % 8) * 64:(h % 8 + 1) * 64],
                        k1v[:, h * 64:(h + 1) * 64],
                        vv[:, h * 65:h * 65 + 64],
                        start=fs, stop=ls),
                       reads=[r_k1v, r_vv],
                       writes=[r_pb] if first else [],
                       wnowait=[r_pb] if last else [])
                    Tn(lambda e, h=h, tc=tc, k1v=k1v, vv=vv: e.matmul(
                        P[2][0:64, h:h + 1],
                        k1v[:, h * 64:(h + 1) * 64],
                        vv[:, h * 65 + 64:h * 65 + 65],
                        start=(tc == 0 and h == 0), stop=(tc == 3 and h == 15)),
                       reads=[r_k1v, r_vv],
                       writes=[r_P[2]] if (tc == 0 and h == 0) else [],
                       wnowait=[r_P[2]] if (tc == 3 and h == 15) else [])

            pend = None
            for tc in range(4):
                ks = tc % 2
                k1v, r_k1v = k1_tm[ks], r_k1[ks]
                vv, r_vv = v65[ks], r_v65[ks]
                for qi, qtr in enumerate([0, 2, 1, 3]):
                    kb = [3, 6, 7][(tc * 4 + qi) % 3]
                    pk, r_pk = P[kb], r_P[kb]
                    for dch in range(DC):
                        Tn(lambda e, tc=tc, qtr=qtr, dch=dch, pk=pk: e.matmul(
                            pk[:, 0:512],
                            xn2[:, dch * 512 + tc * 128: dch * 512 + (tc + 1) * 128],
                            wqkv_bf(dch)[:, 1024 + qtr * 512:1024 + (qtr + 1) * 512],
                            start=(dch == 0), stop=(dch == DC - 1)),
                           reads=[res(f"bb{blk % 2}_{dch}"), r_wq[dch]],
                           writes=[r_pk] if dch == 0 else [],
                           wnowait=[r_pk] if dch == DC - 1 else [])
                    if pend is not None:
                        kv_pair_mms(*pend)
                        pend = None
                    if qtr < 2:
                        Sc(lambda e, pk=pk, qtr=qtr: e.activation(
                            m_t[1][:, qtr * 512:(qtr + 1) * 512], pk[:, 0:512],
                            AF.Relu, scale=-1.0),
                           reads=[r_pk], writes=[r_m1[qtr]])
                        Sc(lambda e, ks=ks, qtr=qtr: e.activation(
                            e_t[1][:, qtr * 512:(qtr + 1) * 512],
                            m_t[1][:, qtr * 512:(qtr + 1) * 512], AF.Exp, scale=-1.0),
                           reads=[r_m1[qtr]], writes=[r_e1[qtr]])
                        V(lambda e, pk=pk, ks=ks, qtr=qtr, k1v=k1v: e.scalar_tensor_tensor(
                            k1v[:, qtr * 512:(qtr + 1) * 512], pk[:, 0:512], 0.0,
                            e_t[1][:, qtr * 512:(qtr + 1) * 512], ALU.max, ALU.add),
                          reads=[r_pk, r_e1[qtr]], writes=[r_k1v])
                    else:
                        Sc(lambda e, pk=pk, vv=vv, qtr=qtr: e.activation(
                            vv[:, :].rearrange("p (h v) -> p h v", v=65)
                              [:, (qtr - 2) * 8:(qtr - 1) * 8, 0:64],
                            pk[:, 0:512].rearrange("p (h v) -> p h v", v=64),
                            AF.Copy),
                           reads=[r_pk], writes=[r_vv])
                    if qi == 1:
                        if tc == 0 and rows_mm is not None:
                            rows_mm()
                        pend = (tc, range(0, 8), k1v, vv, r_k1v, r_vv)
                        drain(2)
                    elif qi == 3:
                        if tc == 3:
                            kv_pair_mms(tc, range(8, 16), k1v, vv, r_k1v, r_vv)
                        else:
                            pend = (tc, range(8, 16), k1v, vv, r_k1v, r_vv)
                        drain(2)
                    else:
                        drain(1, kinds={"early", "latev"})
                drain(1)
            for bk in range(2):
                coff = bat * 1024 + bk * 512
                V(lambda e, bk=bk, coff=coff: e.tensor_tensor(
                    kvacc[0:64, coff:coff + 512], kvacc[0:64, coff:coff + 512],
                    P[bk][0:64, 0:512], ALU.add),
                  reads=[r_P[bk]], writes=[r_kvacc])
            V(lambda e, bat=bat: e.tensor_tensor(
                kvacc[0:64, 2048 + bat * 16:2048 + (bat + 1) * 16],
                kvacc[0:64, 2048 + bat * 16:2048 + (bat + 1) * 16],
                P[2][0:64, 0:16], ALU.add),
              reads=[r_P[2]], writes=[r_kvacc])

            den = max(NBLK - 1, 1)
            for wci in (range(blk * 8 // den, (blk + 1) * 8 // den)
                        if blk < den else []):
                DMA("sync", lambda e, rlo=wci * 512: e.dma_start(
                    out=bto[:, 0:4096].rearrange("p (c f) -> p c f", c=4),
                    in_=d_w1b[rlo:rlo + 512, :].rearrange("(p c) f -> p c f", c=4)),
                    writes=[r_bto], sem="ld_wm")
                V(lambda e: e.tensor_reduce(
                    tred[:, :], bto[:, 0:4096], AX.X, ALU.max, apply_absolute_value=True),
                  reads=[r_bto], writes=[r_tred])
                V(lambda e: e.tensor_tensor(
                    mx_w1[:, :], mx_w1[:, :], tred[:, :], ALU.max),
                  reads=[r_tred], writes=[r_mxw1])
            drain(99)

        bslot = [wbst[0], wbst[1], bto[:, 0:1024]]
        r_bslot = [r_wbst[0], r_wbst[1], r_bto]
        bsem = ["ld_wb0", "ld_wb1", "ld_wm"]

        def build_load(fo):
            ws = fo % 3
            op("scalar", lambda e, fo=fo, ws=ws: e.dma_start(
                out=bslot[ws][:, 0:1024],
                in_=d_w1b[fo * 128:(fo + 1) * 128, :]),
                writes=[r_bslot[ws]], sem=bsem[ws], dinc=16)

        def build_compute(fo):
            ws = fo % 3
            Sc(lambda e, ws=ws: e.activation(zbld[:, :], bslot[ws][:, :1024], AF.Copy,
                                             bias=MAGIC, scale=sc_col(SC_SWQINV)),
               reads=[r_bslot[ws], r_scal], writes=list(r_zbld))
            wr = [r_w1q[fo]] + (r_wq if fo == 0 else [])
            V(lambda e, fo=fo: e.tensor_scalar_add(
                arena[:, :].rearrange("p (c f) -> p c f", c=DC)[:, :, fo * 128:(fo + 1) * 128],
                zbld[:, :].rearrange("p (c f) -> p c f", c=DC), -MAGIC),
              reads=list(r_zbld), writes=wr)

        def boundary67():
            cross_part_max(r_mxxn, mx_xn, SC_T0)
            DMA("sync", lambda e: e.dma_start(out=oh_sb, in_=d_oh[:, :]),
                writes=[r_oh], sem="ld_cc")
            V(lambda e: e.tensor_scalar(oh_t[0:1, 0:8], oh_sb[0:1, 0:8],
                                        sc1(SC_T0), None, ALU.mult),
              reads=[r_oh, r_scal], writes=[r_oht])
            DMA("gpsimd", lambda e: e.dma_start(out=d_cc1ai[0:1, 0:8], in_=oh_t[0:1, 0:8]),
                reads=[r_oht], writes=[r_cc1ai], sem="st_cc")
            op("gpsimd", lambda e: e.collective_compute(
                "AllReduce", ALU.add, replica_groups=[list(range(n_cores))],
                ins=[d_cc1ai.ap()], outs=[d_cc1ao.ap()]),
               reads=[r_cc1ai], writes=[r_cc1ao], sem="cc1a", dinc=1)
            DMA("sync", lambda e: e.dma_start(
                out=btx[:, :], in_=d_xfm[0:128, :]),
                writes=[r_btx], sem="ld_x0")
            cross_part_max(r_mxw1, mx_w1, SC_SW1C)
            V(lambda e: e.tensor_scalar(sc1(SC_SW1V), sc1(SC_SW1C), MIN_RANGE,
                                        INV_QP, ALU.max, ALU.mult),
              writes=[r_scal])
            V(lambda e: e.reciprocal(sc1(SC_SW1I), sc1(SC_SW1V)), writes=[r_scal])
            bcast_scalar(SC_SW1I, SC_SWQINV)
            for fo in range(3):
                build_load(fo)

        pq_a, rm0 = a_prep_pieces(0)
        for k, f in pq_a[:7]:
            f()
        rm0()
        for k, f in pq_a[7:]:
            f()
        for blk in range(NBLK):
            if blk + 1 < NBLK:
                nxt, rm_next = a_prep_pieces(blk + 1)
            else:
                nxt, rm_next = [], None
            if blk == NBLK - 1:
                boundary67()
            a_main(blk, nxt, rm_next)

        DMA("gpsimd", lambda e: e.dma_start(out=gmx8[0:1, 0:8], in_=d_cc1ao[0:1, 0:8]),
            reads=[r_cc1ao], writes=[r_gmx8], sem="ld_cc2")

        DMA("gpsimd", lambda e: e.dma_start(out=d_cc1i[:, 0:2080], in_=kvacc[:, :]),
            reads=[r_kvacc], writes=[r_cc1i], sem="st_cc")
        op("gpsimd", lambda e: e.collective_compute(
            "AllReduce", ALU.add, replica_groups=[list(range(n_cores))],
            ins=[d_cc1i.ap()], outs=[d_cc1o.ap()]),
           reads=[r_cc1i], writes=[r_cc1o], sem="cc1", dinc=1)

        build_compute(0)
        build_load(3)
        build_compute(1)
        build_load(4)

        def b1_prep_pieces(blk):
            pieces = []

            def loads():
                DMA("sync", lambda e, b=blk: e.dma_start(
                    out=btx[:, :], in_=d_xfm[b * 128:(b + 1) * 128, :]),
                    writes=[r_btx], sem="ld_x0")
            pieces.append(loads)

            def rows_bcast():
                bb0 = blk * 24

                def spv(c):
                    return statp[:, bb0:bb0 + 24].rearrange(
                        "p (s c) -> p c s", c=6)[:, c:c + 1, :]
                V(lambda e: e.tensor_scalar(spv(2), spv(1), sc_col(SC_SXINV), None,
                                            ALU.mult),
                  reads=[r_scal], writes=[r_statp])
                V(lambda e: e.tensor_tensor(spv(3), spv(0), spv(2), ALU.mult),
                  writes=[r_statp])
                stats_rows2(blk, 2, P[6], r_P[6])
                Tn(lambda e: e.matmul(P[0][:, 0:512], sel2[:, 0:128],
                                      rows2[0:2, 0:512], start=True, stop=True),
                   reads=[r_sel2, r_rows2], writes=[r_P[0]])
                Tn(lambda e: e.matmul(P[1][:, 0:512], sel2[:, 128:256],
                                      rows2[0:2, 0:512], start=True, stop=True),
                   reads=[r_sel2, r_rows2], writes=[r_P[1]])
            pieces.append(rows_bcast)

            for dch in range(DC):
                def xq_piece(dch=dch):
                    xs = dch % 2
                    zqv, r_zqv = zq_t[xs], r_zq[xs]
                    V(lambda e, dch=dch, zqv=zqv: e.tensor_tensor(
                        zqv[:, :], btx[:, dch * 512:(dch + 1) * 512],
                        P[0][:, 0:512], ALU.mult),
                      reads=[r_btx, r_P[0]], writes=[r_zqv])
                    V(lambda e, zqv=zqv: e.tensor_tensor(
                        zqv[:, :], zqv[:, :], P[1][:, 0:512], ALU.subtract),
                      reads=[r_P[1]], writes=[r_zqv])
                    V(lambda e, dch=dch, zqv=zqv, b=blk: e.tensor_scalar(
                        bb[b % 2][:, dch * 512:(dch + 1) * 512],
                        zqv[:, :], MAGIC, MAGIC, ALU.add, ALU.subtract),
                      reads=[r_zqv], writes=[res(f"bb{blk % 2}_{dch}")])
                pieces.append(xq_piece)
            return pieces

        pq_b1 = b1_prep_pieces(0)[1:]

        V(lambda e: e.tensor_reduce(sc1(SC_T0), gmx8[0:1, 0:8], AX.X, ALU.max),
          reads=[r_gmx8], writes=[r_scal])
        V(lambda e: e.tensor_scalar(sc1(SC_GXC), sc1(SC_T0), MIN_RANGE, INV_QP,
                                    ALU.max, ALU.mult),
          writes=[r_scal])
        V(lambda e: e.reciprocal(sc1(SC_SXV), sc1(SC_GXC)), writes=[r_scal])
        V(lambda e: e.tensor_tensor(sc1(SC_T1), sc1(SC_SW1V), sc1(SC_GXC), ALU.mult),
          writes=[r_scal])
        bcast_scalar(SC_SXV, SC_SXINV)
        bcast_scalar(SC_T1, SC_SS1)
        V(lambda e: e.tensor_scalar(b1ss[:, :], b1_pp[:, :], sc_col(SC_SS1), None, ALU.mult),
          reads=[r_b1, r_scal], writes=[r_b1ss])

        while pq_b1:
            pq_b1.pop(0)()

        def kvblk_dma_piece(b, c):
            def f():
                off = b * 1024 + 2 * c * 64
                soff = 2048 + b * 16 + 2 * c
                V(lambda e: e.memset(kvstage[:, :], 0.0), writes=[r_kvstage])
                V(lambda e: e.memset(denstage[:, :], 0.0), writes=[r_denstage])
                op("sync", lambda e, off=off: e.dma_start(
                    out=kvstage[0:64, 0:64], in_=d_cc1o[0:64, off:off + 64]),
                   reads=[r_cc1o], writes=[r_kvstage], sem="ld_kvb", dinc=16)
                op("sync", lambda e, off=off: e.dma_start(
                    out=kvstage[64:128, 64:128], in_=d_cc1o[0:64, off + 64:off + 128]),
                   reads=[r_cc1o], writes=[r_kvstage], sem="ld_kvb", dinc=16)
                op("sync", lambda e, soff=soff, c=c: e.dma_start(
                    out=denstage[0:64, 2 * c:2 * c + 1], in_=d_cc1o[0:64, soff:soff + 1],
                    allow_slow_non_contiguous=True),
                   reads=[r_cc1o], writes=[r_denstage], sem="ld_kvb", dinc=16)
                op("sync", lambda e, soff=soff, c=c: e.dma_start(
                    out=denstage[64:128, 2 * c + 1:2 * c + 2],
                    in_=d_cc1o[0:64, soff + 1:soff + 2],
                    allow_slow_non_contiguous=True),
                   reads=[r_cc1o], writes=[r_denstage], sem="ld_kvb", dinc=16)
                tot = ("ld_kvb", ctx.sems["ld_kvb"].count)
                r_kvstage.w, r_kvstage.rs = tot, []
                r_denstage.w, r_denstage.rs = tot, []
            return f

        def kvblk_cast_piece(b, c):
            def f():
                t = b * 8 + c
                V(lambda e, t=t: e.tensor_copy(kvblk[:, t * 128:(t + 1) * 128], kvstage[:, :]),
                  reads=[r_kvstage], writes=[r_kvblk])
                V(lambda e, t=t: e.tensor_copy(denblk[:, t * 16:(t + 1) * 16], denstage[:, :]),
                  reads=[r_denstage], writes=[r_denblk])
            return f

        def wout_piece(d):
            def f():
                DMA("sync", lambda e, d=d: e.dma_start(
                    out=wbst[d % 2][:, :], in_=d_wout[d * 128:(d + 1) * 128, :]),
                    writes=[r_wbst[d % 2]], sem=f"ld_wb{d % 2}")
                V(lambda e, d=d: e.tensor_copy(woutbf[:, d * 1024:(d + 1) * 1024],
                                               wbst[d % 2][:, :]),
                  reads=[r_wbst[d % 2]], writes=[r_woutbf])
            return f

        def w2max_piece(wci):
            def f():
                rlo = wci * 512
                DMA("sync", lambda e, rlo=rlo: e.dma_start(
                    out=bto[:, 0:4096].rearrange("p (c f) -> p c f", c=4),
                    in_=d_w2t[rlo:rlo + 512, :].rearrange("(c p) f -> p c f", p=128)),
                    writes=[r_bto], sem="ld_wm")
                V(lambda e: e.tensor_reduce(
                    tred[:, :], bto[:, 0:4096], AX.X, ALU.max, apply_absolute_value=True),
                  reads=[r_bto], writes=[r_tred])
                V(lambda e: e.tensor_tensor(
                    mx_w2[:, :], mx_w2[:, :], tred[:, :], ALU.max),
                  reads=[r_tred], writes=[r_mxw2])
            return f

        extra_b1 = {}
        kv_bc = [(b, c) for b in range(2) for c in range(8)]
        all_extras = []
        for i, (b, c) in enumerate(kv_bc):
            all_extras.append(kvblk_dma_piece(b, c))
            all_extras.append(kvblk_cast_piece(b, c))
        all_extras += [wout_piece(d) for d in range(DC)]
        all_extras += [w2max_piece(i) for i in range(8)]
        for i, p in enumerate(all_extras):
            key = min(2 + i * (NBLK - 2) // len(all_extras), NBLK - 1)
            extra_b1.setdefault(key, []).append(p)

        for blk in range(NBLK):
            pq = (b1_prep_pieces(blk + 1) if blk + 1 < NBLK else [])
            pq = pq + extra_b1.get(blk, [])

            def drain(n=1):
                for _ in range(n):
                    if pq:
                        pq.pop(0)()

            xqb = bb[blk % 2]
            for fo in range(HC):
                pf, r_pf = P[2 + fo % 4], r_P[2 + fo % 4]
                for dch in range(DC):
                    Tn(lambda e, fo=fo, dch=dch, pf=pf: e.matmul(
                        pf[:, 0:512], w1qv(dch)[:, fo * 128:(fo + 1) * 128],
                        xqb[:, dch * 512:(dch + 1) * 512],
                        start=(dch == 0), stop=(dch == DC - 1)),
                       reads=[r_w1q[fo], res(f"bb{blk % 2}_{dch}")],
                       writes=[r_pf] if dch == 0 else [],
                       wnowait=[r_pf] if dch == DC - 1 else [])
                qrt = (fo // 2) % 4
                hsl = bts[:, qrt * 1024 + (fo % 2) * 512: qrt * 1024 + (fo % 2 + 1) * 512]
                hres = r_btsh[qrt]
                extra = [r_bts] if (blk == 0 and fo < 8) else []
                Sc(lambda e, pf=pf, hsl=hsl, fo=fo: e.activation(
                    hsl, pf[:, 0:512], AF.Gelu,
                    bias=b1ss[:, fo:fo + 1], scale=sc_col(SC_SS1)),
                   reads=[r_pf, r_b1ss, r_scal], writes=[hres] + extra)
                if fo % 2 == 1:
                    V(lambda e, qrt=qrt: e.tensor_reduce(
                        tred[:, :], bts[:, qrt * 1024:(qrt + 1) * 1024],
                        AX.X, ALU.max, apply_absolute_value=True),
                      reads=[hres], writes=[r_tred])
                    V(lambda e: e.tensor_tensor(mx_h[:, :], mx_h[:, :], tred[:, :], ALU.max),
                      reads=[r_tred], writes=[r_mxh])
                    tile = blk * 16 + (fo - 1) // 2
                    seng = "gpsimd" if qrt % 2 == 0 else "scalar"
                    op(seng, lambda e, tile=tile, qrt=qrt: e.dma_start(
                        out=d_h[tile * 128:(tile + 1) * 128, :],
                        in_=bts[:, qrt * 1024:(qrt + 1) * 1024]),
                       reads=[hres], writes=[r_hd[qrt % 2]], sem=f"st_h{qrt % 2}", dinc=16)
                    drain(1)
                if blk == 0:
                    if fo + 2 < HC:
                        build_compute(fo + 2)
                    if fo + 5 < HC:
                        build_load(fo + 5)
            drain(99)

        cross_part_max(r_mxh, mx_h, SC_T0)
        DMA("sync", lambda e: e.dma_start(out=oh_sb, in_=d_oh[:, :]),
            writes=[r_oh], sem="ld_cc")
        V(lambda e: e.tensor_scalar(oh_t[0:1, 0:8], oh_sb[0:1, 0:8],
                                    sc1(SC_T0), None, ALU.mult),
          reads=[r_oh, r_scal], writes=[r_oht])
        DMA("gpsimd", lambda e: e.dma_start(out=d_cc2i[0:1, 0:8], in_=oh_t[0:1, 0:8]),
            reads=[r_oht], writes=[r_cc2i], sem="st_cc")
        op("gpsimd", lambda e: e.collective_compute(
            "AllReduce", ALU.add, replica_groups=[list(range(n_cores))],
            ins=[d_cc2i.ap()], outs=[d_cc2o.ap()]),
           reads=[r_cc2i], writes=[r_cc2o], sem="cc2", dinc=1)

        cross_part_max(r_mxw2, mx_w2, SC_SW2C)
        V(lambda e: e.tensor_scalar(sc1(SC_SW2V), sc1(SC_SW2C), MIN_RANGE, INV_QP,
                                    ALU.max, ALU.mult),
          writes=[r_scal])
        V(lambda e: e.reciprocal(sc1(SC_T0), sc1(SC_SW2V)), writes=[r_scal])
        bcast_scalar(SC_T0, SC_SWQINV)

        def w2_load(f):
            ws = f % 2
            DMA("sync", lambda e, f=f, ws=ws: e.dma_start(
                out=wbst[ws][:, :], in_=d_w2t[f * 128:(f + 1) * 128, :]),
                writes=[r_wbst[ws]], sem=f"ld_wb{ws}")

        def w2_compute(f):
            ws = f % 2
            Sc(lambda e, ws=ws: e.activation(zbld[:, :], wbst[ws][:, :], AF.Copy,
                                             bias=MAGIC, scale=sc_col(SC_SWQINV)),
               reads=[r_wbst[ws], r_scal], writes=list(r_zbld))
            wr = [r_w2q[f]] + (r_w1q if f == 0 else [])
            V(lambda e, f=f: e.tensor_scalar_add(w2qv(f)[:, :], zbld[:, :], -MAGIC),
              reads=list(r_zbld), writes=wr)

        def c_quant(blk, f):
            hs = f % 4
            zs = f % 2
            extra = ([r_bts] + r_btsh if (blk == 0 and f < 4) else [])
            tile = blk * 16 + f // 2
            leng = "sync" if hs % 2 == 0 else "gpsimd"
            op(leng, lambda e, tile=tile, f=f, hs=hs: e.dma_start(
                out=hrel_v[hs],
                in_=d_h[tile * 128:(tile + 1) * 128,
                        (f % 2) * 512:(f % 2 + 1) * 512]),
               reads=[r_hd[(f // 2) % 2]], writes=[r_hrel[hs]] + extra,
               sem=f"ld_h{hs}", dinc=16)
            Sc(lambda e, hs=hs, zs=zs: e.activation(
                zq_t[zs][:, :], hrel_v[hs], AF.Copy,
                bias=MAGIC, scale=sc_col(SC_SHINV)),
               reads=[r_hrel[hs], r_scal], writes=[r_zq[zs]])
            V(lambda e, hs=hs, zs=zs: e.tensor_scalar_add(
                hq_t[hs][:, :], zq_t[zs][:, :], -MAGIC),
              reads=[r_zq[zs]], writes=[r_hq[hs]])

        def c_loads(blk):
            DMA("scalar", lambda e, b=blk: e.dma_start(
                out=btx[:, :], in_=d_xfm[b * 128:(b + 1) * 128, :]),
                writes=[r_btx], sem="ld_x0")
            DMA("scalar", lambda e, b=blk: e.dma_start(
                out=bb[1][:, :], in_=d_q1[b * 128:(b + 1) * 128, :]),
                reads=[r_q1d],
                writes=[r_bb[1]] + [res(f"bb1_{i}") for i in range(DC)], sem="ld_q1")

        def c_attn(blk):
            bat = blk // NB2
            for c in range(8):
                t = bat * 8 + c
                Tn(lambda e, t=t, c=c: e.matmul(
                    P[0][0:16, 0:512], denblk[:, t * 16:(t + 1) * 16],
                    bb[1][:, c * 512:(c + 1) * 512],
                    start=(c == 0), stop=(c == 7)),
                   reads=[r_denblk, r_bb[1]],
                   writes=[r_P[0]] if c == 0 else [],
                   wnowait=[r_P[0]] if c == 7 else [])
            Sc(lambda e: e.activation(den_sb[0:16, 0:512], P[0][0:16, 0:512],
                                      AF.Ln, bias=scal[0:16, SC_DEPS:SC_DEPS + 1]),
               reads=[r_P[0], r_scal], writes=[r_densb])
            Sc(lambda e: e.activation(rec_bf[0:16, 0:512], den_sb[0:16, 0:512],
                                      AF.Exp, scale=-1.0),
               reads=[r_densb], writes=[r_recbf])
            def qkv_mm(c):
                t = bat * 8 + c
                pqb, r_pqb = P[1 + c % 4], r_P[1 + c % 4]
                Tn(lambda e, t=t, c=c, pqb=pqb: e.matmul(
                    pqb[:, 0:512], kvblk[:, t * 128:(t + 1) * 128],
                    bb[1][:, c * 512:(c + 1) * 512], start=True, stop=True),
                   reads=[r_kvblk, r_bb[1]], writes=[r_pqb])
            for c in range(4):
                qkv_mm(c)
            rslot = [recb_v[0], recb_v[1], xn_t[0][:, :], xn_t[1][:, :]]
            r_rslot = [r_recb[0], r_recb[1], r_xn[0], r_xn[1]]
            rbank = [5, 6, 7, 0]

            def recb_bcast(c):
                pr, r_pr = P[rbank[c % 4]], r_P[rbank[c % 4]]
                Tn(lambda e, c=c, pr=pr: e.matmul(
                    pr[:, 0:512], maskc[:, c * 128:(c + 1) * 128],
                    rec_bf[0:16, 0:512], start=True, stop=True),
                   reads=[r_maskc, r_recbf], writes=[r_pr])
            for c in range(4):
                recb_bcast(c)
            for c in range(8):
                pqb, r_pqb = P[1 + c % 4], r_P[1 + c % 4]
                pr, r_pr = P[rbank[c % 4]], r_P[rbank[c % 4]]
                cs = c % 4
                Sc(lambda e, pr=pr, cs=cs: e.activation(
                    rslot[cs], pr[:, 0:512], AF.Copy),
                   reads=[r_pr],
                   writes=[r_rslot[cs]])
                V(lambda e, pqb=pqb, cs=cs, c=c: e.tensor_tensor(
                    bb[2][:, c * 512:(c + 1) * 512], pqb[:, 0:512],
                    rslot[cs], ALU.mult),
                  reads=[r_pqb, r_rslot[cs]], writes=[r_bb[2]])
                if c + 4 < 8:
                    qkv_mm(c + 4)
                    recb_bcast(c + 4)
            if blk > 0:
                for f in range(4):
                    c_quant(blk, f)

            def wout_group(do):
                pw, r_pw = P[5 + do % 2], r_P[5 + do % 2]
                for di in range(DC):
                    Tn(lambda e, do=do, di=di, pw=pw: e.matmul(
                        pw[:, 0:512],
                        woutbf[:, di * 1024 + do * 128: di * 1024 + (do + 1) * 128],
                        bb[2][:, di * 512:(di + 1) * 512],
                        start=(di == 0), stop=(di == DC - 1)),
                       reads=[r_woutbf, r_bb[2]],
                       writes=[r_pw] if di == 0 else [],
                       wnowait=[r_pw] if di == DC - 1 else [])

            stats_rows2(blk, 4, P[0], r_P[0])
            wout_group(0)
            wout_group(1)
            Tn(lambda e: e.matmul(P[1][:, 0:512], sel2[:, 128:256], rows2[0:2, 0:512],
                                  start=True, stop=True),
               reads=[r_sel2, r_rows2], writes=[r_P[1]])
            Tn(lambda e: e.matmul(P[2][:, 0:512], sel2[:, 0:128], rows2[0:2, 0:512],
                                  start=True, stop=True),
               reads=[r_sel2, r_rows2], writes=[r_P[2]])
            for do in range(DC):
                V(lambda e, do=do: e.tensor_tensor(
                    bto[:, do * 512:(do + 1) * 512],
                    btx[:, do * 512:(do + 1) * 512], P[1][:, 0:512], ALU.mult),
                  reads=[r_btx, r_P[1]], writes=[r_bto])
                V(lambda e, do=do: e.tensor_tensor(
                    bto[:, do * 512:(do + 1) * 512],
                    bto[:, do * 512:(do + 1) * 512], P[2][:, 0:512], ALU.subtract),
                  reads=[r_P[2]], writes=[r_bto])
                pw, r_pw = P[5 + do % 2], r_P[5 + do % 2]
                V(lambda e, do=do, pw=pw: e.tensor_tensor(
                    bto[:, do * 512:(do + 1) * 512], pw[:, 0:512],
                    bto[:, do * 512:(do + 1) * 512], ALU.add),
                  reads=[r_pw], writes=[r_bto])
                if do + 2 < DC:
                    wout_group(do + 2)

        def c_ffn2(blk):
            if blk == 0:
                w2_load(0)
                w2_load(1)
            for f in range(HC):
                if blk == 0:
                    w2_compute(f)
                    if f + 2 < HC:
                        w2_load(f + 2)
                hs = f % 4
                if blk == 0 or f >= 4:
                    c_quant(blk, f)
                for do in range(DC):
                    Tn(lambda e, f=f, do=do, hs=hs: e.matmul(
                        P[do][:, 0:512], w2qv(f)[:, do * 128:(do + 1) * 128],
                        hq_t[hs][:, :], start=(f == 0), stop=(f == HC - 1)),
                       reads=[r_w2q[f], r_hq[hs]],
                       writes=[r_P[do]] if f == 0 else [],
                       wnowait=[r_P[do]] if f == HC - 1 else [])
                if blk + 1 < NBLK and f == 24:
                    c_loads(blk + 1)
            for do in range(DC):
                ds = do % 2
                Sc(lambda e, do=do, ds=ds: e.activation(
                    et_v[ds], P[do][:, 0:512], AF.Identity,
                    bias=b2ss[:, do:do + 1], scale=sc_col(SC_SS2)),
                   reads=[r_P[do], r_b2ss, r_scal],
                   writes=[r_et[ds]])
                V(lambda e, do=do, ds=ds: e.tensor_tensor(
                    bto[:, do * 512:(do + 1) * 512], et_v[ds],
                    bto[:, do * 512:(do + 1) * 512], ALU.add),
                  reads=[r_et[ds]], writes=[r_bto])
            DMA("scalar", lambda e, b=blk: e.dma_start(
                out=d_out[b * 128:(b + 1) * 128, :], in_=bto[:, :]),
                reads=[r_bto], writes=[r_out], sem="st_out")

        c_loads(0)
        c_attn(0)

        DMA("sync", lambda e: e.dma_start(out=gmx8[0:1, 0:8], in_=d_cc2o[0:1, 0:8]),
            reads=[r_cc2o], writes=[r_gmx8], sem="ld_cc")
        V(lambda e: e.tensor_reduce(sc1(SC_GHC), gmx8[0:1, 0:8], AX.X, ALU.max),
          reads=[r_gmx8], writes=[r_scal])
        V(lambda e: e.tensor_scalar(sc1(SC_SHV), sc1(SC_GHC), MIN_RANGE, INV_QP,
                                    ALU.max, ALU.mult),
          writes=[r_scal])
        V(lambda e: e.reciprocal(sc1(SC_T0), sc1(SC_SHV)), writes=[r_scal])
        V(lambda e: e.tensor_tensor(sc1(SC_T1), sc1(SC_SW2V), sc1(SC_SHV), ALU.mult),
          writes=[r_scal])
        bcast_scalar(SC_T0, SC_SHINV)
        bcast_scalar(SC_T1, SC_SS2)
        V(lambda e: e.tensor_scalar(b2ss[:, :], b2_pp[:, :], sc_col(SC_SS2), None, ALU.mult),
          reads=[r_b2, r_scal], writes=[r_b2ss])

        c_ffn2(0)
        for blk in range(1, NBLK):
            c_attn(blk)
            c_ffn2(blk)

        ctx.raw_wait("gpsimd", "st_out", 16 * NBLK)

    with nc.Block() as block:

        @block.sync
        def _(sync):
            schedule(_Ctx("sync", sync, fresh_sems()))

        @block.gpsimd
        def _(gpsimd):
            schedule(_Ctx("gpsimd", gpsimd, fresh_sems()))

        @block.vector
        def _(vector):
            schedule(_Ctx("vector", vector, fresh_sems()))

        @block.scalar
        def _(scalar):
            schedule(_Ctx("scalar", scalar, fresh_sems()))

        @block.tensor
        def _(tensor):
            schedule(_Ctx("tensor", tensor, fresh_sems()))

    es.close()
    return nc


_NC_CACHE = {}


def _get_program(T_core, n_cores):
    key = (T_core, n_cores)
    if key not in _NC_CACHE:
        _NC_CACHE[key] = build_program(T_core, n_cores)
    return _NC_CACHE[key]


def kernel(L0_attn_x, L0_attn_gamma, L0_attn_beta, L0_attn_Wqkv, L0_attn_Wout,
           L0_ffn_w1, L0_ffn_b1, L0_ffn_w2, L0_ffn_b2):
    x = np.asarray(L0_attn_x, dtype=np.float32)
    B, L, D_ = x.shape
    n = 8
    Lc = L // n
    T_core = B * Lc

    nc = _get_program(T_core, n)

    wqkv = np.ascontiguousarray(np.asarray(L0_attn_Wqkv, dtype=np.float32))
    wout = np.ascontiguousarray(np.asarray(L0_attn_Wout, dtype=np.float32))
    w1 = np.asarray(L0_ffn_w1, dtype=np.float32)
    w1b = np.ascontiguousarray(
        w1.reshape(HC, 128, DC, 128).transpose(0, 3, 2, 1).reshape(HF, D))
    w2t = np.ascontiguousarray(np.asarray(L0_ffn_w2, dtype=np.float32).T)
    gam = np.ascontiguousarray(np.asarray(L0_attn_gamma, dtype=np.float32).reshape(DC, 128).T)
    beta_pp = np.ascontiguousarray(np.asarray(L0_attn_beta, dtype=np.float32).reshape(DC, 128).T)
    b1_pp = np.ascontiguousarray(np.asarray(L0_ffn_b1, dtype=np.float32).reshape(HC, 128).T)
    b2_pp = np.ascontiguousarray(np.asarray(L0_ffn_b2, dtype=np.float32).reshape(DC, 128).T)
    ident = np.eye(128, dtype=np.float32)
    sel2c = np.zeros((2, 256), np.float32)
    sel2c[0, 0:128] = 1.0
    sel2c[1, 128:256] = 1.0
    maskc_f = np.zeros((16, 1024), np.float32)
    for cc in range(8):
        maskc_f[2 * cc, cc * 128:cc * 128 + 64] = 1.0
        maskc_f[2 * cc + 1, cc * 128 + 64:(cc + 1) * 128] = 1.0

    in_maps = []
    nbt = T_core // 512
    for c in range(n):
        xs = x[:, c * Lc:(c + 1) * Lc, :]
        x_tm0 = xs.reshape(T_core, D_)
        x_fm = np.ascontiguousarray(
            x_tm0.reshape(nbt, 512, 8, 128).transpose(0, 3, 2, 1).reshape(nbt * 128, 4096))
        x_tm = np.ascontiguousarray(
            x_tm0.reshape(nbt, 4, 128, 1024).transpose(0, 2, 1, 3).reshape(nbt * 128, 4096))
        oh = np.zeros((1, 8), np.float32)
        oh[0, c] = 1.0
        in_maps.append(dict(x_fm=x_fm, x_tm=x_tm, wqkv=wqkv, wout=wout,
                            w1b=w1b, w2t=w2t, gam_pp=gam, beta_pp=beta_pp,
                            b1_pp=b1_pp, b2_pp=b2_pp, ident=ident, onehot=oh,
                            sel2c=sel2c, maskc_f=maskc_f))

    res = run_bass_kernel_spmd(nc, in_maps, core_ids=list(range(n)))
    parts = []
    for r in res.results:
        arr = np.asarray(r["out_fm"])
        out_core = arr.reshape(nbt, 128, 8, 512).transpose(0, 3, 2, 1).reshape(T_core, D_)
        parts.append(out_core.reshape(B, Lc, D_))
    return np.ascontiguousarray(np.concatenate(parts, axis=1)).astype(np.float32)


# version 33 (final)
# speedup vs baseline: 1.0272x; 1.0266x over previous
import contextlib
import numpy as np

import concourse.bass as bass
import concourse.mybir as mybir
from concourse.bass_utils import run_bass_kernel_spmd

F32 = mybir.dt.float32
BF16 = mybir.dt.bfloat16
AF = mybir.ActivationFunctionType
ALU = mybir.AluOpType
AX = mybir.AxisListType

MAGIC = 12582912.0
LN_EPS = 1e-5
DEN_EPS = 1e-6
MIN_RANGE = 1e-6
INV_QP = float(np.float32(1.0) / np.float32(127.0))

D = 1024
DC = 8
HF = 4096
HC = 32
TB = 512


class _Sem:
    __slots__ = ("name", "handle", "count")

    def __init__(self, name, handle):
        self.name, self.handle, self.count = name, handle, 0


class _Res:
    __slots__ = ("name", "w", "rs")

    def __init__(self, name):
        self.name, self.w, self.rs = name, None, []


class _Ctx:

    def __init__(self, active, engobj, sems):
        self.active = active
        self.engobj = engobj
        self.sems = sems
        self.marks = {}

    def _wait(self, engname, tok):
        if tok is None:
            return
        semname, val = tok
        m = self.marks.setdefault(engname, {})
        if m.get(semname, 0) >= val:
            return
        m[semname] = val
        if self.active == engname:
            self.engobj.wait_ge(self.sems[semname].handle, val)

    def op(self, engname, emit, reads=(), writes=(), wnowait=(), sem=None, dinc=1,
           track=True):
        for r in reads:
            self._wait(engname, r.w)
        for w in writes:
            self._wait(engname, w.w)
            for t in w.rs:
                self._wait(engname, t)
        s = self.sems[sem if sem is not None else ("s_" + engname)]
        inst = emit(self.engobj) if self.active == engname else None
        s.count += dinc
        if inst is not None:
            inst.then_inc(s.handle, dinc)
        if not track:
            return None
        tok = (s.name, s.count)
        for r in reads:
            r.rs.append(tok)
        for w in writes:
            w.w, w.rs = tok, []
        for w in wnowait:
            w.w, w.rs = tok, []
        return tok

    def raw_wait(self, engname, semname, val):
        self._wait(engname, (semname, val))


def build_program(T_core, n_cores):
    NBLK = T_core // TB
    NB2 = NBLK // 2
    assert NBLK % 2 == 0 and NB2 >= 1

    nc = bass.Bass("TRN2", target_bir_lowering=False)

    NBT = T_core // 512
    d_xfm = nc.dram_tensor("x_fm", [NBT * 128, 4096], F32, kind="ExternalInput")
    d_xtm = nc.dram_tensor("x_tm", [NBT * 128, 4096], F32, kind="ExternalInput")
    d_wqkv = nc.dram_tensor("wqkv", [D, 3 * D], F32, kind="ExternalInput")
    d_wout = nc.dram_tensor("wout", [D, D], F32, kind="ExternalInput")
    d_w1b = nc.dram_tensor("w1b", [HF, D], F32, kind="ExternalInput")
    d_w2t = nc.dram_tensor("w2t", [HF, D], F32, kind="ExternalInput")
    d_gam = nc.dram_tensor("gam_pp", [128, DC], F32, kind="ExternalInput")
    d_beta = nc.dram_tensor("beta_pp", [128, DC], F32, kind="ExternalInput")
    d_b1 = nc.dram_tensor("b1_pp", [128, HC], F32, kind="ExternalInput")
    d_b2 = nc.dram_tensor("b2_pp", [128, DC], F32, kind="ExternalInput")
    d_ident = nc.dram_tensor("ident", [128, 128], F32, kind="ExternalInput")
    d_oh = nc.dram_tensor("onehot", [1, 8], F32, kind="ExternalInput")
    d_sel2 = nc.dram_tensor("sel2c", [2, 256], F32, kind="ExternalInput")
    d_mask = nc.dram_tensor("maskc_f", [16, 1024], F32, kind="ExternalInput")
    d_out = nc.dram_tensor("out_fm", [NBT * 128, 4096], F32, kind="ExternalOutput")

    d_q1 = nc.dram_tensor("q1_spill", [NBT * 128, 4096], BF16)
    d_h = nc.dram_tensor("h_spill", [(T_core // 512) * 16 * 128, 1024], F32)
    d_cc1i = nc.dram_tensor("cc1_in", [64, 2080], F32)
    d_cc1o = nc.dram_tensor("cc1_out", [64, 2080], F32)
    d_cc1ai = nc.dram_tensor("cc1a_in", [1, 8], F32)
    d_cc1ao = nc.dram_tensor("cc1a_out", [1, 8], F32)
    d_cc2i = nc.dram_tensor("cc2_in", [1, 8], F32)
    d_cc2o = nc.dram_tensor("cc2_out", [1, 8], F32)

    A = nc.alloc_sbuf_tensor
    btx = A("btx", [128, 4096], F32)
    bts = A("bts", [128, 4096], F32)
    bto = A("bto", [128, 4096], F32)
    bb = [A(f"bb{i}", [128, 4096], BF16) for i in range(3)]
    arena = A("arena_w", [128, 32768], BF16)
    woutbf = A("wout_bf", [128, 8192], BF16)
    wbst = [A(f"wbst{i}", [128, 1024], F32) for i in range(2)]
    xn_t = [A(f"xn{i}", [128, 512], F32) for i in range(2)]
    m_t = [A("m0", [128, 1024], F32)] * 2
    e_t = [A("e0", [128, 1024], F32)] * 2
    hq_t = [A(f"hq{i}", [128, 512], BF16) for i in range(4)]
    k1_tm = [A(f"k1tm{i}", [128, 1024], BF16) for i in range(2)]
    v65 = [A(f"v65_{i}", [128, 1040], BF16) for i in range(2)]
    kvacc = A("kvacc", [64, 2080], F32)
    kvblk = A("kvblk", [128, 2048], BF16)
    denblk = A("denblk", [128, 256], BF16)
    maskc = A("maskc", [16, 1024], BF16)
    sel2 = A("sel2", [2, 256], F32)
    gam_pp = A("gam_pp_sb", [128, DC], F32)
    ones_row = None
    beta_pp = A("beta_pp_sb", [128, DC], F32)
    b1_pp = A("b1_pp_sb", [128, HC], F32)
    b2_pp = A("b2_pp_sb", [128, DC], F32)
    b1ss = A("b1ss", [128, HC], F32)
    b2ss = A("b2ss", [128, DC], F32)
    ident_sb = A("ident_sb", [128, 128], F32)
    statp = A("statp", [128, NBLK * 4 * 6], F32)
    bnst = A("bnst", [128, 12], F32)
    agg8 = A("agg8", [128, 8], F32)
    tmp4 = A("tmp4", [128, 8], F32)
    rows2 = A("rows2", [2, 512], F32)
    den_sb = A("den_sb", [16, 512], F32)
    rec_bf = A("rec_bf", [16, 512], BF16)
    kvstage = A("kvstage", [128, 128], F32)
    denstage = A("denstage", [128, 16], F32)
    mx_xn = A("mx_xn", [128, 1], F32)
    mx_h = A("mx_h", [128, 1], F32)
    mx_w1 = A("mx_w1", [128, 1], F32)
    mx_w2 = A("mx_w2", [128, 1], F32)
    tred = A("tred", [128, 1], F32)
    scal = A("scal", [128, 24], F32)
    oh_t = A("oh_t", [1, 8], F32)

    SC_SXINV, SC_SWQINV, SC_SS1, SC_SHINV, SC_SS2 = 0, 1, 2, 3, 4
    SC_GXC, SC_SXV = 8, 9
    SC_SW1C, SC_SW1V, SC_SW1I = 10, 11, 12
    SC_GHC, SC_SHV = 13, 14
    SC_SW2C, SC_SW2V = 15, 16
    SC_T0, SC_T1, SC_DEPS, SC_EPS = 17, 18, 22, 23

    zbld = m_t[0]
    gam_row = None
    r2row = den_sb
    r2rowbf = rec_bf

    zq_t = [e_t[0][:, 0:512], e_t[0][:, 512:1024]]
    recb_v = [m_t[1][:, 0:512], m_t[1][:, 512:1024]]
    et_v = [e_t[1][:, 0:512], e_t[1][:, 512:1024]]

    gmx8 = oh_t
    ones_row = sel2
    oh_sb = bnst[0:1, 0:8]

    P = [nc.alloc_psum_tensor(f"P{i}", [128, 512], F32) for i in range(8)]

    hrel_v = [bts[:, i * 512:(i + 1) * 512] for i in range(4)]

    es = contextlib.ExitStack()
    semnames = (["s_sync", "s_gpsimd", "s_vector", "s_scalar", "s_tensor",
                 "cc1", "cc1a", "cc2"] +
                ["ld_x0", "ld_x1", "ld_t", "ld_misc", "ld_wq", "ld_h0",
                 "ld_h1", "ld_h2", "ld_h3", "ld_q1", "ld_wb0", "ld_wb1", "ld_kvb", "ld_kvm", "ld_wm",
                 "ld_cc2",
                 "ld_cc"] +
                ["st_q0", "st_q1", "st_h0", "st_h1", "st_out", "st_cc"])
    handles = {n: es.enter_context(nc.semaphore(n)) for n in semnames}

    def fresh_sems():
        return {n: _Sem(n, handles[n]) for n in semnames}

    def wqkv_bf(d):
        return arena[:, d * 3072:(d + 1) * 3072]

    def w1qv(d):
        return arena[:, d * 4096:(d + 1) * 4096]

    def w2qv(f):
        return arena[:, f * 1024:(f + 1) * 1024]

    def schedule(ctx):
        op = ctx.op
        R = {}

        def res(name):
            if name not in R:
                R[name] = _Res(name)
            return R[name]

        r_btx = res("btx")
        r_bts = res("bts")
        r_bto = res("bto")
        r_btsh = [res(f"btsh{i}") for i in range(4)]
        r_hrel = [res(f"hrel{i}") for i in range(4)]
        r_bb = [res(f"bb{i}") for i in range(3)]
        r_wq = [res(f"wq{d}") for d in range(DC)]
        r_w1q = [res(f"w1q{fo}") for fo in range(HC)]
        r_w2q = [res(f"w2q{f}") for f in range(HC)]
        r_woutbf = res("woutbf")
        r_wbst = [res(f"wbst{i}") for i in range(2)]
        r_zbld = [res("m0h0"), res("m0h1")]
        r_xn = [res(f"xn{i}") for i in range(2)]
        r_zq = [res("e0h0"), res("e0h1")]
        r_m0 = [res("m0h0"), res("m0h1")]
        r_m1 = r_m0
        r_e0 = [res("e0h0"), res("e0h1")]
        r_e1 = r_e0
        r_hq = [res(f"hq{i}") for i in range(4)]
        r_et = [res("e0h0"), res("e0h1")]
        r_k1 = [res(f"k1tm{i}") for i in range(2)]
        r_v65 = [res(f"v65_{i}") for i in range(2)]
        r_kvacc = res("kvacc")
        r_kvblk = res("kvblk")
        r_denblk = res("denblk")
        r_maskc = res("maskc")
        r_sel2 = res("sel2")
        r_ones = res("sel2")
        r_gampp = res("gam_pp_sb")
        r_beta = res("beta_pp")
        r_b1 = res("b1_pp")
        r_b2 = res("b2_pp")
        r_b1ss = res("b1ss")
        r_b2ss = res("b2ss")
        r_ident = res("ident_sb")
        r_statp = res("statp")
        r_bnst = res("bnst")
        r_agg8 = res("agg8")
        r_tmp4 = res("tmp4")
        r_rows2 = res("rows2")
        r_r2row = res("den_sb")
        r_r2rowbf = res("rec_bf")
        r_densb = res("den_sb")
        r_recbf = res("rec_bf")
        r_recb = [res("m0h0"), res("m0h1")]
        r_kvstage = res("kvstage")
        r_denstage = res("denstage")
        r_mxxn = res("mx_xn")
        r_mxh = res("mx_h")
        r_mxw1 = res("mx_w1")
        r_mxw2 = res("mx_w2")
        r_tred = res("tred")
        r_epsp = res("scal")
        r_scal = res("scal")
        r_oh = res("bnst")
        r_oht = res("oh_t")
        r_gmx8 = res("oh_t")
        r_P = [res(f"P{i}") for i in range(8)]
        r_q1d = res("q1_dram")
        r_hd = [res("h_dram0"), res("h_dram1")]
        r_cc1i = res("cc1i")
        r_cc1o = res("cc1o")
        r_cc1ai = res("cc1ai")
        r_cc1ao = res("cc1ao")
        r_cc2i = res("cc2i")
        r_cc2o = res("cc2o")
        r_out = res("out_fm")

        def V(emit, reads=(), writes=(), wnowait=()):
            return op("vector", emit, reads, writes, wnowait)

        def Sc(emit, reads=(), writes=(), wnowait=()):
            return op("scalar", emit, reads, writes, wnowait)

        def G(emit, reads=(), writes=(), wnowait=()):
            return op("gpsimd", emit, reads, writes, wnowait)

        def Tn(emit, reads=(), writes=(), wnowait=()):
            return op("tensor", emit, reads, writes, wnowait)

        def DMA(eng, emit, reads=(), writes=(), sem=None):
            return op(eng, emit, reads, writes, sem=sem, dinc=16)

        def sc_col(i):
            return scal[:, i:i + 1]

        def sc1(i):
            return scal[0:1, i:i + 1]

        def stats_rows(blk, c0, dst_rows, r_dst):
            for s in range(4):
                base = (blk * 4 + s) * 6
                Tn(lambda e, b=base, s=s: e.transpose(
                    P[6][0:2, s * 128:(s + 1) * 128],
                    statp[:, b + c0: b + c0 + 2], ident_sb[:, :]),
                   reads=[r_statp, r_ident],
                   writes=[r_P[6]] if s == 0 else [],
                   wnowait=[r_P[6]] if s == 3 else [])
            V(lambda e: e.tensor_copy(dst_rows[0:2, 0:512], P[6][0:2, 0:512]),
              reads=[r_P[6]], writes=[r_dst])

        def cross_part_max(r_acc, acc_t, out_col):
            Tn(lambda e: e.transpose(P[7][0:1, 0:128], acc_t[:, :], ident_sb[:, :]),
               reads=[r_acc, r_ident], writes=[r_P[7]])
            V(lambda e, oc=out_col: e.tensor_reduce(
                scal[0:1, oc:oc + 1], P[7][0:1, 0:128], AX.X, ALU.max),
              reads=[r_P[7]], writes=[r_scal])

        def bcast_scalar(src_col, dst_col):
            Tn(lambda e: e.matmul(P[7][:, 0:1], sel2[0:1, 0:128], sc1(src_col),
                                  start=True, stop=True),
               reads=[r_ones, r_scal], writes=[r_P[7]])
            V(lambda e: e.tensor_copy(sc_col(dst_col), P[7][:, 0:1]),
              reads=[r_P[7]], writes=[r_scal])

        const_dsts = [(r_gampp, gam_pp, d_gam), (r_beta, beta_pp, d_beta),
                      (r_b1, b1_pp, d_b1), (r_b2, b2_pp, d_b2),
                      (r_ident, ident_sb, d_ident),
            (r_sel2, sel2, d_sel2)]
        for (dst, dsth, src) in const_dsts:
            DMA("sync", lambda e, dh=dsth, s=src: e.dma_start(
                out=dh[:, :] if hasattr(dh, "ap") else dh, in_=s[:, :]),
                writes=[dst], sem="ld_misc")

        V(lambda e: e.memset(scal[:, SC_EPS:SC_EPS + 1], LN_EPS), writes=[r_epsp])
        V(lambda e: e.memset(scal[:, SC_DEPS:SC_DEPS + 1], DEN_EPS), writes=[r_epsp])
        DMA("sync", lambda e: e.dma_start(out=bto[0:16, 1024:2048], in_=d_mask[:, :]),
            writes=[r_bto], sem="ld_misc")
        _misc_total = ctx.sems["ld_misc"].count
        for (dst, _, _) in const_dsts:
            dst.w = ("ld_misc", _misc_total)
        r_bto.w = ("ld_misc", _misc_total)
        V(lambda e: e.tensor_copy(maskc[0:16, 0:1024], bto[0:16, 1024:2048]),
          reads=[r_bto], writes=[r_maskc])
        for mres, mt in [(r_mxxn, mx_xn), (r_mxh, mx_h), (r_mxw1, mx_w1), (r_mxw2, mx_w2)]:
            V(lambda e, m=mt: e.memset(m[:, :], 0.0), writes=[mres])
        V(lambda e: e.memset(kvacc[:, :], 0.0), writes=[r_kvacc])
        for s in range(2):
            V(lambda e, s=s: e.memset(
                v65[s][:, :].rearrange("p (h v) -> p h v", v=65)[:, :, 64:65], 1.0),
              writes=[r_v65[s]])

        for d in range(DC):
            slot = bto if d % 2 == 0 else bts
            r_slot = [r_bto] if d % 2 == 0 else [r_bts] + r_btsh
            DMA("sync", lambda e, d=d, slot=slot: e.dma_start(
                out=slot[:, 0:3072], in_=d_wqkv[d * 128:(d + 1) * 128, :]),
                writes=list(r_slot), sem="ld_wq" if d % 2 == 0 else "ld_kvm")
            V(lambda e, d=d, slot=slot: e.tensor_copy(wqkv_bf(d)[:, :], slot[:, 0:3072]),
              reads=list(r_slot), writes=[r_wq[d]])


        def stats_rows2(blk, c0, pbank, r_pbank):
            for s in range(4):
                base = (blk * 4 + s) * 6
                Tn(lambda e, b=base, s=s, pb=pbank: e.transpose(
                    pb[0:2, s * 128:(s + 1) * 128],
                    statp[:, b + c0: b + c0 + 2], ident_sb[:, :]),
                   reads=[r_statp, r_ident],
                   writes=[r_pbank] if s == 0 else [],
                   wnowait=[r_pbank] if s == 3 else [])
            V(lambda e, pb=pbank: e.tensor_copy(rows2[0:2, 0:512], pb[0:2, 0:512]),
              reads=[r_pbank], writes=[r_rows2])

        def emit_stats(blk):
            pieces = []

            def bn_piece(slo, shi):
                def f():
                    for s in range(slo, shi):
                        xv = bts[:, s * 1024:(s + 1) * 1024]
                        V(lambda e, xv=xv: e.bn_stats(bnst[:, 0:6], xv[:, 0:512]),
                          reads=[r_bts], writes=[r_bnst])
                        V(lambda e, xv=xv: e.bn_stats(bnst[:, 6:12], xv[:, 512:1024]),
                          reads=[r_bts], writes=[r_bnst])
                        V(lambda e, s=s: e.bn_aggr(
                            agg8[:, :].rearrange("p (k s) -> p k s", k=2)[:, :, s:s + 1],
                            bnst[:, 0:12]),
                          reads=[r_bnst], writes=[r_agg8])
                return f
            pieces.append(bn_piece(0, 2))
            pieces.append(bn_piece(2, 4))

            def chain():
                bb0 = blk * 24
                MU = agg8[:, 0:4]
                VAR = agg8[:, 4:8]

                def spv(c):
                    return statp[:, bb0:bb0 + 24].rearrange(
                        "p (s c) -> p c s", c=6)[:, c:c + 1, :]
                Sc(lambda e: e.activation(tmp4[:, 0:4], VAR, AF.Ln,
                                          bias=scal[:, SC_EPS:SC_EPS + 1]),
                   reads=[r_agg8, r_scal], writes=[r_tmp4])
                Sc(lambda e: e.activation(spv(1), tmp4[:, 0:4], AF.Exp, scale=-0.5),
                   reads=[r_tmp4], writes=[r_statp])
                V(lambda e: e.tensor_tensor(tmp4[:, 4:8], VAR, spv(1), ALU.mult),
                  reads=[r_agg8, r_statp], writes=[r_tmp4])
                V(lambda e: e.tensor_tensor(tmp4[:, 4:8], tmp4[:, 4:8], spv(1), ALU.mult),
                  reads=[r_statp], writes=[r_tmp4])
                Sc(lambda e: e.activation(tmp4[:, 0:4], tmp4[:, 4:8], AF.Ln,
                                          bias=scal[:, SC_EPS:SC_EPS + 1]),
                   reads=[r_tmp4, r_scal], writes=[r_tmp4])
                Sc(lambda e: e.activation(tmp4[:, 4:8], tmp4[:, 0:4], AF.Exp, scale=-0.5),
                   reads=[r_tmp4], writes=[r_tmp4])
                V(lambda e: e.tensor_tensor(spv(2), spv(1), tmp4[:, 4:8], ALU.mult),
                  reads=[r_tmp4], writes=[r_statp])
                V(lambda e: e.tensor_tensor(spv(3), MU, spv(2), ALU.mult),
                  reads=[r_agg8], writes=[r_statp])
                V(lambda e: e.tensor_tensor(spv(4), MU, spv(1), ALU.mult),
                  reads=[r_agg8], writes=[r_statp])
                V(lambda e: e.tensor_scalar_add(spv(5), spv(1), 1.0),
                  writes=[r_statp])
                V(lambda e: e.tensor_copy(spv(0), MU),
                  reads=[r_agg8], writes=[r_statp])
            pieces.append(chain)

            def maxdev(slo, shi):
                def f():
                    bb0 = blk * 24
                    for s in range(slo, shi):
                        xv = bts[:, s * 1024:(s + 1) * 1024]
                        V(lambda e, xv=xv, s=s: e.tensor_scalar(
                            m_t[0][:, :], xv, agg8[:, s:s + 1], None, ALU.subtract),
                          reads=[r_bts, r_agg8], writes=list(r_m0))
                        V(lambda e: e.tensor_reduce(
                            tred[:, :], m_t[0][:, :], AX.X, ALU.max,
                            apply_absolute_value=True),
                          reads=list(r_m0), writes=[r_tred])
                        V(lambda e, s=s, bb0=bb0: e.tensor_tensor(
                            tred[:, :], tred[:, :],
                            statp[:, bb0 + s * 6 + 1:bb0 + s * 6 + 2], ALU.mult),
                          reads=[r_statp], writes=[r_tred])
                        V(lambda e: e.tensor_tensor(
                            mx_xn[:, :], mx_xn[:, :], tred[:, :], ALU.max),
                          reads=[r_tred], writes=[r_mxxn])
                return f
            pieces.append(maxdev(0, 2))
            pieces.append(maxdev(2, 4))
            return pieces

        def a_prep_pieces(blk):
            pieces = []

            def loads():
                DMA("sync", lambda e, b=blk: e.dma_start(
                    out=btx[:, :], in_=d_xfm[b * 128:(b + 1) * 128, :]),
                    writes=[r_btx], sem="ld_x0")
                DMA("sync", lambda e, b=blk: e.dma_start(
                    out=bts[:, :], in_=d_xtm[b * 128:(b + 1) * 128, :]),
                    writes=[r_bts] + r_btsh, sem="ld_t")
            pieces.append(("early", loads))
            pieces.extend(("early", p) for p in emit_stats(blk))

            def rows_t():
                stats_rows2(blk, 2, P[4], r_P[4])
            pieces.append(("latev", rows_t))

            def rows_mm():
                Tn(lambda e: e.matmul(P[4][:, 0:512], sel2[:, 0:128],
                                      rows2[0:2, 0:512], start=True, stop=True),
                   reads=[r_sel2, r_rows2], writes=[r_P[4]])
                Tn(lambda e: e.matmul(P[5][:, 0:512], sel2[:, 128:256],
                                      rows2[0:2, 0:512], start=True, stop=True),
                   reads=[r_sel2, r_rows2], writes=[r_P[5]])

            xn2 = bb[blk % 2]
            for dch in range(DC):
                def t2v(dch=dch):
                    xs = dch % 2
                    xnv, r_xnv = xn_t[xs], r_xn[xs]
                    V(lambda e, dch=dch, xnv=xnv: e.tensor_tensor(
                        xnv[:, :], btx[:, dch * 512:(dch + 1) * 512],
                        P[4][:, 0:512], ALU.mult),
                      reads=[r_btx, r_P[4]], writes=[r_xnv])
                    V(lambda e, xnv=xnv: e.tensor_tensor(
                        xnv[:, :], xnv[:, :], P[5][:, 0:512], ALU.subtract),
                      reads=[r_P[5]], writes=[r_xnv])

                def t2s(dch=dch, xn2=xn2):
                    xs = dch % 2
                    xnv, r_xnv = xn_t[xs], r_xn[xs]
                    Sc(lambda e, dch=dch, xnv=xnv, x2=xn2: e.activation(
                        x2[:, dch * 512:(dch + 1) * 512], xnv[:, :], AF.Identity,
                        scale=gam_pp[:, dch:dch + 1], bias=beta_pp[:, dch:dch + 1]),
                       reads=[r_xnv, r_beta, r_gampp],
                       writes=[res(f"bb{blk % 2}_{dch}")] +
                              ([r_bb[1]] if (blk % 2 == 1 and dch == 0) else []))
                pieces.append(("latev", t2v))
                pieces.append(("lates", t2s))
            return pieces, rows_mm

        def a_main(blk, pq, rows_mm=None):
            bat = blk // NB2
            xn2 = bb[blk % 2]

            def drain(n=1, kinds=None):
                for _ in range(n):
                    if kinds is None:
                        if pq:
                            pq.pop(0)[1]()
                    else:
                        for i, (k, f) in enumerate(pq):
                            if k in kinds:
                                pq.pop(i)
                                f()
                                break

            q1b, r_q1b = bb[2], r_bb[2]
            for fo in range(DC):
                pqb, r_pqb = P[4 + fo % 2], r_P[4 + fo % 2]
                for dch in range(DC):
                    Tn(lambda e, fo=fo, dch=dch, pqb=pqb: e.matmul(
                        pqb[:, 0:512], wqkv_bf(dch)[:, fo * 128:(fo + 1) * 128],
                        xn2[:, dch * 512:(dch + 1) * 512],
                        start=(dch == 0), stop=(dch == DC - 1)),
                       reads=[r_wq[dch], res(f"bb{blk % 2}_{dch}")],
                       writes=[r_pqb] if dch == 0 else [],
                       wnowait=[r_pqb] if dch == DC - 1 else [])
                ms = fo % 2
                Sc(lambda e, pqb=pqb, ms=ms: e.activation(
                    m_t[0][:, ms * 512:(ms + 1) * 512], pqb[:, 0:512],
                    AF.Relu, scale=-1.0),
                   reads=[r_pqb], writes=[r_m0[ms]])
                Sc(lambda e, ms=ms: e.activation(
                    e_t[0][:, ms * 512:(ms + 1) * 512],
                    m_t[0][:, ms * 512:(ms + 1) * 512], AF.Exp, scale=-1.0),
                   reads=[r_m0[ms]], writes=[r_e0[ms]])
                V(lambda e, pqb=pqb, ms=ms, fo=fo: e.scalar_tensor_tensor(
                    q1b[:, fo * 512:(fo + 1) * 512], pqb[:, 0:512], 0.0,
                    e_t[0][:, ms * 512:(ms + 1) * 512], ALU.max, ALU.add),
                  reads=[r_pqb, r_e0[ms]], writes=[r_q1b])
                drain(1, kinds={"early"})
            DMA("gpsimd", lambda e, b=blk, q=q1b: e.dma_start(
                out=d_q1[b * 128:(b + 1) * 128, :], in_=q[:, :]),
                reads=[r_q1b], writes=[r_q1d], sem=f"st_q{blk % 2}")

            def kv_pair_mms(tc, hs, k1v, vv, r_k1v, r_vv):
                for h in hs:
                    pb, r_pb = P[h // 8], r_P[h // 8]
                    first = (tc == 0 and h % 8 == 0)
                    last = (tc == 3 and h % 8 == 7)
                    Tn(lambda e, h=h, pb=pb, fs=first, ls=last,
                       k1v=k1v, vv=vv: e.matmul(
                        pb[0:64, (h % 8) * 64:(h % 8 + 1) * 64],
                        k1v[:, h * 64:(h + 1) * 64],
                        vv[:, h * 65:h * 65 + 64],
                        start=fs, stop=ls),
                       reads=[r_k1v, r_vv],
                       writes=[r_pb] if first else [],
                       wnowait=[r_pb] if last else [])
                    Tn(lambda e, h=h, tc=tc, k1v=k1v, vv=vv: e.matmul(
                        P[2][0:64, h:h + 1],
                        k1v[:, h * 64:(h + 1) * 64],
                        vv[:, h * 65 + 64:h * 65 + 65],
                        start=(tc == 0 and h == 0), stop=(tc == 3 and h == 15)),
                       reads=[r_k1v, r_vv],
                       writes=[r_P[2]] if (tc == 0 and h == 0) else [],
                       wnowait=[r_P[2]] if (tc == 3 and h == 15) else [])

            pend = None
            for tc in range(4):
                ks = tc % 2
                k1v, r_k1v = k1_tm[ks], r_k1[ks]
                vv, r_vv = v65[ks], r_v65[ks]
                for qi, qtr in enumerate([0, 2, 1, 3]):
                    kb = [3, 6, 7][(tc * 4 + qi) % 3]
                    pk, r_pk = P[kb], r_P[kb]
                    for dch in range(DC):
                        Tn(lambda e, tc=tc, qtr=qtr, dch=dch, pk=pk: e.matmul(
                            pk[:, 0:512],
                            xn2[:, dch * 512 + tc * 128: dch * 512 + (tc + 1) * 128],
                            wqkv_bf(dch)[:, 1024 + qtr * 512:1024 + (qtr + 1) * 512],
                            start=(dch == 0), stop=(dch == DC - 1)),
                           reads=[res(f"bb{blk % 2}_{dch}"), r_wq[dch]],
                           writes=[r_pk] if dch == 0 else [],
                           wnowait=[r_pk] if dch == DC - 1 else [])
                    if pend is not None:
                        kv_pair_mms(*pend)
                        pend = None
                    if qtr < 2:
                        Sc(lambda e, pk=pk, qtr=qtr: e.activation(
                            m_t[1][:, qtr * 512:(qtr + 1) * 512], pk[:, 0:512],
                            AF.Relu, scale=-1.0),
                           reads=[r_pk], writes=[r_m1[qtr]])
                        Sc(lambda e, ks=ks, qtr=qtr: e.activation(
                            e_t[1][:, qtr * 512:(qtr + 1) * 512],
                            m_t[1][:, qtr * 512:(qtr + 1) * 512], AF.Exp, scale=-1.0),
                           reads=[r_m1[qtr]], writes=[r_e1[qtr]])
                        V(lambda e, pk=pk, ks=ks, qtr=qtr, k1v=k1v: e.scalar_tensor_tensor(
                            k1v[:, qtr * 512:(qtr + 1) * 512], pk[:, 0:512], 0.0,
                            e_t[1][:, qtr * 512:(qtr + 1) * 512], ALU.max, ALU.add),
                          reads=[r_pk, r_e1[qtr]], writes=[r_k1v])
                    else:
                        Sc(lambda e, pk=pk, vv=vv, qtr=qtr: e.activation(
                            vv[:, :].rearrange("p (h v) -> p h v", v=65)
                              [:, (qtr - 2) * 8:(qtr - 1) * 8, 0:64],
                            pk[:, 0:512].rearrange("p (h v) -> p h v", v=64),
                            AF.Copy),
                           reads=[r_pk], writes=[r_vv])
                    if qi == 1:
                        if tc == 0 and rows_mm is not None:
                            rows_mm()
                        pend = (tc, range(0, 8), k1v, vv, r_k1v, r_vv)
                        drain(2)
                    elif qi == 3:
                        if tc == 3:
                            kv_pair_mms(tc, range(8, 16), k1v, vv, r_k1v, r_vv)
                        else:
                            pend = (tc, range(8, 16), k1v, vv, r_k1v, r_vv)
                        drain(2)
                    else:
                        drain(1, kinds={"early", "latev"})
                drain(1)
            for bk in range(2):
                coff = bat * 1024 + bk * 512
                V(lambda e, bk=bk, coff=coff: e.tensor_tensor(
                    kvacc[0:64, coff:coff + 512], kvacc[0:64, coff:coff + 512],
                    P[bk][0:64, 0:512], ALU.add),
                  reads=[r_P[bk]], writes=[r_kvacc])
            V(lambda e, bat=bat: e.tensor_tensor(
                kvacc[0:64, 2048 + bat * 16:2048 + (bat + 1) * 16],
                kvacc[0:64, 2048 + bat * 16:2048 + (bat + 1) * 16],
                P[2][0:64, 0:16], ALU.add),
              reads=[r_P[2]], writes=[r_kvacc])

            den = max(NBLK - 1, 1)
            for wci in (range(blk * 8 // den, (blk + 1) * 8 // den)
                        if blk < den else []):
                DMA("sync", lambda e, rlo=wci * 512: e.dma_start(
                    out=bto[:, 0:4096].rearrange("p (c f) -> p c f", c=4),
                    in_=d_w1b[rlo:rlo + 512, :].rearrange("(p c) f -> p c f", c=4)),
                    writes=[r_bto], sem="ld_wm")
                V(lambda e: e.tensor_reduce(
                    tred[:, :], bto[:, 0:4096], AX.X, ALU.max, apply_absolute_value=True),
                  reads=[r_bto], writes=[r_tred])
                V(lambda e: e.tensor_tensor(
                    mx_w1[:, :], mx_w1[:, :], tred[:, :], ALU.max),
                  reads=[r_tred], writes=[r_mxw1])
            drain(99)

        bslot = [wbst[0], wbst[1], bto[:, 0:1024]]
        r_bslot = [r_wbst[0], r_wbst[1], r_bto]
        bsem = ["ld_wb0", "ld_wb1", "ld_wm"]

        def build_load(fo):
            ws = fo % 3
            op("scalar", lambda e, fo=fo, ws=ws: e.dma_start(
                out=bslot[ws][:, 0:1024],
                in_=d_w1b[fo * 128:(fo + 1) * 128, :]),
                writes=[r_bslot[ws]], sem=bsem[ws], dinc=16)

        def build_compute(fo):
            ws = fo % 3
            Sc(lambda e, ws=ws: e.activation(zbld[:, :], bslot[ws][:, :1024], AF.Copy,
                                             bias=MAGIC, scale=sc_col(SC_SWQINV)),
               reads=[r_bslot[ws], r_scal], writes=list(r_zbld))
            wr = [r_w1q[fo]] + (r_wq if fo == 0 else [])
            V(lambda e, fo=fo: e.tensor_scalar_add(
                arena[:, :].rearrange("p (c f) -> p c f", c=DC)[:, :, fo * 128:(fo + 1) * 128],
                zbld[:, :].rearrange("p (c f) -> p c f", c=DC), -MAGIC),
              reads=list(r_zbld), writes=wr)

        def boundary67():
            cross_part_max(r_mxxn, mx_xn, SC_T0)
            DMA("sync", lambda e: e.dma_start(out=oh_sb, in_=d_oh[:, :]),
                writes=[r_oh], sem="ld_cc")
            V(lambda e: e.tensor_scalar(oh_t[0:1, 0:8], oh_sb[0:1, 0:8],
                                        sc1(SC_T0), None, ALU.mult),
              reads=[r_oh, r_scal], writes=[r_oht])
            DMA("gpsimd", lambda e: e.dma_start(out=d_cc1ai[0:1, 0:8], in_=oh_t[0:1, 0:8]),
                reads=[r_oht], writes=[r_cc1ai], sem="st_cc")
            op("gpsimd", lambda e: e.collective_compute(
                "AllReduce", ALU.add, replica_groups=[list(range(n_cores))],
                ins=[d_cc1ai.ap()], outs=[d_cc1ao.ap()]),
               reads=[r_cc1ai], writes=[r_cc1ao], sem="cc1a", dinc=1)
            DMA("sync", lambda e: e.dma_start(
                out=btx[:, :], in_=d_xfm[0:128, :]),
                writes=[r_btx], sem="ld_x0")
            cross_part_max(r_mxw1, mx_w1, SC_SW1C)
            V(lambda e: e.tensor_scalar(sc1(SC_SW1V), sc1(SC_SW1C), MIN_RANGE,
                                        INV_QP, ALU.max, ALU.mult),
              writes=[r_scal])
            V(lambda e: e.reciprocal(sc1(SC_SW1I), sc1(SC_SW1V)), writes=[r_scal])
            bcast_scalar(SC_SW1I, SC_SWQINV)
            for fo in range(3):
                build_load(fo)

        pq_a, rm0 = a_prep_pieces(0)
        for k, f in pq_a[:7]:
            f()
        rm0()
        for k, f in pq_a[7:]:
            f()
        for blk in range(NBLK):
            if blk + 1 < NBLK:
                nxt, rm_next = a_prep_pieces(blk + 1)
            else:
                nxt, rm_next = [], None
            if blk == NBLK - 1:
                boundary67()
            a_main(blk, nxt, rm_next)

        DMA("gpsimd", lambda e: e.dma_start(out=gmx8[0:1, 0:8], in_=d_cc1ao[0:1, 0:8]),
            reads=[r_cc1ao], writes=[r_gmx8], sem="ld_cc2")

        DMA("gpsimd", lambda e: e.dma_start(out=d_cc1i[:, 0:2080], in_=kvacc[:, :]),
            reads=[r_kvacc], writes=[r_cc1i], sem="st_cc")
        op("gpsimd", lambda e: e.collective_compute(
            "AllReduce", ALU.add, replica_groups=[list(range(n_cores))],
            ins=[d_cc1i.ap()], outs=[d_cc1o.ap()]),
           reads=[r_cc1i], writes=[r_cc1o], sem="cc1", dinc=1)

        build_compute(0)
        build_load(3)
        build_compute(1)
        build_load(4)

        def b1_prep_pieces(blk):
            pieces = []

            def loads():
                DMA("sync", lambda e, b=blk: e.dma_start(
                    out=btx[:, :], in_=d_xfm[b * 128:(b + 1) * 128, :]),
                    writes=[r_btx], sem="ld_x0")
            pieces.append(loads)

            def rows_bcast():
                bb0 = blk * 24

                def spv(c):
                    return statp[:, bb0:bb0 + 24].rearrange(
                        "p (s c) -> p c s", c=6)[:, c:c + 1, :]
                V(lambda e: e.tensor_scalar(spv(2), spv(1), sc_col(SC_SXINV), None,
                                            ALU.mult),
                  reads=[r_scal], writes=[r_statp])
                V(lambda e: e.tensor_tensor(spv(3), spv(0), spv(2), ALU.mult),
                  writes=[r_statp])
                stats_rows2(blk, 2, P[6], r_P[6])
                Tn(lambda e: e.matmul(P[0][:, 0:512], sel2[:, 0:128],
                                      rows2[0:2, 0:512], start=True, stop=True),
                   reads=[r_sel2, r_rows2], writes=[r_P[0]])
                Tn(lambda e: e.matmul(P[1][:, 0:512], sel2[:, 128:256],
                                      rows2[0:2, 0:512], start=True, stop=True),
                   reads=[r_sel2, r_rows2], writes=[r_P[1]])
            pieces.append(rows_bcast)

            for dch in range(DC):
                def xq_piece(dch=dch):
                    xs = dch % 2
                    zqv, r_zqv = zq_t[xs], r_zq[xs]
                    V(lambda e, dch=dch, zqv=zqv: e.tensor_tensor(
                        zqv[:, :], btx[:, dch * 512:(dch + 1) * 512],
                        P[0][:, 0:512], ALU.mult),
                      reads=[r_btx, r_P[0]], writes=[r_zqv])
                    V(lambda e, zqv=zqv: e.tensor_tensor(
                        zqv[:, :], zqv[:, :], P[1][:, 0:512], ALU.subtract),
                      reads=[r_P[1]], writes=[r_zqv])
                    V(lambda e, dch=dch, zqv=zqv, b=blk: e.tensor_scalar(
                        bb[b % 2][:, dch * 512:(dch + 1) * 512],
                        zqv[:, :], MAGIC, MAGIC, ALU.add, ALU.subtract),
                      reads=[r_zqv], writes=[res(f"bb{blk % 2}_{dch}")])
                pieces.append(xq_piece)
            return pieces

        pq_b1 = b1_prep_pieces(0)[1:]

        V(lambda e: e.tensor_reduce(sc1(SC_T0), gmx8[0:1, 0:8], AX.X, ALU.max),
          reads=[r_gmx8], writes=[r_scal])
        V(lambda e: e.tensor_scalar(sc1(SC_GXC), sc1(SC_T0), MIN_RANGE, INV_QP,
                                    ALU.max, ALU.mult),
          writes=[r_scal])
        V(lambda e: e.reciprocal(sc1(SC_SXV), sc1(SC_GXC)), writes=[r_scal])
        V(lambda e: e.tensor_tensor(sc1(SC_T1), sc1(SC_SW1V), sc1(SC_GXC), ALU.mult),
          writes=[r_scal])
        bcast_scalar(SC_SXV, SC_SXINV)
        bcast_scalar(SC_T1, SC_SS1)
        V(lambda e: e.tensor_scalar(b1ss[:, :], b1_pp[:, :], sc_col(SC_SS1), None, ALU.mult),
          reads=[r_b1, r_scal], writes=[r_b1ss])

        while pq_b1:
            pq_b1.pop(0)()

        def kvblk_dma_piece(b, c):
            def f():
                off = b * 1024 + 2 * c * 64
                soff = 2048 + b * 16 + 2 * c
                V(lambda e: e.memset(kvstage[:, :], 0.0), writes=[r_kvstage])
                V(lambda e: e.memset(denstage[:, :], 0.0), writes=[r_denstage])
                op("sync", lambda e, off=off: e.dma_start(
                    out=kvstage[0:64, 0:64], in_=d_cc1o[0:64, off:off + 64]),
                   reads=[r_cc1o], writes=[r_kvstage], sem="ld_kvb", dinc=16)
                op("sync", lambda e, off=off: e.dma_start(
                    out=kvstage[64:128, 64:128], in_=d_cc1o[0:64, off + 64:off + 128]),
                   reads=[r_cc1o], writes=[r_kvstage], sem="ld_kvb", dinc=16)
                op("sync", lambda e, soff=soff, c=c: e.dma_start(
                    out=denstage[0:64, 2 * c:2 * c + 1], in_=d_cc1o[0:64, soff:soff + 1],
                    allow_slow_non_contiguous=True),
                   reads=[r_cc1o], writes=[r_denstage], sem="ld_kvb", dinc=16)
                op("sync", lambda e, soff=soff, c=c: e.dma_start(
                    out=denstage[64:128, 2 * c + 1:2 * c + 2],
                    in_=d_cc1o[0:64, soff + 1:soff + 2],
                    allow_slow_non_contiguous=True),
                   reads=[r_cc1o], writes=[r_denstage], sem="ld_kvb", dinc=16)
                tot = ("ld_kvb", ctx.sems["ld_kvb"].count)
                r_kvstage.w, r_kvstage.rs = tot, []
                r_denstage.w, r_denstage.rs = tot, []
            return f

        def kvblk_cast_piece(b, c):
            def f():
                t = b * 8 + c
                V(lambda e, t=t: e.tensor_copy(kvblk[:, t * 128:(t + 1) * 128], kvstage[:, :]),
                  reads=[r_kvstage], writes=[r_kvblk])
                V(lambda e, t=t: e.tensor_copy(denblk[:, t * 16:(t + 1) * 16], denstage[:, :]),
                  reads=[r_denstage], writes=[r_denblk])
            return f

        def wout_piece(d):
            def f():
                DMA("sync", lambda e, d=d: e.dma_start(
                    out=wbst[d % 2][:, :], in_=d_wout[d * 128:(d + 1) * 128, :]),
                    writes=[r_wbst[d % 2]], sem=f"ld_wb{d % 2}")
                V(lambda e, d=d: e.tensor_copy(woutbf[:, d * 1024:(d + 1) * 1024],
                                               wbst[d % 2][:, :]),
                  reads=[r_wbst[d % 2]], writes=[r_woutbf])
            return f

        def w2max_piece(wci):
            def f():
                rlo = wci * 512
                DMA("sync", lambda e, rlo=rlo: e.dma_start(
                    out=bto[:, 0:4096].rearrange("p (c f) -> p c f", c=4),
                    in_=d_w2t[rlo:rlo + 512, :].rearrange("(c p) f -> p c f", p=128)),
                    writes=[r_bto], sem="ld_wm")
                V(lambda e: e.tensor_reduce(
                    tred[:, :], bto[:, 0:4096], AX.X, ALU.max, apply_absolute_value=True),
                  reads=[r_bto], writes=[r_tred])
                V(lambda e: e.tensor_tensor(
                    mx_w2[:, :], mx_w2[:, :], tred[:, :], ALU.max),
                  reads=[r_tred], writes=[r_mxw2])
            return f

        extra_b1 = {}
        kv_bc = [(b, c) for b in range(2) for c in range(8)]
        all_extras = []
        for i, (b, c) in enumerate(kv_bc):
            all_extras.append(kvblk_dma_piece(b, c))
            all_extras.append(kvblk_cast_piece(b, c))
        all_extras += [wout_piece(d) for d in range(DC)]
        all_extras += [w2max_piece(i) for i in range(8)]
        for i, p in enumerate(all_extras):
            key = min(2 + i * (NBLK - 2) // len(all_extras), NBLK - 1)
            extra_b1.setdefault(key, []).append(p)

        for blk in range(NBLK):
            pq = (b1_prep_pieces(blk + 1) if blk + 1 < NBLK else [])
            pq = pq + extra_b1.get(blk, [])

            def drain(n=1):
                for _ in range(n):
                    if pq:
                        pq.pop(0)()

            xqb = bb[blk % 2]
            for fo in range(HC):
                pf, r_pf = P[2 + fo % 4], r_P[2 + fo % 4]
                for dch in range(DC):
                    Tn(lambda e, fo=fo, dch=dch, pf=pf: e.matmul(
                        pf[:, 0:512], w1qv(dch)[:, fo * 128:(fo + 1) * 128],
                        xqb[:, dch * 512:(dch + 1) * 512],
                        start=(dch == 0), stop=(dch == DC - 1)),
                       reads=[r_w1q[fo], res(f"bb{blk % 2}_{dch}")],
                       writes=[r_pf] if dch == 0 else [],
                       wnowait=[r_pf] if dch == DC - 1 else [])
                qrt = (fo // 2) % 4
                hsl = bts[:, qrt * 1024 + (fo % 2) * 512: qrt * 1024 + (fo % 2 + 1) * 512]
                hres = r_btsh[qrt]
                extra = [r_bts] if (blk == 0 and fo < 8) else []
                Sc(lambda e, pf=pf, hsl=hsl, fo=fo: e.activation(
                    hsl, pf[:, 0:512], AF.Gelu,
                    bias=b1ss[:, fo:fo + 1], scale=sc_col(SC_SS1)),
                   reads=[r_pf, r_b1ss, r_scal], writes=[hres] + extra)
                if fo % 2 == 1:
                    V(lambda e, qrt=qrt: e.tensor_reduce(
                        tred[:, :], bts[:, qrt * 1024:(qrt + 1) * 1024],
                        AX.X, ALU.max, apply_absolute_value=True),
                      reads=[hres], writes=[r_tred])
                    V(lambda e: e.tensor_tensor(mx_h[:, :], mx_h[:, :], tred[:, :], ALU.max),
                      reads=[r_tred], writes=[r_mxh])
                    tile = blk * 16 + (fo - 1) // 2
                    seng = "gpsimd" if qrt % 2 == 0 else "scalar"
                    op(seng, lambda e, tile=tile, qrt=qrt: e.dma_start(
                        out=d_h[tile * 128:(tile + 1) * 128, :],
                        in_=bts[:, qrt * 1024:(qrt + 1) * 1024]),
                       reads=[hres], writes=[r_hd[qrt % 2]], sem=f"st_h{qrt % 2}", dinc=16)
                    drain(1)
                if blk == 0:
                    if fo + 2 < HC:
                        build_compute(fo + 2)
                    if fo + 5 < HC:
                        build_load(fo + 5)
            drain(99)

        cross_part_max(r_mxh, mx_h, SC_T0)
        DMA("sync", lambda e: e.dma_start(out=oh_sb, in_=d_oh[:, :]),
            writes=[r_oh], sem="ld_cc")
        V(lambda e: e.tensor_scalar(oh_t[0:1, 0:8], oh_sb[0:1, 0:8],
                                    sc1(SC_T0), None, ALU.mult),
          reads=[r_oh, r_scal], writes=[r_oht])
        DMA("gpsimd", lambda e: e.dma_start(out=d_cc2i[0:1, 0:8], in_=oh_t[0:1, 0:8]),
            reads=[r_oht], writes=[r_cc2i], sem="st_cc")
        op("gpsimd", lambda e: e.collective_compute(
            "AllReduce", ALU.add, replica_groups=[list(range(n_cores))],
            ins=[d_cc2i.ap()], outs=[d_cc2o.ap()]),
           reads=[r_cc2i], writes=[r_cc2o], sem="cc2", dinc=1)

        cross_part_max(r_mxw2, mx_w2, SC_SW2C)
        V(lambda e: e.tensor_scalar(sc1(SC_SW2V), sc1(SC_SW2C), MIN_RANGE, INV_QP,
                                    ALU.max, ALU.mult),
          writes=[r_scal])
        V(lambda e: e.reciprocal(sc1(SC_T0), sc1(SC_SW2V)), writes=[r_scal])
        bcast_scalar(SC_T0, SC_SWQINV)

        def build_w2q(f):
            ws = f % 2
            DMA("sync", lambda e, f=f, ws=ws: e.dma_start(
                out=wbst[ws][:, :], in_=d_w2t[f * 128:(f + 1) * 128, :]),
                writes=[r_wbst[ws]], sem=f"ld_wb{ws}")
            V(lambda e, ws=ws: e.tensor_scalar(
                zbld[:, :], wbst[ws][:, :], sc_col(SC_SWQINV), MAGIC,
                ALU.mult, ALU.add),
              reads=[r_wbst[ws], r_scal], writes=list(r_zbld))
            wr = [r_w2q[f]] + (r_w1q if f == 0 else [])
            V(lambda e, f=f: e.tensor_scalar_add(w2qv(f)[:, :], zbld[:, :], -MAGIC),
              reads=list(r_zbld), writes=wr)

        def c_quant(blk, f):
            hs = f % 4
            zs = f % 2
            extra = ([r_bts] + r_btsh if (blk == 0 and f < 4) else [])
            tile = blk * 16 + f // 2
            leng = "sync" if hs % 2 == 0 else "gpsimd"
            op(leng, lambda e, tile=tile, f=f, hs=hs: e.dma_start(
                out=hrel_v[hs],
                in_=d_h[tile * 128:(tile + 1) * 128,
                        (f % 2) * 512:(f % 2 + 1) * 512]),
               reads=[r_hd[(f // 2) % 2]], writes=[r_hrel[hs]] + extra,
               sem=f"ld_h{hs}", dinc=16)
            Sc(lambda e, hs=hs, zs=zs: e.activation(
                zq_t[zs][:, :], hrel_v[hs], AF.Copy,
                bias=MAGIC, scale=sc_col(SC_SHINV)),
               reads=[r_hrel[hs], r_scal], writes=[r_zq[zs]])
            V(lambda e, hs=hs, zs=zs: e.tensor_scalar_add(
                hq_t[hs][:, :], zq_t[zs][:, :], -MAGIC),
              reads=[r_zq[zs]], writes=[r_hq[hs]])

        def c_loads(blk):
            DMA("scalar", lambda e, b=blk: e.dma_start(
                out=btx[:, :], in_=d_xfm[b * 128:(b + 1) * 128, :]),
                writes=[r_btx], sem="ld_x0")
            DMA("scalar", lambda e, b=blk: e.dma_start(
                out=bb[1][:, :], in_=d_q1[b * 128:(b + 1) * 128, :]),
                reads=[r_q1d],
                writes=[r_bb[1]] + [res(f"bb1_{i}") for i in range(DC)], sem="ld_q1")

        def c_attn(blk):
            bat = blk // NB2
            for c in range(8):
                t = bat * 8 + c
                Tn(lambda e, t=t, c=c: e.matmul(
                    P[0][0:16, 0:512], denblk[:, t * 16:(t + 1) * 16],
                    bb[1][:, c * 512:(c + 1) * 512],
                    start=(c == 0), stop=(c == 7)),
                   reads=[r_denblk, r_bb[1]],
                   writes=[r_P[0]] if c == 0 else [],
                   wnowait=[r_P[0]] if c == 7 else [])
            Sc(lambda e: e.activation(den_sb[0:16, 0:512], P[0][0:16, 0:512],
                                      AF.Ln, bias=scal[0:16, SC_DEPS:SC_DEPS + 1]),
               reads=[r_P[0], r_scal], writes=[r_densb])
            Sc(lambda e: e.activation(rec_bf[0:16, 0:512], den_sb[0:16, 0:512],
                                      AF.Exp, scale=-1.0),
               reads=[r_densb], writes=[r_recbf])
            def qkv_mm(c):
                t = bat * 8 + c
                pqb, r_pqb = P[1 + c % 4], r_P[1 + c % 4]
                Tn(lambda e, t=t, c=c, pqb=pqb: e.matmul(
                    pqb[:, 0:512], kvblk[:, t * 128:(t + 1) * 128],
                    bb[1][:, c * 512:(c + 1) * 512], start=True, stop=True),
                   reads=[r_kvblk, r_bb[1]], writes=[r_pqb])
            for c in range(4):
                qkv_mm(c)
            rslot = [recb_v[0], recb_v[1], xn_t[0][:, :], xn_t[1][:, :]]
            r_rslot = [r_recb[0], r_recb[1], r_xn[0], r_xn[1]]
            rbank = [5, 6, 7, 0]

            def recb_bcast(c):
                pr, r_pr = P[rbank[c % 4]], r_P[rbank[c % 4]]
                Tn(lambda e, c=c, pr=pr: e.matmul(
                    pr[:, 0:512], maskc[:, c * 128:(c + 1) * 128],
                    rec_bf[0:16, 0:512], start=True, stop=True),
                   reads=[r_maskc, r_recbf], writes=[r_pr])
            for c in range(4):
                recb_bcast(c)
            for c in range(8):
                pqb, r_pqb = P[1 + c % 4], r_P[1 + c % 4]
                pr, r_pr = P[rbank[c % 4]], r_P[rbank[c % 4]]
                cs = c % 4
                Sc(lambda e, pr=pr, cs=cs: e.activation(
                    rslot[cs], pr[:, 0:512], AF.Copy),
                   reads=[r_pr],
                   writes=[r_rslot[cs]])
                V(lambda e, pqb=pqb, cs=cs, c=c: e.tensor_tensor(
                    bb[2][:, c * 512:(c + 1) * 512], pqb[:, 0:512],
                    rslot[cs], ALU.mult),
                  reads=[r_pqb, r_rslot[cs]], writes=[r_bb[2]])
                if c + 4 < 8:
                    qkv_mm(c + 4)
                    recb_bcast(c + 4)
            if blk > 0:
                for f in range(4):
                    c_quant(blk, f)

            def wout_group(do):
                pw, r_pw = P[5 + do % 2], r_P[5 + do % 2]
                for di in range(DC):
                    Tn(lambda e, do=do, di=di, pw=pw: e.matmul(
                        pw[:, 0:512],
                        woutbf[:, di * 1024 + do * 128: di * 1024 + (do + 1) * 128],
                        bb[2][:, di * 512:(di + 1) * 512],
                        start=(di == 0), stop=(di == DC - 1)),
                       reads=[r_woutbf, r_bb[2]],
                       writes=[r_pw] if di == 0 else [],
                       wnowait=[r_pw] if di == DC - 1 else [])

            stats_rows2(blk, 4, P[0], r_P[0])
            wout_group(0)
            wout_group(1)
            Tn(lambda e: e.matmul(P[1][:, 0:512], sel2[:, 128:256], rows2[0:2, 0:512],
                                  start=True, stop=True),
               reads=[r_sel2, r_rows2], writes=[r_P[1]])
            Tn(lambda e: e.matmul(P[2][:, 0:512], sel2[:, 0:128], rows2[0:2, 0:512],
                                  start=True, stop=True),
               reads=[r_sel2, r_rows2], writes=[r_P[2]])
            for do in range(DC):
                V(lambda e, do=do: e.tensor_tensor(
                    bto[:, do * 512:(do + 1) * 512],
                    btx[:, do * 512:(do + 1) * 512], P[1][:, 0:512], ALU.mult),
                  reads=[r_btx, r_P[1]], writes=[r_bto])
                V(lambda e, do=do: e.tensor_tensor(
                    bto[:, do * 512:(do + 1) * 512],
                    bto[:, do * 512:(do + 1) * 512], P[2][:, 0:512], ALU.subtract),
                  reads=[r_P[2]], writes=[r_bto])
                pw, r_pw = P[5 + do % 2], r_P[5 + do % 2]
                V(lambda e, do=do, pw=pw: e.tensor_tensor(
                    bto[:, do * 512:(do + 1) * 512], pw[:, 0:512],
                    bto[:, do * 512:(do + 1) * 512], ALU.add),
                  reads=[r_pw], writes=[r_bto])
                if do + 2 < DC:
                    wout_group(do + 2)

        def c_ffn2(blk):
            for f in range(HC):
                if blk == 0:
                    build_w2q(f)
                hs = f % 4
                if blk == 0 or f >= 4:
                    c_quant(blk, f)
                for do in range(DC):
                    Tn(lambda e, f=f, do=do, hs=hs: e.matmul(
                        P[do][:, 0:512], w2qv(f)[:, do * 128:(do + 1) * 128],
                        hq_t[hs][:, :], start=(f == 0), stop=(f == HC - 1)),
                       reads=[r_w2q[f], r_hq[hs]],
                       writes=[r_P[do]] if f == 0 else [],
                       wnowait=[r_P[do]] if f == HC - 1 else [])
                if blk + 1 < NBLK and f == 24:
                    c_loads(blk + 1)
            for do in range(DC):
                ds = do % 2
                Sc(lambda e, do=do, ds=ds: e.activation(
                    et_v[ds], P[do][:, 0:512], AF.Identity,
                    bias=b2ss[:, do:do + 1], scale=sc_col(SC_SS2)),
                   reads=[r_P[do], r_b2ss, r_scal],
                   writes=[r_et[ds]])
                V(lambda e, do=do, ds=ds: e.tensor_tensor(
                    bto[:, do * 512:(do + 1) * 512], et_v[ds],
                    bto[:, do * 512:(do + 1) * 512], ALU.add),
                  reads=[r_et[ds]], writes=[r_bto])
            DMA("scalar", lambda e, b=blk: e.dma_start(
                out=d_out[b * 128:(b + 1) * 128, :], in_=bto[:, :]),
                reads=[r_bto], writes=[r_out], sem="st_out")

        c_loads(0)
        c_attn(0)

        DMA("sync", lambda e: e.dma_start(out=gmx8[0:1, 0:8], in_=d_cc2o[0:1, 0:8]),
            reads=[r_cc2o], writes=[r_gmx8], sem="ld_cc")
        V(lambda e: e.tensor_reduce(sc1(SC_GHC), gmx8[0:1, 0:8], AX.X, ALU.max),
          reads=[r_gmx8], writes=[r_scal])
        V(lambda e: e.tensor_scalar(sc1(SC_SHV), sc1(SC_GHC), MIN_RANGE, INV_QP,
                                    ALU.max, ALU.mult),
          writes=[r_scal])
        V(lambda e: e.reciprocal(sc1(SC_T0), sc1(SC_SHV)), writes=[r_scal])
        V(lambda e: e.tensor_tensor(sc1(SC_T1), sc1(SC_SW2V), sc1(SC_SHV), ALU.mult),
          writes=[r_scal])
        bcast_scalar(SC_T0, SC_SHINV)
        bcast_scalar(SC_T1, SC_SS2)
        V(lambda e: e.tensor_scalar(b2ss[:, :], b2_pp[:, :], sc_col(SC_SS2), None, ALU.mult),
          reads=[r_b2, r_scal], writes=[r_b2ss])

        c_ffn2(0)
        for blk in range(1, NBLK):
            c_attn(blk)
            c_ffn2(blk)

        ctx.raw_wait("gpsimd", "st_out", 16 * NBLK)

    with nc.Block() as block:

        @block.sync
        def _(sync):
            schedule(_Ctx("sync", sync, fresh_sems()))

        @block.gpsimd
        def _(gpsimd):
            schedule(_Ctx("gpsimd", gpsimd, fresh_sems()))

        @block.vector
        def _(vector):
            schedule(_Ctx("vector", vector, fresh_sems()))

        @block.scalar
        def _(scalar):
            schedule(_Ctx("scalar", scalar, fresh_sems()))

        @block.tensor
        def _(tensor):
            schedule(_Ctx("tensor", tensor, fresh_sems()))

    es.close()
    return nc


_NC_CACHE = {}


def _get_program(T_core, n_cores):
    key = (T_core, n_cores)
    if key not in _NC_CACHE:
        _NC_CACHE[key] = build_program(T_core, n_cores)
    return _NC_CACHE[key]


def kernel(L0_attn_x, L0_attn_gamma, L0_attn_beta, L0_attn_Wqkv, L0_attn_Wout,
           L0_ffn_w1, L0_ffn_b1, L0_ffn_w2, L0_ffn_b2):
    x = np.asarray(L0_attn_x, dtype=np.float32)
    B, L, D_ = x.shape
    n = 8
    Lc = L // n
    T_core = B * Lc

    nc = _get_program(T_core, n)

    wqkv = np.ascontiguousarray(np.asarray(L0_attn_Wqkv, dtype=np.float32))
    wout = np.ascontiguousarray(np.asarray(L0_attn_Wout, dtype=np.float32))
    w1 = np.asarray(L0_ffn_w1, dtype=np.float32)
    w1b = np.ascontiguousarray(
        w1.reshape(HC, 128, DC, 128).transpose(0, 3, 2, 1).reshape(HF, D))
    w2t = np.ascontiguousarray(np.asarray(L0_ffn_w2, dtype=np.float32).T)
    gam = np.ascontiguousarray(np.asarray(L0_attn_gamma, dtype=np.float32).reshape(DC, 128).T)
    beta_pp = np.ascontiguousarray(np.asarray(L0_attn_beta, dtype=np.float32).reshape(DC, 128).T)
    b1_pp = np.ascontiguousarray(np.asarray(L0_ffn_b1, dtype=np.float32).reshape(HC, 128).T)
    b2_pp = np.ascontiguousarray(np.asarray(L0_ffn_b2, dtype=np.float32).reshape(DC, 128).T)
    ident = np.eye(128, dtype=np.float32)
    sel2c = np.zeros((2, 256), np.float32)
    sel2c[0, 0:128] = 1.0
    sel2c[1, 128:256] = 1.0
    maskc_f = np.zeros((16, 1024), np.float32)
    for cc in range(8):
        maskc_f[2 * cc, cc * 128:cc * 128 + 64] = 1.0
        maskc_f[2 * cc + 1, cc * 128 + 64:(cc + 1) * 128] = 1.0

    in_maps = []
    nbt = T_core // 512
    for c in range(n):
        xs = x[:, c * Lc:(c + 1) * Lc, :]
        x_tm0 = xs.reshape(T_core, D_)
        x_fm = np.ascontiguousarray(
            x_tm0.reshape(nbt, 512, 8, 128).transpose(0, 3, 2, 1).reshape(nbt * 128, 4096))
        x_tm = np.ascontiguousarray(
            x_tm0.reshape(nbt, 4, 128, 1024).transpose(0, 2, 1, 3).reshape(nbt * 128, 4096))
        oh = np.zeros((1, 8), np.float32)
        oh[0, c] = 1.0
        in_maps.append(dict(x_fm=x_fm, x_tm=x_tm, wqkv=wqkv, wout=wout,
                            w1b=w1b, w2t=w2t, gam_pp=gam, beta_pp=beta_pp,
                            b1_pp=b1_pp, b2_pp=b2_pp, ident=ident, onehot=oh,
                            sel2c=sel2c, maskc_f=maskc_f))

    res = run_bass_kernel_spmd(nc, in_maps, core_ids=list(range(n)))
    parts = []
    for r in res.results:
        arr = np.asarray(r["out_fm"])
        out_core = arr.reshape(nbt, 128, 8, 512).transpose(0, 3, 2, 1).reshape(T_core, D_)
        parts.append(out_core.reshape(B, Lc, D_))
    return np.ascontiguousarray(np.concatenate(parts, axis=1)).astype(np.float32)


# version 34
# speedup vs baseline: 1.0442x; 1.0166x over previous
import contextlib
import numpy as np

import concourse.bass as bass
import concourse.mybir as mybir
from concourse.bass_utils import run_bass_kernel_spmd

F32 = mybir.dt.float32
BF16 = mybir.dt.bfloat16
AF = mybir.ActivationFunctionType
ALU = mybir.AluOpType
AX = mybir.AxisListType

MAGIC = 12582912.0
LN_EPS = 1e-5
DEN_EPS = 1e-6
MIN_RANGE = 1e-6
INV_QP = float(np.float32(1.0) / np.float32(127.0))

D = 1024
DC = 8
HF = 4096
HC = 32
TB = 512


class _Sem:
    __slots__ = ("name", "handle", "count")

    def __init__(self, name, handle):
        self.name, self.handle, self.count = name, handle, 0


class _Res:
    __slots__ = ("name", "w", "rs")

    def __init__(self, name):
        self.name, self.w, self.rs = name, None, []


class _Ctx:

    def __init__(self, active, engobj, sems):
        self.active = active
        self.engobj = engobj
        self.sems = sems
        self.marks = {}

    def _wait(self, engname, tok):
        if tok is None:
            return
        semname, val = tok
        m = self.marks.setdefault(engname, {})
        if m.get(semname, 0) >= val:
            return
        m[semname] = val
        if self.active == engname:
            self.engobj.wait_ge(self.sems[semname].handle, val)

    def op(self, engname, emit, reads=(), writes=(), wnowait=(), sem=None, dinc=1,
           track=True):
        for r in reads:
            self._wait(engname, r.w)
        for w in writes:
            self._wait(engname, w.w)
            for t in w.rs:
                self._wait(engname, t)
        s = self.sems[sem if sem is not None else ("s_" + engname)]
        inst = emit(self.engobj) if self.active == engname else None
        s.count += dinc
        if inst is not None:
            inst.then_inc(s.handle, dinc)
        if not track:
            return None
        tok = (s.name, s.count)
        for r in reads:
            r.rs.append(tok)
        for w in writes:
            w.w, w.rs = tok, []
        for w in wnowait:
            w.w, w.rs = tok, []
        return tok

    def raw_wait(self, engname, semname, val):
        self._wait(engname, (semname, val))


def build_program(T_core, n_cores):
    NBLK = T_core // TB
    NB2 = NBLK // 2
    assert NBLK % 2 == 0 and NB2 >= 1

    nc = bass.Bass("TRN2", target_bir_lowering=False)

    NBT = T_core // 512
    d_xfm = nc.dram_tensor("x_fm", [NBT * 128, 4096], F32, kind="ExternalInput")
    d_xtm = nc.dram_tensor("x_tm", [NBT * 128, 4096], F32, kind="ExternalInput")
    d_wqkv = nc.dram_tensor("wqkv", [D, 3 * D], F32, kind="ExternalInput")
    d_wout = nc.dram_tensor("wout", [D, D], F32, kind="ExternalInput")
    d_w1b = nc.dram_tensor("w1b", [HF, D], F32, kind="ExternalInput")
    d_w2t = nc.dram_tensor("w2t", [HF, D], F32, kind="ExternalInput")
    d_gam = nc.dram_tensor("gam_pp", [128, DC], F32, kind="ExternalInput")
    d_beta = nc.dram_tensor("beta_pp", [128, DC], F32, kind="ExternalInput")
    d_b1 = nc.dram_tensor("b1_pp", [128, HC], F32, kind="ExternalInput")
    d_b2 = nc.dram_tensor("b2_pp", [128, DC], F32, kind="ExternalInput")
    d_ident = nc.dram_tensor("ident", [128, 128], F32, kind="ExternalInput")
    d_oh = nc.dram_tensor("onehot", [1, 8], F32, kind="ExternalInput")
    d_sel2 = nc.dram_tensor("sel2c", [2, 256], F32, kind="ExternalInput")
    d_mask = nc.dram_tensor("maskc_f", [16, 1024], F32, kind="ExternalInput")
    d_out = nc.dram_tensor("out_fm", [NBT * 128, 4096], F32, kind="ExternalOutput")

    d_q1 = nc.dram_tensor("q1_spill", [NBT * 128, 4096], BF16)
    d_h = nc.dram_tensor("h_spill", [(T_core // 512) * 16 * 128, 1024], F32)
    d_cc1i = nc.dram_tensor("cc1_in", [64, 2080], F32)
    d_cc1o = nc.dram_tensor("cc1_out", [64, 2080], F32)
    d_cc1ai = nc.dram_tensor("cc1a_in", [1, 8], F32)
    d_cc1ao = nc.dram_tensor("cc1a_out", [1, 8], F32)
    d_cc2i = nc.dram_tensor("cc2_in", [1, 8], F32)
    d_cc2o = nc.dram_tensor("cc2_out", [1, 8], F32)

    A = nc.alloc_sbuf_tensor
    btx = A("btx", [128, 4096], F32)
    bts = A("bts", [128, 4096], F32)
    bto = A("bto", [128, 4096], F32)
    bb = [A(f"bb{i}", [128, 4096], BF16) for i in range(3)]
    arena = A("arena_w", [128, 32768], BF16)
    woutbf = A("wout_bf", [128, 8192], BF16)
    wbst = [A(f"wbst{i}", [128, 1024], F32) for i in range(2)]
    xn_t = [A(f"xn{i}", [128, 512], F32) for i in range(2)]
    m_t = [A("m0", [128, 1024], F32)] * 2
    e_t = [A("e0", [128, 1024], F32)] * 2
    hq_t = [A(f"hq{i}", [128, 512], BF16) for i in range(4)]
    k1_tm = [A(f"k1tm{i}", [128, 1024], BF16) for i in range(2)]
    v65 = [A(f"v65_{i}", [128, 1040], BF16) for i in range(2)]
    kvacc = A("kvacc", [64, 2080], F32)
    kvblk = A("kvblk", [128, 2048], BF16)
    denblk = A("denblk", [128, 256], BF16)
    maskc = A("maskc", [16, 1024], BF16)
    sel2 = A("sel2", [2, 256], F32)
    gam_pp = A("gam_pp_sb", [128, DC], F32)
    ones_row = None
    beta_pp = A("beta_pp_sb", [128, DC], F32)
    b1_pp = A("b1_pp_sb", [128, HC], F32)
    b2_pp = A("b2_pp_sb", [128, DC], F32)
    b1ss = A("b1ss", [128, HC], F32)
    b2ss = A("b2ss", [128, DC], F32)
    ident_sb = A("ident_sb", [128, 128], F32)
    statp = A("statp", [128, NBLK * 4 * 6], F32)
    bnst = A("bnst", [128, 12], F32)
    agg8 = A("agg8", [128, 8], F32)
    tmp4 = A("tmp4", [128, 8], F32)
    rows2 = A("rows2", [2, 512], F32)
    den_sb = A("den_sb", [16, 512], F32)
    rec_bf = A("rec_bf", [16, 512], BF16)
    kvstage = A("kvstage", [128, 128], F32)
    denstage = A("denstage", [128, 16], F32)
    mx_xn = A("mx_xn", [128, 1], F32)
    mx_h = A("mx_h", [128, 1], F32)
    mx_w1 = A("mx_w1", [128, 1], F32)
    mx_w2 = A("mx_w2", [128, 1], F32)
    tred = A("tred", [128, 1], F32)
    scal = A("scal", [128, 24], F32)
    oh_t = A("oh_t", [1, 8], F32)

    SC_SXINV, SC_SWQINV, SC_SS1, SC_SHINV, SC_SS2 = 0, 1, 2, 3, 4
    SC_GXC, SC_SXV = 8, 9
    SC_SW1C, SC_SW1V, SC_SW1I = 10, 11, 12
    SC_GHC, SC_SHV = 13, 14
    SC_SW2C, SC_SW2V = 15, 16
    SC_T0, SC_T1, SC_DEPS, SC_EPS = 17, 18, 22, 23

    zbld = m_t[0]
    gam_row = None
    r2row = den_sb
    r2rowbf = rec_bf

    zq_t = [e_t[0][:, 0:512], e_t[0][:, 512:1024]]
    recb_v = [m_t[1][:, 0:512], m_t[1][:, 512:1024]]
    et_v = [e_t[1][:, 0:512], e_t[1][:, 512:1024]]

    gmx8 = oh_t
    ones_row = sel2
    oh_sb = bnst[0:1, 0:8]

    P = [nc.alloc_psum_tensor(f"P{i}", [128, 512], F32) for i in range(8)]

    hrel_v = [bts[:, i * 512:(i + 1) * 512] for i in range(4)]

    es = contextlib.ExitStack()
    semnames = (["s_sync", "s_gpsimd", "s_vector", "s_scalar", "s_tensor",
                 "cc1", "cc1a", "cc2"] +
                ["ld_x0", "ld_x1", "ld_t", "ld_misc", "ld_wq", "ld_h0",
                 "ld_h1", "ld_h2", "ld_h3", "ld_q1", "ld_wb0", "ld_wb1", "ld_kvb", "ld_kvm", "ld_wm",
                 "ld_cc2",
                 "ld_cc"] +
                ["st_q0", "st_q1", "st_h0", "st_h1", "st_out", "st_cc"])
    handles = {n: es.enter_context(nc.semaphore(n)) for n in semnames}

    def fresh_sems():
        return {n: _Sem(n, handles[n]) for n in semnames}

    def wqkv_bf(d):
        return arena[:, d * 3072:(d + 1) * 3072]

    def w1qv(d):
        return arena[:, d * 4096:(d + 1) * 4096]

    def w2qv(f):
        return arena[:, f * 1024:(f + 1) * 1024]

    def schedule(ctx):
        op = ctx.op
        R = {}

        def res(name):
            if name not in R:
                R[name] = _Res(name)
            return R[name]

        r_btx = res("btx")
        r_bts = res("bts")
        r_bto = res("bto")
        r_btsh = [res(f"btsh{i}") for i in range(4)]
        r_hrel = [res(f"hrel{i}") for i in range(4)]
        r_bb = [res(f"bb{i}") for i in range(3)]
        r_wq = [res(f"wq{d}") for d in range(DC)]
        r_w1q = [res(f"w1q{fo}") for fo in range(HC)]
        r_w2q = [res(f"w2q{f}") for f in range(HC)]
        r_woutbf = res("woutbf")
        r_wbst = [res(f"wbst{i}") for i in range(2)]
        r_zbld = [res("m0h0"), res("m0h1")]
        r_xn = [res(f"xn{i}") for i in range(2)]
        r_zq = [res("e0h0"), res("e0h1")]
        r_m0 = [res("m0h0"), res("m0h1")]
        r_m1 = r_m0
        r_e0 = [res("e0h0"), res("e0h1")]
        r_e1 = r_e0
        r_hq = [res(f"hq{i}") for i in range(4)]
        r_et = [res("e0h0"), res("e0h1")]
        r_k1 = [res(f"k1tm{i}") for i in range(2)]
        r_v65 = [res(f"v65_{i}") for i in range(2)]
        r_kvacc = res("kvacc")
        r_kvblk = res("kvblk")
        r_denblk = res("denblk")
        r_maskc = res("maskc")
        r_sel2 = res("sel2")
        r_ones = res("sel2")
        r_gampp = res("gam_pp_sb")
        r_beta = res("beta_pp")
        r_b1 = res("b1_pp")
        r_b2 = res("b2_pp")
        r_b1ss = res("b1ss")
        r_b2ss = res("b2ss")
        r_ident = res("ident_sb")
        r_statp = res("statp")
        r_bnst = res("bnst")
        r_agg8 = res("agg8")
        r_tmp4 = res("tmp4")
        r_rows2 = res("rows2")
        r_r2row = res("den_sb")
        r_r2rowbf = res("rec_bf")
        r_densb = res("den_sb")
        r_recbf = res("rec_bf")
        r_recb = [res("m0h0"), res("m0h1")]
        r_kvstage = res("kvstage")
        r_denstage = res("denstage")
        r_mxxn = res("mx_xn")
        r_mxh = res("mx_h")
        r_mxw1 = res("mx_w1")
        r_mxw2 = res("mx_w2")
        r_tred = res("tred")
        r_epsp = res("scal")
        r_scal = res("scal")
        r_oh = res("bnst")
        r_oht = res("oh_t")
        r_gmx8 = res("oh_t")
        r_P = [res(f"P{i}") for i in range(8)]
        r_q1d = res("q1_dram")
        r_hd = [res("h_dram0"), res("h_dram1")]
        r_cc1i = res("cc1i")
        r_cc1o = res("cc1o")
        r_cc1ai = res("cc1ai")
        r_cc1ao = res("cc1ao")
        r_cc2i = res("cc2i")
        r_cc2o = res("cc2o")
        r_out = res("out_fm")

        def V(emit, reads=(), writes=(), wnowait=()):
            return op("vector", emit, reads, writes, wnowait)

        def Sc(emit, reads=(), writes=(), wnowait=()):
            return op("scalar", emit, reads, writes, wnowait)

        def G(emit, reads=(), writes=(), wnowait=()):
            return op("gpsimd", emit, reads, writes, wnowait)

        def Tn(emit, reads=(), writes=(), wnowait=()):
            return op("tensor", emit, reads, writes, wnowait)

        def DMA(eng, emit, reads=(), writes=(), sem=None):
            return op(eng, emit, reads, writes, sem=sem, dinc=16)

        def sc_col(i):
            return scal[:, i:i + 1]

        def sc1(i):
            return scal[0:1, i:i + 1]

        def stats_rows(blk, c0, dst_rows, r_dst):
            for s in range(4):
                base = (blk * 4 + s) * 6
                Tn(lambda e, b=base, s=s: e.transpose(
                    P[6][0:2, s * 128:(s + 1) * 128],
                    statp[:, b + c0: b + c0 + 2], ident_sb[:, :]),
                   reads=[r_statp, r_ident],
                   writes=[r_P[6]] if s == 0 else [],
                   wnowait=[r_P[6]] if s == 3 else [])
            V(lambda e: e.tensor_copy(dst_rows[0:2, 0:512], P[6][0:2, 0:512]),
              reads=[r_P[6]], writes=[r_dst])

        def cross_part_max(r_acc, acc_t, out_col):
            Tn(lambda e: e.transpose(P[7][0:1, 0:128], acc_t[:, :], ident_sb[:, :]),
               reads=[r_acc, r_ident], writes=[r_P[7]])
            V(lambda e, oc=out_col: e.tensor_reduce(
                scal[0:1, oc:oc + 1], P[7][0:1, 0:128], AX.X, ALU.max),
              reads=[r_P[7]], writes=[r_scal])

        def bcast_scalar(src_col, dst_col):
            Tn(lambda e: e.matmul(P[7][:, 0:1], sel2[0:1, 0:128], sc1(src_col),
                                  start=True, stop=True),
               reads=[r_ones, r_scal], writes=[r_P[7]])
            V(lambda e: e.tensor_copy(sc_col(dst_col), P[7][:, 0:1]),
              reads=[r_P[7]], writes=[r_scal])

        const_dsts = [(r_gampp, gam_pp, d_gam), (r_beta, beta_pp, d_beta),
                      (r_b1, b1_pp, d_b1), (r_b2, b2_pp, d_b2),
                      (r_ident, ident_sb, d_ident),
            (r_sel2, sel2, d_sel2)]
        for (dst, dsth, src) in const_dsts:
            DMA("sync", lambda e, dh=dsth, s=src: e.dma_start(
                out=dh[:, :] if hasattr(dh, "ap") else dh, in_=s[:, :]),
                writes=[dst], sem="ld_misc")

        V(lambda e: e.memset(scal[:, SC_EPS:SC_EPS + 1], LN_EPS), writes=[r_epsp])
        V(lambda e: e.memset(scal[:, SC_DEPS:SC_DEPS + 1], DEN_EPS), writes=[r_epsp])
        DMA("sync", lambda e: e.dma_start(out=bto[0:16, 1024:2048], in_=d_mask[:, :]),
            writes=[r_bto], sem="ld_misc")
        _misc_total = ctx.sems["ld_misc"].count
        for (dst, _, _) in const_dsts:
            dst.w = ("ld_misc", _misc_total)
        r_bto.w = ("ld_misc", _misc_total)
        V(lambda e: e.tensor_copy(maskc[0:16, 0:1024], bto[0:16, 1024:2048]),
          reads=[r_bto], writes=[r_maskc])
        for mres, mt in [(r_mxxn, mx_xn), (r_mxh, mx_h), (r_mxw1, mx_w1), (r_mxw2, mx_w2)]:
            V(lambda e, m=mt: e.memset(m[:, :], 0.0), writes=[mres])
        V(lambda e: e.memset(kvacc[:, :], 0.0), writes=[r_kvacc])
        for s in range(2):
            V(lambda e, s=s: e.memset(
                v65[s][:, :].rearrange("p (h v) -> p h v", v=65)[:, :, 64:65], 1.0),
              writes=[r_v65[s]])

        for d in range(DC):
            slot = bto if d % 2 == 0 else bts
            r_slot = [r_bto] if d % 2 == 0 else [r_bts] + r_btsh
            DMA("sync", lambda e, d=d, slot=slot: e.dma_start(
                out=slot[:, 0:3072], in_=d_wqkv[d * 128:(d + 1) * 128, :]),
                writes=list(r_slot), sem="ld_wq" if d % 2 == 0 else "ld_kvm")
            V(lambda e, d=d, slot=slot: e.tensor_copy(wqkv_bf(d)[:, :], slot[:, 0:3072]),
              reads=list(r_slot), writes=[r_wq[d]])


        def stats_rows2(blk, c0, pbank, r_pbank):
            for s in range(4):
                base = (blk * 4 + s) * 6
                Tn(lambda e, b=base, s=s, pb=pbank: e.transpose(
                    pb[0:2, s * 128:(s + 1) * 128],
                    statp[:, b + c0: b + c0 + 2], ident_sb[:, :]),
                   reads=[r_statp, r_ident],
                   writes=[r_pbank] if s == 0 else [],
                   wnowait=[r_pbank] if s == 3 else [])
            V(lambda e, pb=pbank: e.tensor_copy(rows2[0:2, 0:512], pb[0:2, 0:512]),
              reads=[r_pbank], writes=[r_rows2])

        def emit_stats(blk):
            pieces = []

            def bn_piece(slo, shi):
                def f():
                    for s in range(slo, shi):
                        xv = bts[:, s * 1024:(s + 1) * 1024]
                        V(lambda e, xv=xv: e.bn_stats(bnst[:, 0:6], xv[:, 0:512]),
                          reads=[r_bts], writes=[r_bnst])
                        V(lambda e, xv=xv: e.bn_stats(bnst[:, 6:12], xv[:, 512:1024]),
                          reads=[r_bts], writes=[r_bnst])
                        V(lambda e, s=s: e.bn_aggr(
                            agg8[:, :].rearrange("p (k s) -> p k s", k=2)[:, :, s:s + 1],
                            bnst[:, 0:12]),
                          reads=[r_bnst], writes=[r_agg8])
                return f
            pieces.append(bn_piece(0, 2))
            pieces.append(bn_piece(2, 4))

            def chain():
                bb0 = blk * 24
                MU = agg8[:, 0:4]
                VAR = agg8[:, 4:8]

                def spv(c):
                    return statp[:, bb0:bb0 + 24].rearrange(
                        "p (s c) -> p c s", c=6)[:, c:c + 1, :]
                Sc(lambda e: e.activation(tmp4[:, 0:4], VAR, AF.Ln,
                                          bias=scal[:, SC_EPS:SC_EPS + 1]),
                   reads=[r_agg8, r_scal], writes=[r_tmp4])
                Sc(lambda e: e.activation(spv(1), tmp4[:, 0:4], AF.Exp, scale=-0.5),
                   reads=[r_tmp4], writes=[r_statp])
                V(lambda e: e.tensor_tensor(tmp4[:, 4:8], VAR, spv(1), ALU.mult),
                  reads=[r_agg8, r_statp], writes=[r_tmp4])
                V(lambda e: e.tensor_tensor(tmp4[:, 4:8], tmp4[:, 4:8], spv(1), ALU.mult),
                  reads=[r_statp], writes=[r_tmp4])
                Sc(lambda e: e.activation(tmp4[:, 0:4], tmp4[:, 4:8], AF.Ln,
                                          bias=scal[:, SC_EPS:SC_EPS + 1]),
                   reads=[r_tmp4, r_scal], writes=[r_tmp4])
                Sc(lambda e: e.activation(tmp4[:, 4:8], tmp4[:, 0:4], AF.Exp, scale=-0.5),
                   reads=[r_tmp4], writes=[r_tmp4])
                V(lambda e: e.tensor_tensor(spv(2), spv(1), tmp4[:, 4:8], ALU.mult),
                  reads=[r_tmp4], writes=[r_statp])
                V(lambda e: e.tensor_tensor(spv(3), MU, spv(2), ALU.mult),
                  reads=[r_agg8], writes=[r_statp])
                V(lambda e: e.tensor_tensor(spv(4), MU, spv(1), ALU.mult),
                  reads=[r_agg8], writes=[r_statp])
                V(lambda e: e.tensor_scalar_add(spv(5), spv(1), 1.0),
                  writes=[r_statp])
                V(lambda e: e.tensor_copy(spv(0), MU),
                  reads=[r_agg8], writes=[r_statp])
            pieces.append(chain)

            def maxdev(slo, shi):
                def f():
                    bb0 = blk * 24
                    for s in range(slo, shi):
                        xv = bts[:, s * 1024:(s + 1) * 1024]
                        V(lambda e, xv=xv, s=s: e.tensor_scalar(
                            m_t[0][:, :], xv, agg8[:, s:s + 1], None, ALU.subtract),
                          reads=[r_bts, r_agg8], writes=list(r_m0))
                        V(lambda e: e.tensor_reduce(
                            tred[:, :], m_t[0][:, :], AX.X, ALU.max,
                            apply_absolute_value=True),
                          reads=list(r_m0), writes=[r_tred])
                        V(lambda e, s=s, bb0=bb0: e.tensor_tensor(
                            tred[:, :], tred[:, :],
                            statp[:, bb0 + s * 6 + 1:bb0 + s * 6 + 2], ALU.mult),
                          reads=[r_statp], writes=[r_tred])
                        V(lambda e: e.tensor_tensor(
                            mx_xn[:, :], mx_xn[:, :], tred[:, :], ALU.max),
                          reads=[r_tred], writes=[r_mxxn])
                return f
            pieces.append(maxdev(0, 2))
            pieces.append(maxdev(2, 4))
            return pieces

        def a_prep_pieces(blk):
            pieces = []

            def loads():
                DMA("sync", lambda e, b=blk: e.dma_start(
                    out=btx[:, :], in_=d_xfm[b * 128:(b + 1) * 128, :]),
                    writes=[r_btx], sem="ld_x0")
                DMA("sync", lambda e, b=blk: e.dma_start(
                    out=bts[:, :], in_=d_xtm[b * 128:(b + 1) * 128, :]),
                    writes=[r_bts] + r_btsh, sem="ld_t")
            pieces.append(("early", loads))
            pieces.extend(("early", p) for p in emit_stats(blk))

            def rows_t():
                stats_rows2(blk, 2, P[4], r_P[4])
            pieces.append(("latev", rows_t))

            def rows_mm():
                Tn(lambda e: e.matmul(P[4][:, 0:512], sel2[:, 0:128],
                                      rows2[0:2, 0:512], start=True, stop=True),
                   reads=[r_sel2, r_rows2], writes=[r_P[4]])
                Tn(lambda e: e.matmul(P[5][:, 0:512], sel2[:, 128:256],
                                      rows2[0:2, 0:512], start=True, stop=True),
                   reads=[r_sel2, r_rows2], writes=[r_P[5]])

            xn2 = bb[blk % 2]
            for dch in range(DC):
                def t2v(dch=dch):
                    xs = dch % 2
                    xnv, r_xnv = xn_t[xs], r_xn[xs]
                    V(lambda e, dch=dch, xnv=xnv: e.tensor_tensor(
                        xnv[:, :], btx[:, dch * 512:(dch + 1) * 512],
                        P[4][:, 0:512], ALU.mult),
                      reads=[r_btx, r_P[4]], writes=[r_xnv])
                    V(lambda e, xnv=xnv: e.tensor_tensor(
                        xnv[:, :], xnv[:, :], P[5][:, 0:512], ALU.subtract),
                      reads=[r_P[5]], writes=[r_xnv])

                def t2s(dch=dch, xn2=xn2):
                    xs = dch % 2
                    xnv, r_xnv = xn_t[xs], r_xn[xs]
                    Sc(lambda e, dch=dch, xnv=xnv, x2=xn2: e.activation(
                        x2[:, dch * 512:(dch + 1) * 512], xnv[:, :], AF.Identity,
                        scale=gam_pp[:, dch:dch + 1], bias=beta_pp[:, dch:dch + 1]),
                       reads=[r_xnv, r_beta, r_gampp],
                       writes=[res(f"bb{blk % 2}_{dch}")] +
                              ([r_bb[1]] if (blk % 2 == 1 and dch == 0) else []))
                pieces.append(("latev", t2v))
                pieces.append(("lates", t2s))
            return pieces, rows_mm

        def a_main(blk, pq, rows_mm=None):
            bat = blk // NB2
            xn2 = bb[blk % 2]

            def drain(n=1, kinds=None):
                for _ in range(n):
                    if kinds is None:
                        if pq:
                            pq.pop(0)[1]()
                    else:
                        for i, (k, f) in enumerate(pq):
                            if k in kinds:
                                pq.pop(i)
                                f()
                                break

            q1b, r_q1b = bb[2], r_bb[2]
            for fo in range(DC):
                pqb, r_pqb = P[4 + fo % 2], r_P[4 + fo % 2]
                for dch in range(DC):
                    Tn(lambda e, fo=fo, dch=dch, pqb=pqb: e.matmul(
                        pqb[:, 0:512], wqkv_bf(dch)[:, fo * 128:(fo + 1) * 128],
                        xn2[:, dch * 512:(dch + 1) * 512],
                        start=(dch == 0), stop=(dch == DC - 1)),
                       reads=[r_wq[dch], res(f"bb{blk % 2}_{dch}")],
                       writes=[r_pqb] if dch == 0 else [],
                       wnowait=[r_pqb] if dch == DC - 1 else [])
                ms = fo % 2
                Sc(lambda e, pqb=pqb, ms=ms: e.activation(
                    m_t[0][:, ms * 512:(ms + 1) * 512], pqb[:, 0:512],
                    AF.Relu, scale=-1.0),
                   reads=[r_pqb], writes=[r_m0[ms]])
                Sc(lambda e, ms=ms: e.activation(
                    e_t[0][:, ms * 512:(ms + 1) * 512],
                    m_t[0][:, ms * 512:(ms + 1) * 512], AF.Exp, scale=-1.0),
                   reads=[r_m0[ms]], writes=[r_e0[ms]])
                V(lambda e, pqb=pqb, ms=ms, fo=fo: e.scalar_tensor_tensor(
                    q1b[:, fo * 512:(fo + 1) * 512], pqb[:, 0:512], 0.0,
                    e_t[0][:, ms * 512:(ms + 1) * 512], ALU.max, ALU.add),
                  reads=[r_pqb, r_e0[ms]], writes=[r_q1b])
                drain(1, kinds={"early"})
            DMA("gpsimd", lambda e, b=blk, q=q1b: e.dma_start(
                out=d_q1[b * 128:(b + 1) * 128, :], in_=q[:, :]),
                reads=[r_q1b], writes=[r_q1d], sem=f"st_q{blk % 2}")

            def kv_pair_mms(tc, hs, k1v, vv, r_k1v, r_vv):
                for h in hs:
                    pb, r_pb = P[h // 8], r_P[h // 8]
                    first = (tc == 0 and h % 8 == 0)
                    last = (tc == 3 and h % 8 == 7)
                    Tn(lambda e, h=h, pb=pb, fs=first, ls=last,
                       k1v=k1v, vv=vv: e.matmul(
                        pb[0:64, (h % 8) * 64:(h % 8 + 1) * 64],
                        k1v[:, h * 64:(h + 1) * 64],
                        vv[:, h * 65:h * 65 + 64],
                        start=fs, stop=ls),
                       reads=[r_k1v, r_vv],
                       writes=[r_pb] if first else [],
                       wnowait=[r_pb] if last else [])
                    Tn(lambda e, h=h, tc=tc, k1v=k1v, vv=vv: e.matmul(
                        P[2][0:64, h:h + 1],
                        k1v[:, h * 64:(h + 1) * 64],
                        vv[:, h * 65 + 64:h * 65 + 65],
                        start=(tc == 0 and h == 0), stop=(tc == 3 and h == 15)),
                       reads=[r_k1v, r_vv],
                       writes=[r_P[2]] if (tc == 0 and h == 0) else [],
                       wnowait=[r_P[2]] if (tc == 3 and h == 15) else [])

            pend = None
            for tc in range(4):
                ks = tc % 2
                k1v, r_k1v = k1_tm[ks], r_k1[ks]
                vv, r_vv = v65[ks], r_v65[ks]
                for qi, qtr in enumerate([0, 2, 1, 3]):
                    kb = [3, 6, 7][(tc * 4 + qi) % 3]
                    pk, r_pk = P[kb], r_P[kb]
                    for dch in range(DC):
                        Tn(lambda e, tc=tc, qtr=qtr, dch=dch, pk=pk: e.matmul(
                            pk[:, 0:512],
                            xn2[:, dch * 512 + tc * 128: dch * 512 + (tc + 1) * 128],
                            wqkv_bf(dch)[:, 1024 + qtr * 512:1024 + (qtr + 1) * 512],
                            start=(dch == 0), stop=(dch == DC - 1)),
                           reads=[res(f"bb{blk % 2}_{dch}"), r_wq[dch]],
                           writes=[r_pk] if dch == 0 else [],
                           wnowait=[r_pk] if dch == DC - 1 else [])
                    if pend is not None:
                        kv_pair_mms(*pend)
                        pend = None
                    if qtr < 2:
                        Sc(lambda e, pk=pk, qtr=qtr: e.activation(
                            m_t[1][:, qtr * 512:(qtr + 1) * 512], pk[:, 0:512],
                            AF.Relu, scale=-1.0),
                           reads=[r_pk], writes=[r_m1[qtr]])
                        Sc(lambda e, ks=ks, qtr=qtr: e.activation(
                            e_t[1][:, qtr * 512:(qtr + 1) * 512],
                            m_t[1][:, qtr * 512:(qtr + 1) * 512], AF.Exp, scale=-1.0),
                           reads=[r_m1[qtr]], writes=[r_e1[qtr]])
                        V(lambda e, pk=pk, ks=ks, qtr=qtr, k1v=k1v: e.scalar_tensor_tensor(
                            k1v[:, qtr * 512:(qtr + 1) * 512], pk[:, 0:512], 0.0,
                            e_t[1][:, qtr * 512:(qtr + 1) * 512], ALU.max, ALU.add),
                          reads=[r_pk, r_e1[qtr]], writes=[r_k1v])
                    else:
                        Sc(lambda e, pk=pk, vv=vv, qtr=qtr: e.activation(
                            vv[:, :].rearrange("p (h v) -> p h v", v=65)
                              [:, (qtr - 2) * 8:(qtr - 1) * 8, 0:64],
                            pk[:, 0:512].rearrange("p (h v) -> p h v", v=64),
                            AF.Copy),
                           reads=[r_pk], writes=[r_vv])
                    if qi == 1:
                        if tc == 0 and rows_mm is not None:
                            rows_mm()
                        pend = (tc, range(0, 8), k1v, vv, r_k1v, r_vv)
                        drain(2)
                    elif qi == 3:
                        if tc == 3:
                            kv_pair_mms(tc, range(8, 16), k1v, vv, r_k1v, r_vv)
                        else:
                            pend = (tc, range(8, 16), k1v, vv, r_k1v, r_vv)
                        drain(2)
                    else:
                        drain(1, kinds={"early", "latev"})
                drain(1)
            for bk in range(2):
                coff = bat * 1024 + bk * 512
                V(lambda e, bk=bk, coff=coff: e.tensor_tensor(
                    kvacc[0:64, coff:coff + 512], kvacc[0:64, coff:coff + 512],
                    P[bk][0:64, 0:512], ALU.add),
                  reads=[r_P[bk]], writes=[r_kvacc])
            V(lambda e, bat=bat: e.tensor_tensor(
                kvacc[0:64, 2048 + bat * 16:2048 + (bat + 1) * 16],
                kvacc[0:64, 2048 + bat * 16:2048 + (bat + 1) * 16],
                P[2][0:64, 0:16], ALU.add),
              reads=[r_P[2]], writes=[r_kvacc])

            den = max(NBLK - 1, 1)
            for wci in (range(blk * 8 // den, (blk + 1) * 8 // den)
                        if blk < den else []):
                DMA("sync", lambda e, rlo=wci * 512: e.dma_start(
                    out=bto[:, 0:4096].rearrange("p (c f) -> p c f", c=4),
                    in_=d_w1b[rlo:rlo + 512, :].rearrange("(p c) f -> p c f", c=4)),
                    writes=[r_bto], sem="ld_wm")
                V(lambda e: e.tensor_reduce(
                    tred[:, :], bto[:, 0:4096], AX.X, ALU.max, apply_absolute_value=True),
                  reads=[r_bto], writes=[r_tred])
                V(lambda e: e.tensor_tensor(
                    mx_w1[:, :], mx_w1[:, :], tred[:, :], ALU.max),
                  reads=[r_tred], writes=[r_mxw1])
            drain(99)

        bslot = [wbst[0], wbst[1], bto[:, 0:1024]]
        r_bslot = [r_wbst[0], r_wbst[1], r_bto]
        bsem = ["ld_wb0", "ld_wb1", "ld_wm"]

        def build_load(fo):
            ws = fo % 3
            op("scalar", lambda e, fo=fo, ws=ws: e.dma_start(
                out=bslot[ws][:, 0:1024],
                in_=d_w1b[fo * 128:(fo + 1) * 128, :]),
                writes=[r_bslot[ws]], sem=bsem[ws], dinc=16)

        def build_compute(fo):
            ws = fo % 3
            Sc(lambda e, ws=ws: e.activation(zbld[:, :], bslot[ws][:, :1024], AF.Copy,
                                             bias=MAGIC, scale=sc_col(SC_SWQINV)),
               reads=[r_bslot[ws], r_scal], writes=list(r_zbld))
            wr = [r_w1q[fo]] + (r_wq if fo == 0 else [])
            V(lambda e, fo=fo: e.tensor_scalar_add(
                arena[:, :].rearrange("p (c f) -> p c f", c=DC)[:, :, fo * 128:(fo + 1) * 128],
                zbld[:, :].rearrange("p (c f) -> p c f", c=DC), -MAGIC),
              reads=list(r_zbld), writes=wr)

        def boundary67():
            cross_part_max(r_mxxn, mx_xn, SC_T0)
            DMA("sync", lambda e: e.dma_start(out=oh_sb, in_=d_oh[:, :]),
                writes=[r_oh], sem="ld_cc")
            V(lambda e: e.tensor_scalar(oh_t[0:1, 0:8], oh_sb[0:1, 0:8],
                                        sc1(SC_T0), None, ALU.mult),
              reads=[r_oh, r_scal], writes=[r_oht])
            DMA("gpsimd", lambda e: e.dma_start(out=d_cc1ai[0:1, 0:8], in_=oh_t[0:1, 0:8]),
                reads=[r_oht], writes=[r_cc1ai], sem="st_cc")
            op("gpsimd", lambda e: e.collective_compute(
                "AllReduce", ALU.add, replica_groups=[list(range(n_cores))],
                ins=[d_cc1ai.ap()], outs=[d_cc1ao.ap()]),
               reads=[r_cc1ai], writes=[r_cc1ao], sem="cc1a", dinc=1)
            DMA("sync", lambda e: e.dma_start(
                out=btx[:, :], in_=d_xfm[0:128, :]),
                writes=[r_btx], sem="ld_x0")
            cross_part_max(r_mxw1, mx_w1, SC_SW1C)
            V(lambda e: e.tensor_scalar(sc1(SC_SW1V), sc1(SC_SW1C), MIN_RANGE,
                                        INV_QP, ALU.max, ALU.mult),
              writes=[r_scal])
            V(lambda e: e.reciprocal(sc1(SC_SW1I), sc1(SC_SW1V)), writes=[r_scal])
            bcast_scalar(SC_SW1I, SC_SWQINV)
            for fo in range(3):
                build_load(fo)

        pq_a, rm0 = a_prep_pieces(0)
        for k, f in pq_a[:7]:
            f()
        rm0()
        for k, f in pq_a[7:]:
            f()
        for blk in range(NBLK):
            if blk + 1 < NBLK:
                nxt, rm_next = a_prep_pieces(blk + 1)
            else:
                nxt, rm_next = [], None
            if blk == NBLK - 1:
                boundary67()
            a_main(blk, nxt, rm_next)

        DMA("gpsimd", lambda e: e.dma_start(out=gmx8[0:1, 0:8], in_=d_cc1ao[0:1, 0:8]),
            reads=[r_cc1ao], writes=[r_gmx8], sem="ld_cc2")

        DMA("gpsimd", lambda e: e.dma_start(out=d_cc1i[:, 0:2080], in_=kvacc[:, :]),
            reads=[r_kvacc], writes=[r_cc1i], sem="st_cc")
        op("gpsimd", lambda e: e.collective_compute(
            "AllReduce", ALU.add, replica_groups=[list(range(n_cores))],
            ins=[d_cc1i.ap()], outs=[d_cc1o.ap()]),
           reads=[r_cc1i], writes=[r_cc1o], sem="cc1", dinc=1)

        build_compute(0)
        build_load(3)
        build_compute(1)
        build_load(4)

        def b1_prep_pieces(blk):
            pieces = []

            def loads():
                DMA("sync", lambda e, b=blk: e.dma_start(
                    out=btx[:, :], in_=d_xfm[b * 128:(b + 1) * 128, :]),
                    writes=[r_btx], sem="ld_x0")
            pieces.append(loads)

            def rows_bcast():
                bb0 = blk * 24

                def spv(c):
                    return statp[:, bb0:bb0 + 24].rearrange(
                        "p (s c) -> p c s", c=6)[:, c:c + 1, :]
                V(lambda e: e.tensor_scalar(spv(2), spv(1), sc_col(SC_SXINV), None,
                                            ALU.mult),
                  reads=[r_scal], writes=[r_statp])
                V(lambda e: e.tensor_tensor(spv(3), spv(0), spv(2), ALU.mult),
                  writes=[r_statp])
                stats_rows2(blk, 2, P[6], r_P[6])
                Tn(lambda e: e.matmul(P[0][:, 0:512], sel2[:, 0:128],
                                      rows2[0:2, 0:512], start=True, stop=True),
                   reads=[r_sel2, r_rows2], writes=[r_P[0]])
                Tn(lambda e: e.matmul(P[1][:, 0:512], sel2[:, 128:256],
                                      rows2[0:2, 0:512], start=True, stop=True),
                   reads=[r_sel2, r_rows2], writes=[r_P[1]])
            pieces.append(rows_bcast)

            for dch in range(DC):
                def xq_piece(dch=dch):
                    xs = dch % 2
                    zqv, r_zqv = zq_t[xs], r_zq[xs]
                    V(lambda e, dch=dch, zqv=zqv: e.tensor_tensor(
                        zqv[:, :], btx[:, dch * 512:(dch + 1) * 512],
                        P[0][:, 0:512], ALU.mult),
                      reads=[r_btx, r_P[0]], writes=[r_zqv])
                    V(lambda e, zqv=zqv: e.tensor_tensor(
                        zqv[:, :], zqv[:, :], P[1][:, 0:512], ALU.subtract),
                      reads=[r_P[1]], writes=[r_zqv])
                    V(lambda e, dch=dch, zqv=zqv, b=blk: e.tensor_scalar(
                        bb[b % 2][:, dch * 512:(dch + 1) * 512],
                        zqv[:, :], MAGIC, MAGIC, ALU.add, ALU.subtract),
                      reads=[r_zqv], writes=[res(f"bb{blk % 2}_{dch}")])
                pieces.append(xq_piece)
            return pieces

        pq_b1 = b1_prep_pieces(0)[1:]

        V(lambda e: e.tensor_reduce(sc1(SC_T0), gmx8[0:1, 0:8], AX.X, ALU.max),
          reads=[r_gmx8], writes=[r_scal])
        V(lambda e: e.tensor_scalar(sc1(SC_GXC), sc1(SC_T0), MIN_RANGE, INV_QP,
                                    ALU.max, ALU.mult),
          writes=[r_scal])
        V(lambda e: e.reciprocal(sc1(SC_SXV), sc1(SC_GXC)), writes=[r_scal])
        V(lambda e: e.tensor_tensor(sc1(SC_T1), sc1(SC_SW1V), sc1(SC_GXC), ALU.mult),
          writes=[r_scal])
        bcast_scalar(SC_SXV, SC_SXINV)
        bcast_scalar(SC_T1, SC_SS1)
        V(lambda e: e.tensor_scalar(b1ss[:, :], b1_pp[:, :], sc_col(SC_SS1), None, ALU.mult),
          reads=[r_b1, r_scal], writes=[r_b1ss])

        while pq_b1:
            pq_b1.pop(0)()

        def kvblk_dma_piece(b, c):
            def f():
                off = b * 1024 + 2 * c * 64
                soff = 2048 + b * 16 + 2 * c
                V(lambda e: e.memset(kvstage[:, :], 0.0), writes=[r_kvstage])
                V(lambda e: e.memset(denstage[:, :], 0.0), writes=[r_denstage])
                op("sync", lambda e, off=off: e.dma_start(
                    out=kvstage[0:64, 0:64], in_=d_cc1o[0:64, off:off + 64]),
                   reads=[r_cc1o], writes=[r_kvstage], sem="ld_kvb", dinc=16)
                op("sync", lambda e, off=off: e.dma_start(
                    out=kvstage[64:128, 64:128], in_=d_cc1o[0:64, off + 64:off + 128]),
                   reads=[r_cc1o], writes=[r_kvstage], sem="ld_kvb", dinc=16)
                op("sync", lambda e, soff=soff, c=c: e.dma_start(
                    out=denstage[0:64, 2 * c:2 * c + 1], in_=d_cc1o[0:64, soff:soff + 1],
                    allow_slow_non_contiguous=True),
                   reads=[r_cc1o], writes=[r_denstage], sem="ld_kvb", dinc=16)
                op("sync", lambda e, soff=soff, c=c: e.dma_start(
                    out=denstage[64:128, 2 * c + 1:2 * c + 2],
                    in_=d_cc1o[0:64, soff + 1:soff + 2],
                    allow_slow_non_contiguous=True),
                   reads=[r_cc1o], writes=[r_denstage], sem="ld_kvb", dinc=16)
                tot = ("ld_kvb", ctx.sems["ld_kvb"].count)
                r_kvstage.w, r_kvstage.rs = tot, []
                r_denstage.w, r_denstage.rs = tot, []
            return f

        def kvblk_cast_piece(b, c):
            def f():
                t = b * 8 + c
                V(lambda e, t=t: e.tensor_copy(kvblk[:, t * 128:(t + 1) * 128], kvstage[:, :]),
                  reads=[r_kvstage], writes=[r_kvblk])
                V(lambda e, t=t: e.tensor_copy(denblk[:, t * 16:(t + 1) * 16], denstage[:, :]),
                  reads=[r_denstage], writes=[r_denblk])
            return f

        def wout_piece(d):
            def f():
                DMA("sync", lambda e, d=d: e.dma_start(
                    out=wbst[d % 2][:, :], in_=d_wout[d * 128:(d + 1) * 128, :]),
                    writes=[r_wbst[d % 2]], sem=f"ld_wb{d % 2}")
                V(lambda e, d=d: e.tensor_copy(woutbf[:, d * 1024:(d + 1) * 1024],
                                               wbst[d % 2][:, :]),
                  reads=[r_wbst[d % 2]], writes=[r_woutbf])
            return f

        def w2max_piece(wci):
            def f():
                rlo = wci * 512
                DMA("sync", lambda e, rlo=rlo: e.dma_start(
                    out=bto[:, 0:4096].rearrange("p (c f) -> p c f", c=4),
                    in_=d_w2t[rlo:rlo + 512, :].rearrange("(c p) f -> p c f", p=128)),
                    writes=[r_bto], sem="ld_wm")
                V(lambda e: e.tensor_reduce(
                    tred[:, :], bto[:, 0:4096], AX.X, ALU.max, apply_absolute_value=True),
                  reads=[r_bto], writes=[r_tred])
                V(lambda e: e.tensor_tensor(
                    mx_w2[:, :], mx_w2[:, :], tred[:, :], ALU.max),
                  reads=[r_tred], writes=[r_mxw2])
            return f

        extra_b1 = {}
        kv_bc = [(b, c) for b in range(2) for c in range(8)]
        all_extras = []
        for i, (b, c) in enumerate(kv_bc):
            all_extras.append(kvblk_dma_piece(b, c))
            all_extras.append(kvblk_cast_piece(b, c))
        all_extras += [wout_piece(d) for d in range(DC)]
        all_extras += [w2max_piece(i) for i in range(8)]
        for i, p in enumerate(all_extras):
            key = min(2 + i * (NBLK - 2) // len(all_extras), NBLK - 1)
            extra_b1.setdefault(key, []).append(p)

        for blk in range(NBLK):
            pq = (b1_prep_pieces(blk + 1) if blk + 1 < NBLK else [])
            pq = pq + extra_b1.get(blk, [])

            def drain(n=1):
                for _ in range(n):
                    if pq:
                        pq.pop(0)()

            xqb = bb[blk % 2]
            for fo in range(HC):
                pf, r_pf = P[2 + fo % 4], r_P[2 + fo % 4]
                for dch in range(DC):
                    Tn(lambda e, fo=fo, dch=dch, pf=pf: e.matmul(
                        pf[:, 0:512], w1qv(dch)[:, fo * 128:(fo + 1) * 128],
                        xqb[:, dch * 512:(dch + 1) * 512],
                        start=(dch == 0), stop=(dch == DC - 1)),
                       reads=[r_w1q[fo], res(f"bb{blk % 2}_{dch}")],
                       writes=[r_pf] if dch == 0 else [],
                       wnowait=[r_pf] if dch == DC - 1 else [])
                qrt = (fo // 2) % 4
                hsl = bts[:, qrt * 1024 + (fo % 2) * 512: qrt * 1024 + (fo % 2 + 1) * 512]
                hres = r_btsh[qrt]
                extra = [r_bts] if (blk == 0 and fo < 8) else []
                Sc(lambda e, pf=pf, hsl=hsl, fo=fo: e.activation(
                    hsl, pf[:, 0:512], AF.Gelu,
                    bias=b1ss[:, fo:fo + 1], scale=sc_col(SC_SS1)),
                   reads=[r_pf, r_b1ss, r_scal], writes=[hres] + extra)
                if fo % 2 == 1:
                    V(lambda e, qrt=qrt: e.tensor_reduce(
                        tred[:, :], bts[:, qrt * 1024:(qrt + 1) * 1024],
                        AX.X, ALU.max, apply_absolute_value=True),
                      reads=[hres], writes=[r_tred])
                    V(lambda e: e.tensor_tensor(mx_h[:, :], mx_h[:, :], tred[:, :], ALU.max),
                      reads=[r_tred], writes=[r_mxh])
                    tile = blk * 16 + (fo - 1) // 2
                    seng = "gpsimd" if qrt % 2 == 0 else "scalar"
                    op(seng, lambda e, tile=tile, qrt=qrt: e.dma_start(
                        out=d_h[tile * 128:(tile + 1) * 128, :],
                        in_=bts[:, qrt * 1024:(qrt + 1) * 1024]),
                       reads=[hres], writes=[r_hd[qrt % 2]], sem=f"st_h{qrt % 2}", dinc=16)
                    drain(1)
                if blk == 0:
                    if fo + 2 < HC:
                        build_compute(fo + 2)
                    if fo + 5 < HC:
                        build_load(fo + 5)
            drain(99)

        cross_part_max(r_mxh, mx_h, SC_T0)
        DMA("gpsimd", lambda e: e.dma_start(out=oh_sb, in_=d_oh[:, :]),
            writes=[r_oh], sem="ld_cc2")
        V(lambda e: e.tensor_scalar(oh_t[0:1, 0:8], oh_sb[0:1, 0:8],
                                    sc1(SC_T0), None, ALU.mult),
          reads=[r_oh, r_scal], writes=[r_oht])
        DMA("gpsimd", lambda e: e.dma_start(out=d_cc2i[0:1, 0:8], in_=oh_t[0:1, 0:8]),
            reads=[r_oht], writes=[r_cc2i], sem="st_cc")
        op("gpsimd", lambda e: e.collective_compute(
            "AllReduce", ALU.add, replica_groups=[list(range(n_cores))],
            ins=[d_cc2i.ap()], outs=[d_cc2o.ap()]),
           reads=[r_cc2i], writes=[r_cc2o], sem="cc2", dinc=1)

        cross_part_max(r_mxw2, mx_w2, SC_SW2C)
        V(lambda e: e.tensor_scalar(sc1(SC_SW2V), sc1(SC_SW2C), MIN_RANGE, INV_QP,
                                    ALU.max, ALU.mult),
          writes=[r_scal])
        V(lambda e: e.reciprocal(sc1(SC_T0), sc1(SC_SW2V)), writes=[r_scal])
        bcast_scalar(SC_T0, SC_SWQINV)

        def build_w2q(f):
            ws = f % 2
            DMA("sync", lambda e, f=f, ws=ws: e.dma_start(
                out=wbst[ws][:, :], in_=d_w2t[f * 128:(f + 1) * 128, :]),
                writes=[r_wbst[ws]], sem=f"ld_wb{ws}")
            V(lambda e, ws=ws: e.tensor_scalar(
                zbld[:, :], wbst[ws][:, :], sc_col(SC_SWQINV), MAGIC,
                ALU.mult, ALU.add),
              reads=[r_wbst[ws], r_scal], writes=list(r_zbld))
            wr = [r_w2q[f]] + (r_w1q if f == 0 else [])
            V(lambda e, f=f: e.tensor_scalar_add(w2qv(f)[:, :], zbld[:, :], -MAGIC),
              reads=list(r_zbld), writes=wr)

        def c_quant(blk, f):
            hs = f % 4
            zs = f % 2
            extra = ([r_bts] + r_btsh if (blk == 0 and f < 4) else [])
            tile = blk * 16 + f // 2
            leng = "sync" if hs % 2 == 0 else "gpsimd"
            op(leng, lambda e, tile=tile, f=f, hs=hs: e.dma_start(
                out=hrel_v[hs],
                in_=d_h[tile * 128:(tile + 1) * 128,
                        (f % 2) * 512:(f % 2 + 1) * 512]),
               reads=[r_hd[(f // 2) % 2]], writes=[r_hrel[hs]] + extra,
               sem=f"ld_h{hs}", dinc=16)
            Sc(lambda e, hs=hs, zs=zs: e.activation(
                zq_t[zs][:, :], hrel_v[hs], AF.Copy,
                bias=MAGIC, scale=sc_col(SC_SHINV)),
               reads=[r_hrel[hs], r_scal], writes=[r_zq[zs]])
            V(lambda e, hs=hs, zs=zs: e.tensor_scalar_add(
                hq_t[hs][:, :], zq_t[zs][:, :], -MAGIC),
              reads=[r_zq[zs]], writes=[r_hq[hs]])

        def c_loads(blk):
            DMA("scalar", lambda e, b=blk: e.dma_start(
                out=btx[:, :], in_=d_xfm[b * 128:(b + 1) * 128, :]),
                writes=[r_btx], sem="ld_x0")
            DMA("scalar", lambda e, b=blk: e.dma_start(
                out=bb[1][:, :], in_=d_q1[b * 128:(b + 1) * 128, :]),
                reads=[r_q1d],
                writes=[r_bb[1]] + [res(f"bb1_{i}") for i in range(DC)], sem="ld_q1")

        def c_attn(blk):
            bat = blk // NB2
            for c in range(8):
                t = bat * 8 + c
                Tn(lambda e, t=t, c=c: e.matmul(
                    P[0][0:16, 0:512], denblk[:, t * 16:(t + 1) * 16],
                    bb[1][:, c * 512:(c + 1) * 512],
                    start=(c == 0), stop=(c == 7)),
                   reads=[r_denblk, r_bb[1]],
                   writes=[r_P[0]] if c == 0 else [],
                   wnowait=[r_P[0]] if c == 7 else [])
            Sc(lambda e: e.activation(den_sb[0:16, 0:512], P[0][0:16, 0:512],
                                      AF.Ln, bias=scal[0:16, SC_DEPS:SC_DEPS + 1]),
               reads=[r_P[0], r_scal], writes=[r_densb])
            Sc(lambda e: e.activation(rec_bf[0:16, 0:512], den_sb[0:16, 0:512],
                                      AF.Exp, scale=-1.0),
               reads=[r_densb], writes=[r_recbf])
            def qkv_mm(c):
                t = bat * 8 + c
                pqb, r_pqb = P[1 + c % 4], r_P[1 + c % 4]
                Tn(lambda e, t=t, c=c, pqb=pqb: e.matmul(
                    pqb[:, 0:512], kvblk[:, t * 128:(t + 1) * 128],
                    bb[1][:, c * 512:(c + 1) * 512], start=True, stop=True),
                   reads=[r_kvblk, r_bb[1]], writes=[r_pqb])
            for c in range(4):
                qkv_mm(c)
            rslot = [recb_v[0], recb_v[1], xn_t[0][:, :], xn_t[1][:, :]]
            r_rslot = [r_recb[0], r_recb[1], r_xn[0], r_xn[1]]
            rbank = [5, 6, 7, 0]

            def recb_bcast(c):
                pr, r_pr = P[rbank[c % 4]], r_P[rbank[c % 4]]
                Tn(lambda e, c=c, pr=pr: e.matmul(
                    pr[:, 0:512], maskc[:, c * 128:(c + 1) * 128],
                    rec_bf[0:16, 0:512], start=True, stop=True),
                   reads=[r_maskc, r_recbf], writes=[r_pr])
            for c in range(4):
                recb_bcast(c)
            for c in range(8):
                pqb, r_pqb = P[1 + c % 4], r_P[1 + c % 4]
                pr, r_pr = P[rbank[c % 4]], r_P[rbank[c % 4]]
                cs = c % 4
                Sc(lambda e, pr=pr, cs=cs: e.activation(
                    rslot[cs], pr[:, 0:512], AF.Copy),
                   reads=[r_pr],
                   writes=[r_rslot[cs]])
                V(lambda e, pqb=pqb, cs=cs, c=c: e.tensor_tensor(
                    bb[2][:, c * 512:(c + 1) * 512], pqb[:, 0:512],
                    rslot[cs], ALU.mult),
                  reads=[r_pqb, r_rslot[cs]], writes=[r_bb[2]])
                if c + 4 < 8:
                    qkv_mm(c + 4)
                    recb_bcast(c + 4)
            if blk > 0:
                for f in range(4):
                    c_quant(blk, f)

            def wout_group(do):
                pw, r_pw = P[5 + do % 2], r_P[5 + do % 2]
                for di in range(DC):
                    Tn(lambda e, do=do, di=di, pw=pw: e.matmul(
                        pw[:, 0:512],
                        woutbf[:, di * 1024 + do * 128: di * 1024 + (do + 1) * 128],
                        bb[2][:, di * 512:(di + 1) * 512],
                        start=(di == 0), stop=(di == DC - 1)),
                       reads=[r_woutbf, r_bb[2]],
                       writes=[r_pw] if di == 0 else [],
                       wnowait=[r_pw] if di == DC - 1 else [])

            stats_rows2(blk, 4, P[0], r_P[0])
            wout_group(0)
            wout_group(1)
            Tn(lambda e: e.matmul(P[1][:, 0:512], sel2[:, 128:256], rows2[0:2, 0:512],
                                  start=True, stop=True),
               reads=[r_sel2, r_rows2], writes=[r_P[1]])
            Tn(lambda e: e.matmul(P[2][:, 0:512], sel2[:, 0:128], rows2[0:2, 0:512],
                                  start=True, stop=True),
               reads=[r_sel2, r_rows2], writes=[r_P[2]])
            for do in range(DC):
                V(lambda e, do=do: e.tensor_tensor(
                    bto[:, do * 512:(do + 1) * 512],
                    btx[:, do * 512:(do + 1) * 512], P[1][:, 0:512], ALU.mult),
                  reads=[r_btx, r_P[1]], writes=[r_bto])
                V(lambda e, do=do: e.tensor_tensor(
                    bto[:, do * 512:(do + 1) * 512],
                    bto[:, do * 512:(do + 1) * 512], P[2][:, 0:512], ALU.subtract),
                  reads=[r_P[2]], writes=[r_bto])
                pw, r_pw = P[5 + do % 2], r_P[5 + do % 2]
                V(lambda e, do=do, pw=pw: e.tensor_tensor(
                    bto[:, do * 512:(do + 1) * 512], pw[:, 0:512],
                    bto[:, do * 512:(do + 1) * 512], ALU.add),
                  reads=[r_pw], writes=[r_bto])
                if do + 2 < DC:
                    wout_group(do + 2)

        def c_ffn2(blk):
            for f in range(HC):
                if blk == 0:
                    build_w2q(f)
                hs = f % 4
                if blk == 0 or f >= 4:
                    c_quant(blk, f)
                for do in range(DC):
                    Tn(lambda e, f=f, do=do, hs=hs: e.matmul(
                        P[do][:, 0:512], w2qv(f)[:, do * 128:(do + 1) * 128],
                        hq_t[hs][:, :], start=(f == 0), stop=(f == HC - 1)),
                       reads=[r_w2q[f], r_hq[hs]],
                       writes=[r_P[do]] if f == 0 else [],
                       wnowait=[r_P[do]] if f == HC - 1 else [])
                if blk + 1 < NBLK and f == 24:
                    c_loads(blk + 1)
            for do in range(DC):
                ds = do % 2
                Sc(lambda e, do=do, ds=ds: e.activation(
                    et_v[ds], P[do][:, 0:512], AF.Identity,
                    bias=b2ss[:, do:do + 1], scale=sc_col(SC_SS2)),
                   reads=[r_P[do], r_b2ss, r_scal],
                   writes=[r_et[ds]])
                V(lambda e, do=do, ds=ds: e.tensor_tensor(
                    bto[:, do * 512:(do + 1) * 512], et_v[ds],
                    bto[:, do * 512:(do + 1) * 512], ALU.add),
                  reads=[r_et[ds]], writes=[r_bto])
            DMA("scalar", lambda e, b=blk: e.dma_start(
                out=d_out[b * 128:(b + 1) * 128, :], in_=bto[:, :]),
                reads=[r_bto], writes=[r_out], sem="st_out")

        c_loads(0)
        c_attn(0)

        DMA("gpsimd", lambda e: e.dma_start(out=gmx8[0:1, 0:8], in_=d_cc2o[0:1, 0:8]),
            reads=[r_cc2o], writes=[r_gmx8], sem="ld_cc2")
        V(lambda e: e.tensor_reduce(sc1(SC_GHC), gmx8[0:1, 0:8], AX.X, ALU.max),
          reads=[r_gmx8], writes=[r_scal])
        V(lambda e: e.tensor_scalar(sc1(SC_SHV), sc1(SC_GHC), MIN_RANGE, INV_QP,
                                    ALU.max, ALU.mult),
          writes=[r_scal])
        V(lambda e: e.reciprocal(sc1(SC_T0), sc1(SC_SHV)), writes=[r_scal])
        V(lambda e: e.tensor_tensor(sc1(SC_T1), sc1(SC_SW2V), sc1(SC_SHV), ALU.mult),
          writes=[r_scal])
        bcast_scalar(SC_T0, SC_SHINV)
        bcast_scalar(SC_T1, SC_SS2)
        V(lambda e: e.tensor_scalar(b2ss[:, :], b2_pp[:, :], sc_col(SC_SS2), None, ALU.mult),
          reads=[r_b2, r_scal], writes=[r_b2ss])

        c_ffn2(0)
        for blk in range(1, NBLK):
            c_attn(blk)
            c_ffn2(blk)

        ctx.raw_wait("gpsimd", "st_out", 16 * NBLK)

    with nc.Block() as block:

        @block.sync
        def _(sync):
            schedule(_Ctx("sync", sync, fresh_sems()))

        @block.gpsimd
        def _(gpsimd):
            schedule(_Ctx("gpsimd", gpsimd, fresh_sems()))

        @block.vector
        def _(vector):
            schedule(_Ctx("vector", vector, fresh_sems()))

        @block.scalar
        def _(scalar):
            schedule(_Ctx("scalar", scalar, fresh_sems()))

        @block.tensor
        def _(tensor):
            schedule(_Ctx("tensor", tensor, fresh_sems()))

    es.close()
    return nc


_NC_CACHE = {}


def _get_program(T_core, n_cores):
    key = (T_core, n_cores)
    if key not in _NC_CACHE:
        _NC_CACHE[key] = build_program(T_core, n_cores)
    return _NC_CACHE[key]


def kernel(L0_attn_x, L0_attn_gamma, L0_attn_beta, L0_attn_Wqkv, L0_attn_Wout,
           L0_ffn_w1, L0_ffn_b1, L0_ffn_w2, L0_ffn_b2):
    x = np.asarray(L0_attn_x, dtype=np.float32)
    B, L, D_ = x.shape
    n = 8
    Lc = L // n
    T_core = B * Lc

    nc = _get_program(T_core, n)

    wqkv = np.ascontiguousarray(np.asarray(L0_attn_Wqkv, dtype=np.float32))
    wout = np.ascontiguousarray(np.asarray(L0_attn_Wout, dtype=np.float32))
    w1 = np.asarray(L0_ffn_w1, dtype=np.float32)
    w1b = np.ascontiguousarray(
        w1.reshape(HC, 128, DC, 128).transpose(0, 3, 2, 1).reshape(HF, D))
    w2t = np.ascontiguousarray(np.asarray(L0_ffn_w2, dtype=np.float32).T)
    gam = np.ascontiguousarray(np.asarray(L0_attn_gamma, dtype=np.float32).reshape(DC, 128).T)
    beta_pp = np.ascontiguousarray(np.asarray(L0_attn_beta, dtype=np.float32).reshape(DC, 128).T)
    b1_pp = np.ascontiguousarray(np.asarray(L0_ffn_b1, dtype=np.float32).reshape(HC, 128).T)
    b2_pp = np.ascontiguousarray(np.asarray(L0_ffn_b2, dtype=np.float32).reshape(DC, 128).T)
    ident = np.eye(128, dtype=np.float32)
    sel2c = np.zeros((2, 256), np.float32)
    sel2c[0, 0:128] = 1.0
    sel2c[1, 128:256] = 1.0
    maskc_f = np.zeros((16, 1024), np.float32)
    for cc in range(8):
        maskc_f[2 * cc, cc * 128:cc * 128 + 64] = 1.0
        maskc_f[2 * cc + 1, cc * 128 + 64:(cc + 1) * 128] = 1.0

    in_maps = []
    nbt = T_core // 512
    for c in range(n):
        xs = x[:, c * Lc:(c + 1) * Lc, :]
        x_tm0 = xs.reshape(T_core, D_)
        x_fm = np.ascontiguousarray(
            x_tm0.reshape(nbt, 512, 8, 128).transpose(0, 3, 2, 1).reshape(nbt * 128, 4096))
        x_tm = np.ascontiguousarray(
            x_tm0.reshape(nbt, 4, 128, 1024).transpose(0, 2, 1, 3).reshape(nbt * 128, 4096))
        oh = np.zeros((1, 8), np.float32)
        oh[0, c] = 1.0
        in_maps.append(dict(x_fm=x_fm, x_tm=x_tm, wqkv=wqkv, wout=wout,
                            w1b=w1b, w2t=w2t, gam_pp=gam, beta_pp=beta_pp,
                            b1_pp=b1_pp, b2_pp=b2_pp, ident=ident, onehot=oh,
                            sel2c=sel2c, maskc_f=maskc_f))

    res = run_bass_kernel_spmd(nc, in_maps, core_ids=list(range(n)))
    parts = []
    for r in res.results:
        arr = np.asarray(r["out_fm"])
        out_core = arr.reshape(nbt, 128, 8, 512).transpose(0, 3, 2, 1).reshape(T_core, D_)
        parts.append(out_core.reshape(B, Lc, D_))
    return np.ascontiguousarray(np.concatenate(parts, axis=1)).astype(np.float32)
